# version 09
# baseline (speedup 1.0000x reference)
import numpy as np
from contextlib import ExitStack

import concourse.bass as bass
import concourse.mybir as mybir
from concourse.bass_utils import run_bass_kernel_spmd

F32 = mybir.dt.float32
BF16 = mybir.dt.bfloat16
AF = mybir.ActivationFunctionType
ALU = mybir.AluOpType
AX = mybir.AxisListType

B, S, E = 2, 8192, 1024
H, HD = 16, 64
K = 45
NEXP, HID = 8, 2048
NHID = NEXP * HID
NTOK = S // 4
ROWS = H * K
RPAD = 768
NRT = 6
TT = 512
NTILE = NTOK // TT
NHC = NHID // 128
EPS32 = float(np.finfo(np.float32).eps)
QB_N = H


class Ctr:
    def __init__(self, h):
        self.h = h
        self.n = 0

    def inc(self, k=1):
        self.n += k
        return self.n


class DmaStream:

    def __init__(self, sems):
        self.sems = sems
        self.marks = []

    def issue(self, e, out, in_, **kw):
        s = self.sems[len(self.marks) % len(self.sems)]
        e.dma_start(out=out, in_=in_, **kw).then_inc(s.h, 16)
        s.inc(16)
        self.marks.append((s.h, s.n))
        return len(self.marks) - 1

    def wait(self, e, i):
        h, v = self.marks[i]
        e.wait_ge(h, v)

    def wait_upto(self, e, i):
        seen = {}
        for j in range(i + 1):
            h, v = self.marks[j]
            seen[id(h)] = (h, v)
        for h, v in seen.values():
            e.wait_ge(h, v)


def build_program(dbg=None, stop_stage=99):
    dbg = dbg or {}
    nc = bass.Bass("TRN2", target_bir_lowering=False,
                   detect_race_conditions=False)

    xb = nc.dram_tensor("xb", [S, E], F32, kind="ExternalInput").ap()
    ipw = nc.dram_tensor("ipw", [3 * E, E], F32, kind="ExternalInput").ap()
    ipb = nc.dram_tensor("ipb", [24, 128], F32, kind="ExternalInput").ap()
    outw = nc.dram_tensor("outw", [E, E], F32, kind="ExternalInput").ap()
    outb = nc.dram_tensor("outb", [1, E], F32, kind="ExternalInput").ap()
    gatew = nc.dram_tensor("gatew", [NEXP, E], F32, kind="ExternalInput").ap()
    gateb = nc.dram_tensor("gateb", [1, NEXP], F32, kind="ExternalInput").ap()
    rmsw = nc.dram_tensor("rmsw", [1, NEXP], F32, kind="ExternalInput").ap()
    w1r = nc.dram_tensor("w1r", [NHC, 128, 1024], F32, kind="ExternalInput").ap()
    w2r = nc.dram_tensor("w2r", [NHID, E], F32, kind="ExternalInput").ap()
    b1c = nc.dram_tensor("b1c", [128, 128], F32, kind="ExternalInput").ap()
    b2 = nc.dram_tensor("b2", [NEXP, E], F32, kind="ExternalInput").ap()
    ident = nc.dram_tensor("ident", [128, 128], F32, kind="ExternalInput").ap()
    onesc = nc.dram_tensor("onesc", [128, 1], F32, kind="ExternalInput").ap()
    onesr = nc.dram_tensor("onesr", [1, 128], F32, kind="ExternalInput").ap()
    yout = nc.dram_tensor("y", [NTOK, E], F32, kind="ExternalOutput").ap()
    dbg_out = {k: nc.dram_tensor("dbg_" + k, list(v[0]), v[1], kind="ExternalOutput").ap()
               for k, v in dbg.items()}

    xscr = nc.dram_tensor("xscr", [S, E], BF16, kind="Internal").ap()
    w1scr = nc.dram_tensor("w1scr", [NHC, 128, 1024], BF16, kind="Internal").ap()
    w2scr = nc.dram_tensor("w2scr", [NHID, E], BF16, kind="Internal").ap()
    attnscr = nc.dram_tensor("attnscr", [RPAD, S], BF16, kind="Internal").ap()

    root = ExitStack()
    sem = lambda name: Ctr(root.enter_context(nc.semaphore(name)))

    with root:
        def mk_sb(stack):
            return lambda name, shape, dt=F32: stack.enter_context(
                nc.sbuf_tensor(name, shape, dt))

        sb = mk_sb(root)

        id32 = sb("id32", [128, 128])
        ones32c = sb("ones32c", [128, 1])
        ones32r = sb("ones32r", [1, 128])
        onesbfr = sb("onesbfr", [1, 128], BF16)
        onesbfc = sb("onesbfc", [128, 1], BF16)
        bcols = sb("bcols", [128, 24])
        b1col = sb("b1col", [128, 128])
        b2sumr = sb("b2sumr", [1, 1024], BF16)
        rmsb = sb("rmsb", [128, NEXP])
        gwT = sb("gwT", [128, 64], BF16)
        gbcol = sb("gbcol", [8, 1])
        cmax = sb("cmax", [128, 1])
        gmax1 = sb("gmax1", [1, 1])
        smax = sb("smax", [1, 1])
        gapA = sb("gapA", [128, 1])
        gapB = sb("gapB", [128, 1])
        bsc = sb("bsc", [1, 1])
        cst = sb("cst", [128, 2])
        rsum = sb("rsum", [128, NRT])
        sumex = sb("sumex", [128, NRT])
        oselr = sb("oselr", [K, 1024], BF16)
        obmrow = sb("obmrow", [1, 1024], BF16)
        meanrow = sb("meanrow", [1, 1024], BF16)
        slhsP = sb("slhsP", [K, 16 * 128], BF16)
        cmrow = sb("cmrow", [1, 16 * 128], BF16)

        LS0 = sem("LS0"); LS1 = sem("LS1"); LS2 = sem("LS2"); LS3 = sem("LS3")
        LC0 = sem("LC0"); LC1 = sem("LC1"); LC2 = sem("LC2"); LC3 = sem("LC3")
        LG0 = sem("LG0"); LG1 = sem("LG1")
        LM = sem("LM")
        LN_ = sem("LN_")
        TC = sem("TC"); VC = sem("VC"); SC = sem("SC")
        BAR = sem("BAR")
        WA = sem("WA"); GE = sem("GE"); WB = sem("WB")
        WCA = sem("WCA"); WCW = sem("WCW")
        YT = sem("YT"); YE = sem("YE"); RN = sem("RN")
        SYN = DmaStream([LS0, LS1, LS2, LS3])
        SCL = DmaStream([LC0, LC1, LC2, LC3])
        GPS = DmaStream([LG0, LG1])
        MSC = DmaStream([LM])
        MGP = DmaStream([LN_])
        ALLQ = [LS0, LS1, LS2, LS3, LC0, LC1, LC2, LC3, LG0, LG1, LM, LN_]

        psum = [root.enter_context(nc.psum_tensor(f"ps{i}", [128, 512], F32))
                for i in range(8)]

        g_mid = ExitStack(); root.enter_context(g_mid)
        sbm = mk_sb(g_mid)
        ss = sbm("ss", [128, 64])
        ssoh = sbm("ssoh", [128, 64])
        ptbf = sbm("ptbf", [128, 64 * K], BF16)
        ebias = sbm("ebias", [128, NRT])
        sump = sbm("sump", [128, 96])
        qtokb = sbm("qtokb", [K, 1024])
        qWf = sbm("qWf", [128, 8 * RPAD], BF16)
        wcb = [sbm(f"wcb{i}", [128, 4096], BF16) for i in range(2)]

        taps = {}

        def barrier(block):
            BAR.inc(5)
            tgt = BAR.n
            qvals = [(s.h, s.n) for s in ALLQ]

            @block.sync
            def _(e):
                for h, v in qvals:
                    e.wait_ge(h, v)
                e.sem_inc(BAR.h, 1)
                e.wait_ge(BAR.h, tgt)

            @block.scalar
            def _(e):
                e.activation(out=bsc[:], in_=ones32c[0:1, 0:1],
                             func=AF.Identity).then_inc(BAR.h, 1)
                e.wait_ge(BAR.h, tgt)

            @block.gpsimd
            def _(e):
                e.engine_nop().then_inc(BAR.h, 1)
                e.wait_ge(BAR.h, tgt)

            @block.tensor
            def _(e):
                e.nop().then_inc(BAR.h, 1)
                e.wait_ge(BAR.h, tgt)

            @block.vector
            def _(e):
                e.engine_nop().then_inc(BAR.h, 1)
                e.wait_ge(BAR.h, tgt)

        def tap(block, name, ap):
            if name not in dbg_out:
                return
            @block.sync
            def _(e):
                MSC.issue(e, dbg_out[name], ap)
            barrier(block)

        with nc.Block() as block:
            g1 = ExitStack(); root.enter_context(g1)
            sb1 = mk_sb(g1)
            ipb_sb = sb1("ipb_sb", [24, 128])
            b2_sb = sb1("b2_sb", [NEXP, 1024])
            rms_sb = sb1("rms_sb", [1, NEXP])
            gb_sb = sb1("gb_sb", [1, NEXP])
            gw_sb = sb1("gw_sb", [NEXP, 1024])
            outb_sb = sb1("outb_sb", [1, 1024], BF16)
            xf = [sb1(f"xf{i}", [128, 1024]) for i in range(2)]
            xbf = [sb1(f"xbf{i}", [128, 1024], BF16) for i in range(2)]
            junk1 = sb1("junk1", [128, 1024])

            @block.sync
            def _(e):
                for t, src in [(id32, ident), (ones32c, onesc), (ones32r, onesr),
                               (b1col, b1c), (ipb_sb, ipb), (b2_sb, b2),
                               (rms_sb, rmsw), (gb_sb, gateb), (gw_sb, gatew)]:
                    MSC.issue(e, t[:], src)

            @block.gpsimd
            def _(e):
                for t, src in [(onesbfr, onesr), (onesbfc, onesc),
                               (outb_sb, outb)]:
                    MGP.issue(e, t[:], src)
            i_const = len(MSC.marks) - 1
            ig_const = len(MGP.marks) - 1

            tc0 = TC.n
            vc0 = VC.n
            @block.tensor
            def _(e):
                MSC.wait_upto(e, i_const)
                MGP.wait_upto(e, ig_const)
                e.transpose(psum[0][:, 0:24], ipb_sb[:], id32[0:24, 0:24]
                            ).then_inc(TC.h, 1); TC.inc()
                e.matmul(psum[4][0:1, :], ones32c[0:8, 0:1], b2_sb[:, 0:512],
                         start=True, stop=True)
                e.matmul(psum[5][0:1, :], ones32c[0:8, 0:1], b2_sb[:, 512:1024],
                         start=True, stop=True).then_inc(TC.h, 1); TC.inc()
                e.matmul(psum[6][:, 0:8], ones32r[:], rms_sb[:],
                         start=True, stop=True).then_inc(TC.h, 1); TC.inc()
                e.transpose(psum[7][0:8, 0:1], gb_sb[:], id32[0:1, 0:1]
                            ).then_inc(TC.h, 1); TC.inc()
                for kc in range(8):
                    if kc >= 2:
                        e.wait_ge(VC.h, vc0 + 6 + (kc - 2) + 1)
                    e.transpose(psum[1 + kc % 2][:, 0:8],
                                gw_sb[:, kc * 128:(kc + 1) * 128],
                                id32[0:8, 0:8]).then_inc(TC.h, 1); TC.inc()
            tc_tab = TC.n

            @block.vector
            def _(e):
                e.wait_ge(TC.h, tc0 + 4)
                e.tensor_copy(bcols[:], psum[0][:, 0:24])
                e.tensor_copy(b2sumr[:, 0:512], psum[4][0:1, :])
                e.tensor_copy(b2sumr[:, 512:1024], psum[5][0:1, :])
                e.tensor_copy(rmsb[:], psum[6][:, 0:8])
                e.tensor_copy(gbcol[:], psum[7][0:8, 0:1])
                MGP.wait_upto(e, ig_const)
                e.memset(slhsP[:], 0.0)
                e.memset(qWf[:], 0.0)
                e.memset(gapB[:], 0.0)
                e.memset(cst[:, 0:1], 1e-5)
                e.memset(cst[:, 1:2], EPS32)
                e.engine_nop().then_inc(VC.h, 6); VC.inc(6)
                for kc in range(8):
                    e.wait_ge(TC.h, tc0 + 4 + kc + 1)
                    e.tensor_copy(gwT[:, kc * 8:(kc + 1) * 8],
                                  psum[1 + kc % 2][:, 0:8]).then_inc(VC.h, 1)
                    VC.inc()
            vc_tab = VC.n

            NXT = S // 128
            s1_sc0, s1_vc0, s1_tc0 = SC.n, VC.n, TC.n
            ixf = [None] * NXT
            iws = [None] * NXT

            @block.sync
            def _(e):
                for i in range(NXT):
                    if i >= 2:
                        e.wait_ge(SC.h, s1_sc0 + (i - 2) + 1)
                        e.wait_ge(VC.h, s1_vc0 + (i - 2) + 1)
                    ixf[i] = SYN.issue(e, xf[i % 2][:],
                                       xb[i * 128:(i + 1) * 128, :])

            @block.scalar
            def _(e):
                for i in range(NXT):
                    SYN.wait(e, ixf[i])
                    e.activation(out=junk1[:], in_=xf[i % 2][:], func=AF.Square,
                                 accum_out=ss[:, i:i + 1]).then_inc(SC.h, 1)
                    SC.inc()
                    e.wait_ge(VC.h, s1_vc0 + i + 1)
                    iws[i] = SCL.issue(e, xscr[i * 128:(i + 1) * 128, :],
                                       xbf[i % 2][:])

            @block.vector
            def _(e):
                for i in range(NXT):
                    SYN.wait(e, ixf[i])
                    if i >= 2:
                        SCL.wait(e, iws[i - 2])
                        e.wait_ge(TC.h, s1_tc0 + (i - 1))
                    e.tensor_copy(xbf[i % 2][:], xf[i % 2][:]).then_inc(VC.h, 1)
                    VC.inc()
            s1_vc_end = VC.n

            @block.tensor
            def _(e):
                for i in range(NXT):
                    e.wait_ge(VC.h, s1_vc0 + i + 1)
                    e.matmul(psum[6][0:1, :], onesbfc[:], xbf[i % 2][:, 0:512],
                             start=(i == 0), stop=(i == NXT - 1))
                    e.matmul(psum[7][0:1, :], onesbfc[:], xbf[i % 2][:, 512:1024],
                             start=(i == 0), stop=(i == NXT - 1)).then_inc(TC.h, 1)
                    TC.inc()
            s1_tc_end = TC.n

            @block.vector
            def _(e):
                e.wait_ge(TC.h, s1_tc_end)
                e.tensor_scalar(meanrow[0:1, 0:512], psum[6][0:1, :], 1.0 / S,
                                None, op0=ALU.mult)
                e.tensor_scalar(meanrow[0:1, 512:1024], psum[7][0:1, :], 1.0 / S,
                                None, op0=ALU.mult).then_inc(VC.h, 1)
                VC.inc()
                e.tensor_copy(gapA[:], gapB[:])
                e.tensor_tensor(obmrow[:], outb_sb[:], meanrow[:],
                                op=ALU.subtract).then_inc(VC.h, 1)
                VC.inc()

            @block.gpsimd
            def _(e):
                for i in range(32):
                    if i >= 2:
                        e.wait_ge(WCW.h, 16 * (i - 1))
                    e.dma_start(out=wcb[i % 2][:].rearrange("p (b f) -> p b f", f=1024),
                                in_=w1r[i * 4:(i + 1) * 4].rearrange("b p f -> p b f")
                                ).then_inc(WCA.h, 16)
                    WCA.inc(16)
                    e.wait_ge(WCA.h, WCA.n)
                    e.dma_start(out=w1scr[i * 4:(i + 1) * 4].rearrange("b p f -> p b f"),
                                in_=wcb[i % 2][:].rearrange("p (b f) -> p b f", f=1024)
                                ).then_inc(WCW.h, 16)
                    WCW.inc(16)

            barrier(block)
            g1.close()
            tap(block, "ss", ss[:])
            tap(block, "mean", meanrow[:])
            if stop_stage <= 1:
                return nc

            s2_tc0, s2_vc0 = TC.n, VC.n

            @block.vector
            def _(e):
                for it in range(K):
                    e.wait_ge(TC.h, s2_tc0 + 2 * it)
                    e.reduce_max(cmax[:], ss[:], axis=AX.X).then_inc(VC.h, 1)
                    VC.inc()
                    e.wait_ge(TC.h, s2_tc0 + 2 * it + 1)
                    e.reduce_max(gmax1[:], psum[2][0:1, 0:128], axis=AX.X
                                 ).then_inc(VC.h, 1)
                    VC.inc()
                    e.wait_ge(TC.h, s2_tc0 + 2 * it + 2)
                    e.tensor_scalar(ssoh[:], ss[:], psum[3][:, 0:1], None,
                                    op0=ALU.is_ge)
                    if it == 0:
                        e.tensor_copy(smax[:], gmax1[:])
                    else:
                        e.tensor_copy(gapA[:], gapB[:])
                    e.scalar_tensor_tensor(ss[:], ssoh[:], -1e30, ss[:],
                                           op0=ALU.mult, op1=ALU.add)
                    e.tensor_copy(ptbf[:, it::K], ssoh[:]).then_inc(VC.h, 1)
                    VC.inc()
            s2_vc_end = VC.n

            @block.tensor
            def _(e):
                for it in range(K):
                    e.wait_ge(VC.h, s2_vc0 + 3 * it + 1)
                    e.transpose(psum[2][0:1, 0:128], cmax[:], id32[:]
                                ).then_inc(TC.h, 1)
                    TC.inc()
                    e.wait_ge(VC.h, s2_vc0 + 3 * it + 2)
                    e.matmul(psum[3][:, 0:1], ones32r[:], gmax1[:],
                             start=True, stop=True).then_inc(TC.h, 1)
                    TC.inc()

            barrier(block)
            tap(block, "ptbf", ptbf[:])
            if stop_stage <= 2:
                return nc

            g2 = ExitStack(); root.enter_context(g2)
            sb2 = mk_sb(g2)
            xq = [sb2(f"xq{i}", [128, 1024], BF16) for i in range(2)]
            s2b_tc0 = TC.n
            ixq = [None] * 64

            @block.sync
            def _(e):
                for c in range(64):
                    if c >= 2:
                        e.wait_ge(TC.h, s2b_tc0 + (c - 2) + 1)
                    ixq[c] = SYN.issue(e, xq[c % 2][:],
                                       xscr[c * 128:(c + 1) * 128, :])

            @block.tensor
            def _(e):
                for c in range(64):
                    SYN.wait(e, ixq[c])
                    e.matmul(psum[0][0:K, :], ptbf[:, c * K:(c + 1) * K],
                             xq[c % 2][:, 0:512], start=(c == 0), stop=(c == 63))
                    e.matmul(psum[1][0:K, :], ptbf[:, c * K:(c + 1) * K],
                             xq[c % 2][:, 512:1024], start=(c == 0), stop=(c == 63)
                             ).then_inc(TC.h, 1)
                    TC.inc()
            s2b_tc_end = TC.n

            @block.vector
            def _(e):
                e.wait_ge(TC.h, s2b_tc_end)
                e.tensor_copy(qtokb[:, 0:512], psum[0][0:K, :])
                e.tensor_copy(qtokb[:, 512:1024], psum[1][0:K, :]).then_inc(VC.h, 1)
                VC.inc()

            barrier(block)
            g2.close()
            tap(block, "qtok", qtokb[:])
            if stop_stage <= 2.5:
                return nc

            g3 = ExitStack(); root.enter_context(g3)
            sb3 = mk_sb(g3)
            wq32 = sb3("wq32", [128, 8192])
            wqT = sb3("wqT", [128, 8192], BF16)
            wkbf = sb3("wkbf", [128, 8192], BF16)
            qtokT = sb3("qtokT", [128, 8 * K], BF16)
            qTsb = sb3("qTsb", [128, 8 * K], BF16)
            tq32 = sb3("tq32", [128, 8 * K])
            sq720 = [sb3(f"sq720_{i}", [128, RPAD]) for i in range(2)]
            qb_row = sb3("qb_row", [1, RPAD])
            rn_row = sb3("rn_row", [1, RPAD])
            eb_row = sb3("eb_row", [1, RPAD])
            tmp_row = sb3("tmp_row", [1, RPAD])

            @block.gpsimd
            def _(e):
                MGP.issue(e, wkbf[:].rearrange("p (r f) -> p r f", f=1024),
                          ipw[1024:2048, :].rearrange("(r p) f -> p r f", p=128))
            ig_s3w = len(MGP.marks) - 1

            @block.gpsimd
            def _(e):
                for i in range(32):
                    e.wait_ge(WCW.h, 16 * (32 + i - 1))
                    e.dma_start(out=wcb[i % 2][:].rearrange("p (b f) -> p b f", f=1024),
                                in_=w2r[i * 512:(i + 1) * 512, :]
                                .rearrange("(b p) f -> p b f", p=128)
                                ).then_inc(WCA.h, 16)
                    WCA.inc(16)
                    e.wait_ge(WCA.h, WCA.n)
                    e.dma_start(out=w2scr[i * 512:(i + 1) * 512, :]
                                .rearrange("(b p) f -> p b f", p=128),
                                in_=wcb[i % 2][:].rearrange("p (b f) -> p b f", f=1024)
                                ).then_inc(WCW.h, 16)
                    WCW.inc(16)

            @block.sync
            def _(e):
                MSC.issue(e, wq32[:].rearrange("p (r f) -> p r f", f=1024),
                          ipw[0:1024, :].rearrange("(r p) f -> p r f", p=128))
            i_s3w = len(MSC.marks) - 1

            s3_tc0, s3_vc0 = TC.n, VC.n

            @block.tensor
            def _(e):
                MSC.wait_upto(e, i_s3w)
                MGP.wait_upto(e, ig_s3w)
                n = 0
                for kc in range(8):
                    for oc in range(8):
                        if n >= 2:
                            e.wait_ge(VC.h, s3_vc0 + n - 1)
                        e.transpose(psum[n % 2][:, 0:128],
                                    wq32[:, oc * 1024 + kc * 128:oc * 1024 + (kc + 1) * 128],
                                    id32[:]).then_inc(TC.h, 1)
                        TC.inc(); n += 1
                for kc in range(8):
                    e.wait_ge(VC.h, s3_vc0 + n - 1)
                    e.transpose(psum[n % 2][0:128, 0:K],
                                qtokb[0:K, kc * 128:(kc + 1) * 128],
                                id32[0:K, 0:K]).then_inc(TC.h, 1)
                    TC.inc(); n += 1
            s3_tc_tr = TC.n

            @block.vector
            def _(e):
                n = 0
                for kc in range(8):
                    for oc in range(8):
                        e.wait_ge(TC.h, s3_tc0 + n + 1)
                        e.tensor_copy(wqT[:, kc * 1024 + oc * 128:kc * 1024 + (oc + 1) * 128],
                                      psum[n % 2][:, 0:128]).then_inc(VC.h, 1)
                        VC.inc(); n += 1
                for kc in range(8):
                    e.wait_ge(TC.h, s3_tc0 + n + 1)
                    e.tensor_copy(qtokT[:, kc * K:(kc + 1) * K],
                                  psum[n % 2][0:128, 0:K]).then_inc(VC.h, 1)
                    VC.inc(); n += 1
            s3_vc_tr = VC.n

            if stop_stage <= 2.6:
                barrier(block)
                return nc

            s3_tc1, s3_vc1 = TC.n, VC.n

            @block.tensor
            def _(e):
                e.wait_ge(VC.h, s3_vc_tr)
                for oc in range(8):
                    if oc >= 2:
                        e.wait_ge(VC.h, s3_vc1 + (oc - 2) + 1)
                    for kc in range(8):
                        mm = e.matmul(psum[oc % 2][:, 0:K],
                                      wqT[:, kc * 1024 + oc * 128:kc * 1024 + (oc + 1) * 128],
                                      qtokT[:, kc * K:(kc + 1) * K],
                                      start=(kc == 0), stop=(kc == 7))
                    mm.then_inc(TC.h, 1)
                    TC.inc()
            s3_tc_q = TC.n

            @block.vector
            def _(e):
                for oc in range(8):
                    e.wait_ge(TC.h, s3_tc1 + oc + 1)
                    e.tensor_scalar(qTsb[:, oc * K:(oc + 1) * K], psum[oc % 2][:, 0:K],
                                    bcols[:, oc:oc + 1], None, op0=ALU.add
                                    ).then_inc(VC.h, 1)
                    VC.inc()
                e.tensor_copy(gapA[:], gapB[:])
                for oc in range(8):
                    e.tensor_scalar(tq32[:, oc * K:(oc + 1) * K],
                                    qTsb[:, oc * K:(oc + 1) * K],
                                    bcols[:, 8 + oc:9 + oc], None, op0=ALU.mult)
                e.engine_nop().then_inc(VC.h, 1)
                VC.inc()
            s3_vc_q = VC.n

            if stop_stage <= 2.7:
                barrier(block)
                return nc

            s3_tc2, s3_vc2 = TC.n, VC.n

            @block.tensor
            def _(e):
                e.wait_ge(VC.h, s3_vc_q)
                n = 0
                for h in range(H):
                    r, po = h // 2, (h % 2) * 64
                    for fc in range(8):
                        if n >= 4:
                            e.wait_ge(VC.h, s3_vc2 + n - 3)
                        e.matmul(psum[4 + n % 4][0:128, 0:K],
                                 wkbf[po:po + 64, r * 1024 + fc * 128:r * 1024 + (fc + 1) * 128],
                                 qTsb[po:po + 64, r * K:(r + 1) * K],
                                 start=True, stop=True,
                                 tile_position=(po, 0)).then_inc(TC.h, 1)
                        TC.inc(); n += 1
                if stop_stage > 2.75:
                    for h in range(QB_N):
                        po = (h % 2) * 64
                        dst = (psum[2] if h % 2 == 0 else psum[3])[
                            0:1, (h // 2) * 64:(h // 2) * 64 + K]
                        e.matmul(dst, ones32c[po:po + 64, 0:1],
                                 tq32[po:po + 64, (h // 2) * K:(h // 2 + 1) * K],
                                 start=True, stop=True, tile_position=(po, 0))
                e.nop().then_inc(TC.h, 1)
                TC.inc()
            s3_tc_qb = TC.n

            @block.vector
            def _(e):
                n = 0
                for h in range(H):
                    for fc in range(8):
                        e.wait_ge(TC.h, s3_tc2 + n + 1)
                        e.tensor_copy(qWf[:, fc * RPAD + h * K:fc * RPAD + (h + 1) * K],
                                      psum[4 + n % 4][0:128, 0:K]).then_inc(VC.h, 1)
                        VC.inc(); n += 1
            s3_vc_qwf = VC.n

            if stop_stage <= 2.8:
                barrier(block)
                return nc

            s3_vc3 = VC.n

            @block.vector
            def _(e):
                e.wait_ge(VC.h, s3_vc_qwf)
                for fc in range(8):
                    if fc >= 2:
                        e.wait_ge(RN.h, fc - 1)
                    e.tensor_tensor(sq720[fc % 2][:],
                                    qWf[:, fc * RPAD:(fc + 1) * RPAD],
                                    qWf[:, fc * RPAD:(fc + 1) * RPAD],
                                    op=ALU.mult).then_inc(VC.h, 1)
                    VC.inc()

            @block.tensor
            def _(e):
                for fc in range(8):
                    e.wait_ge(VC.h, s3_vc3 + fc + 1)
                    e.matmul(psum[6][0:1, :], ones32c[:], sq720[fc % 2][:, 0:512],
                             start=(fc == 0), stop=(fc == 7))
                    e.matmul(psum[7][0:1, 0:256], ones32c[:], sq720[fc % 2][:, 512:768],
                             start=(fc == 0), stop=(fc == 7)).then_inc(RN.h, 1)
                    RN.inc()
                e.nop().then_inc(TC.h, 1)
                TC.inc()
            s3_tc_rn = TC.n

            if stop_stage <= 2.9:
                barrier(block)
                return nc

            s3_sc0 = SC.n
            @block.scalar
            def _(e):
                e.wait_ge(TC.h, s3_tc_rn)
                e.activation(out=rn_row[0:1, 0:512], in_=psum[6][0:1, :], func=AF.Sqrt)
                e.activation(out=rn_row[0:1, 512:768], in_=psum[7][0:1, 0:256],
                             func=AF.Sqrt)
                for h in range(H):
                    srcp = (psum[2] if h % 2 == 0 else psum[3])[
                        0:1, (h // 2) * 64:(h // 2) * 64 + K]
                    e.activation(out=qb_row[0:1, h * K:(h + 1) * K], in_=srcp,
                                 func=AF.Identity)
                e.activation(out=qb_row[0:1, 720:768], in_=psum[3][0:1, 464:512],
                             func=AF.Identity, scale=0.0)
                e.activation(out=tmp_row[:], in_=qb_row[:], func=AF.Abs)
                e.activation(out=smax[:], in_=smax[:], func=AF.Sqrt
                             ).then_inc(SC.h, 1)
                SC.inc()
            s3_sc_rows = SC.n

            s3_vc4 = VC.n
            @block.vector
            def _(e):
                e.wait_ge(SC.h, s3_sc_rows)
                e.tensor_scalar(rn_row[:], rn_row[:], smax[0:1, 0:1], None,
                                op0=ALU.mult)
                e.tensor_tensor(eb_row[:], qb_row[:], tmp_row[:], op=ALU.subtract)
                e.tensor_copy(gapA[:], gapB[:])
                e.tensor_tensor(eb_row[:], eb_row[:], rn_row[:], op=ALU.subtract)
                e.tensor_copy(gapA[:], gapB[:])
                e.tensor_scalar(eb_row[:], eb_row[:], 0.125, None, op0=ALU.mult)
                e.memset(eb_row[0:1, ROWS:RPAD], 0.0)
                e.engine_nop().then_inc(VC.h, 1)
                VC.inc()
            s3_vc_eb = VC.n

            s3_tc5, s3_vc5 = TC.n, VC.n
            @block.tensor
            def _(e):
                e.wait_ge(VC.h, s3_vc_eb)
                for rt in range(NRT):
                    if rt >= 2:
                        e.wait_ge(VC.h, s3_vc5 + (rt - 2) + 1)
                    e.transpose(psum[rt % 2][0:128, 0:1],
                                eb_row[0:1, rt * 128:(rt + 1) * 128],
                                id32[0:1, 0:1]).then_inc(TC.h, 1)
                    TC.inc()
            s3_tc_ebt = TC.n

            @block.vector
            def _(e):
                for rt in range(NRT):
                    e.wait_ge(TC.h, s3_tc5 + rt + 1)
                    e.tensor_copy(ebias[:, rt:rt + 1], psum[rt % 2][0:128, 0:1]
                                  ).then_inc(VC.h, 1)
                    VC.inc()

            barrier(block)
            tap(block, "wq32", wq32[:])
            tap(block, "wkbf", wkbf[:])
            tap(block, "wqT", wqT[:])
            tap(block, "qTsb", qTsb[:])
            tap(block, "tq32", tq32[:])
            tap(block, "qbrow", qb_row[:])
            g3.close()
            tap(block, "qWf", qWf[:])
            tap(block, "ebias", ebias[:])
            if stop_stage <= 3:
                return nc

            g4 = ExitStack(); root.enter_context(g4)
            sb4 = mk_sb(g4)
            SLICE = 2048
            NSL = S // SLICE
            xT = sb4("xT", [128, 8 * SLICE], BF16)
            attst = [sb4(f"attst{i}", [128, 512], BF16) for i in range(2)]
            s4_tc0, s4_sc0 = TC.n, SC.n
            ixT = [[None] * 8 for _ in range(NSL)]
            iaw = [None] * (NSL * NRT * 4)

            @block.sync
            def _(e):
                for sl in range(NSL):
                    if sl >= 1:
                        e.wait_ge(TC.h, s4_tc0 + sl * 24)
                    for fc in range(8):
                        ixT[sl][fc] = SYN.issue(
                            e, xT[:, fc * SLICE:(fc + 1) * SLICE],
                            xscr[sl * SLICE:(sl + 1) * SLICE,
                                 fc * 128:(fc + 1) * 128],
                            transpose=True)

            @block.tensor
            def _(e):
                n = 0
                for sl in range(NSL):
                    SYN.wait_upto(e, ixT[sl][7])
                    for rt in range(NRT):
                        for sub in range(4):
                            if n >= 2:
                                e.wait_ge(SC.h, s4_sc0 + (n - 2) + 1)
                            for kc in range(8):
                                mm = e.matmul(psum[n % 2][:],
                                              qWf[:, kc * RPAD + rt * 128:
                                                  kc * RPAD + (rt + 1) * 128],
                                              xT[:, kc * SLICE + sub * 512:
                                                 kc * SLICE + (sub + 1) * 512],
                                              start=(kc == 0), stop=(kc == 7))
                            mm.then_inc(TC.h, 1)
                            TC.inc(); n += 1

            @block.scalar
            def _(e):
                n = 0
                for sl in range(NSL):
                    for rt in range(NRT):
                        for sub in range(4):
                            e.wait_ge(TC.h, s4_tc0 + n + 1)
                            if n >= 2:
                                SCL.wait(e, iaw[n - 2])
                            e.activation(out=attst[n % 2][:], in_=psum[n % 2][:],
                                         func=AF.Exp, bias=ebias[:, rt:rt + 1],
                                         scale=0.125,
                                         accum_out=sump[:, rt * 16 + sl * 4 + sub:
                                                        rt * 16 + sl * 4 + sub + 1]
                                         ).then_inc(SC.h, 1)
                            SC.inc()
                            e.wait_ge(SC.h, SC.n)
                            iaw[n] = SCL.issue(
                                e, attnscr[rt * 128:(rt + 1) * 128,
                                           sl * SLICE + sub * 512:
                                           sl * SLICE + (sub + 1) * 512],
                                attst[n % 2][:])
                            n += 1
            s4_sc_end = SC.n

            @block.vector
            def _(e):
                e.wait_ge(SC.h, s4_sc_end)
                for rt in range(NRT):
                    e.reduce_sum(sumex[:, rt:rt + 1], sump[:, rt * 16:(rt + 1) * 16],
                                 axis=AX.X)
                e.tensor_copy(gapA[:], gapB[:])
                e.reciprocal(rsum[:], sumex[:])
                e.engine_nop().then_inc(VC.h, 1)
                VC.inc()

            barrier(block)
            g4.close()
            tap(block, "sumex", sumex[:])
            if stop_stage <= 4:
                return nc

            g5 = ExitStack(); root.enter_context(g5)
            sb5 = mk_sb(g5)
            atT = [sb5(f"atT{i}", [128, RPAD], BF16) for i in range(2)]
            xh = [sb5(f"xh{i}", [128, 512], BF16) for i in range(2)]
            ctxxn = sb5("ctxxn", [128, NRT * 1024])
            s5_tc0, s5_vc0 = TC.n, VC.n
            iat = [None] * 128
            ixh = [None] * 128

            @block.sync
            def _(e):
                for i in range(128):
                    c = i % 64
                    if i >= 2:
                        e.wait_ge(TC.h, s5_tc0 + (i - 1))
                    iat[i] = SYN.issue(e, atT[i % 2][:],
                                       attnscr[:, c * 128:(c + 1) * 128],
                                       transpose=True)

            @block.scalar
            def _(e):
                for i in range(128):
                    fh, c = i // 64, i % 64
                    if i >= 2:
                        e.wait_ge(TC.h, s5_tc0 + (i - 1))
                    ixh[i] = SCL.issue(e, xh[i % 2][:],
                                       xscr[c * 128:(c + 1) * 128,
                                            fh * 512:(fh + 1) * 512])

            @block.tensor
            def _(e):
                for i in range(128):
                    fh, c = i // 64, i % 64
                    SYN.wait(e, iat[i])
                    SCL.wait(e, ixh[i])
                    if fh == 1 and c == 0:
                        e.wait_ge(VC.h, s5_vc0 + NRT)
                    for rt in range(NRT):
                        mm = e.matmul(psum[rt][:],
                                      atT[i % 2][:, rt * 128:(rt + 1) * 128],
                                      xh[i % 2][:], start=(c == 0), stop=(c == 63))
                    mm.then_inc(TC.h, 1)
                    TC.inc()
            s5_tc_end = TC.n

            @block.vector
            def _(e):
                for fh in range(2):
                    e.wait_ge(TC.h, s5_tc0 + 64 * (fh + 1))
                    for rt in range(NRT):
                        e.tensor_scalar(ctxxn[:, rt * 1024 + fh * 512:
                                              rt * 1024 + (fh + 1) * 512],
                                        psum[rt][:], rsum[:, rt:rt + 1], None,
                                        op0=ALU.mult).then_inc(VC.h, 1)
                        VC.inc()

            barrier(block)
            tap(block, "ctxxn", ctxxn[:])
            if stop_stage <= 5:
                return nc

            g6 = ExitStack(); root.enter_context(g6)
            sb6 = mk_sb(g6)
            wv32 = sb6("wv32", [128, 8192])
            wvT = sb6("wvT", [128, H * 8 * 64], BF16)
            wo32 = sb6("wo32", [128, 8192])
            woT = sb6("woT", [128, 8192], BF16)
            ctxxT = sb6("ctxxT", [128, 8 * RPAD], BF16)
            ctxT = sb6("ctxT", [128, 8 * K], BF16)
            cm32 = sb6("cm32", [128, 1])
            p32 = [sb6(f"p32_{i}", [128, K]) for i in range(2)]

            @block.sync
            def _(e):
                MSC.issue(e, wv32[:].rearrange("p (r f) -> p r f", f=1024),
                          ipw[2048:3072, :].rearrange("(r p) f -> p r f", p=128))
                MSC.issue(e, wo32[:].rearrange("p (r f) -> p r f", f=1024),
                          outw.rearrange("(r p) f -> p r f", p=128))
            i_s6w = len(MSC.marks) - 1

            s6_tc0, s6_vc0 = TC.n, VC.n

            @block.tensor
            def _(e):
                MSC.wait_upto(e, i_s6w)
                n = 0
                for kc in range(8):
                    for oc in range(8):
                        if n >= 2:
                            e.wait_ge(VC.h, s6_vc0 + n - 1)
                        e.transpose(psum[n % 2][:, 0:128],
                                    wo32[:, oc * 1024 + kc * 128:oc * 1024 + (kc + 1) * 128],
                                    id32[:]).then_inc(TC.h, 1)
                        TC.inc(); n += 1
                for h in range(H):
                    r, po = h // 2, (h % 2) * 64
                    for fc in range(8):
                        e.wait_ge(VC.h, s6_vc0 + n - 1)
                        e.transpose(psum[n % 2][:, 0:64],
                                    wv32[po:po + 64, r * 1024 + fc * 128:r * 1024 + (fc + 1) * 128],
                                    id32[po:po + 64, po:po + 64],
                                    tile_position=(po, 0)).then_inc(TC.h, 1)
                        TC.inc(); n += 1
                for rt in range(NRT):
                    for fc in range(8):
                        e.wait_ge(VC.h, s6_vc0 + n - 1)
                        e.transpose(psum[n % 2][:, 0:128],
                                    ctxxn[:, rt * 1024 + fc * 128:rt * 1024 + (fc + 1) * 128],
                                    id32[:]).then_inc(TC.h, 1)
                        TC.inc(); n += 1
            s6_tc_tr = TC.n

            @block.vector
            def _(e):
                n = 0
                for kc in range(8):
                    for oc in range(8):
                        e.wait_ge(TC.h, s6_tc0 + n + 1)
                        e.tensor_copy(woT[:, kc * 1024 + oc * 128:kc * 1024 + (oc + 1) * 128],
                                      psum[n % 2][:, 0:128]).then_inc(VC.h, 1)
                        VC.inc(); n += 1
                for h in range(H):
                    for fc in range(8):
                        e.wait_ge(TC.h, s6_tc0 + n + 1)
                        e.tensor_copy(wvT[:, (h * 8 + fc) * 64:(h * 8 + fc + 1) * 64],
                                      psum[n % 2][:, 0:64]).then_inc(VC.h, 1)
                        VC.inc(); n += 1
                for rt in range(NRT):
                    for fc in range(8):
                        e.wait_ge(TC.h, s6_tc0 + n + 1)
                        e.tensor_copy(ctxxT[:, fc * RPAD + rt * 128:
                                            fc * RPAD + (rt + 1) * 128],
                                      psum[n % 2][:, 0:128]).then_inc(VC.h, 1)
                        VC.inc(); n += 1
            s6_vc_tr = VC.n

            s6_tc1, s6_vc1 = TC.n, VC.n

            @block.tensor
            def _(e):
                e.wait_ge(VC.h, s6_vc_tr)
                for h in range(H):
                    po = (h % 2) * 64
                    if h >= 2:
                        e.wait_ge(VC.h, s6_vc1 + (h - 2) + 1)
                    for fc in range(8):
                        mm = e.matmul(psum[h % 2][po:po + 64, 0:K],
                                      wvT[:, (h * 8 + fc) * 64:(h * 8 + fc + 1) * 64],
                                      ctxxT[:, fc * RPAD + h * K:fc * RPAD + (h + 1) * K],
                                      start=(fc == 0), stop=(fc == 7),
                                      tile_position=(0, po))
                    mm.then_inc(TC.h, 1)
                    TC.inc()
            s6_tc_ctx = TC.n

            @block.vector
            def _(e):
                for h in range(H):
                    po = (h % 2) * 64
                    e.wait_ge(TC.h, s6_tc1 + h + 1)
                    e.tensor_scalar(ctxT[po:po + 64, (h // 2) * K:(h // 2 + 1) * K],
                                    psum[h % 2][po:po + 64, 0:K],
                                    bcols[po:po + 64, 16 + h // 2:17 + h // 2],
                                    None, op0=ALU.add).then_inc(VC.h, 1)
                    VC.inc()
            s6_vc_ctx = VC.n

            @block.tensor
            def _(e):
                e.wait_ge(VC.h, s6_vc_ctx)
                for ec in range(8):
                    e.matmul(psum[2][0:K, :], ctxT[:, ec * K:(ec + 1) * K],
                             woT[:, ec * 1024:ec * 1024 + 512],
                             start=(ec == 0), stop=(ec == 7))
                    mm = e.matmul(psum[3][0:K, :], ctxT[:, ec * K:(ec + 1) * K],
                                  woT[:, ec * 1024 + 512:(ec + 1) * 1024],
                                  start=(ec == 0), stop=(ec == 7))
                mm.then_inc(TC.h, 1)
                TC.inc()
            s6_tc_os = TC.n

            @block.vector
            def _(e):
                e.wait_ge(TC.h, s6_tc_os)
                e.tensor_copy(oselr[:, 0:512], psum[2][0:K, :])
                e.tensor_copy(oselr[:, 512:1024], psum[3][0:K, :]
                              ).then_inc(VC.h, 1)
                VC.inc()
            s6_vc_os = VC.n

            s6_tc3, s6_vc3, s6_sc3 = TC.n, VC.n, SC.n

            @block.vector
            def _(e):
                for ci in range(16):
                    if ci >= 2:
                        e.wait_ge(TC.h, s6_tc3 + 2 * (ci - 1))
                    e.tensor_copy(p32[ci % 2][:], ptbf[:, ci * K:(ci + 1) * K]
                                  ).then_inc(VC.h, 1)
                    VC.inc()
                    if ci >= 1:
                        e.wait_ge(TC.h, s6_tc3 + 2 * ci)
                    e.reduce_sum(cm32[:], ptbf[:, ci * K:(ci + 1) * K], axis=AX.X
                                 ).then_inc(VC.h, 1)
                    VC.inc()

            @block.tensor
            def _(e):
                for ci in range(16):
                    if ci >= 2:
                        e.wait_ge(SC.h, s6_sc3 + 2 * (ci - 2) + 2)
                    e.wait_ge(VC.h, s6_vc3 + 2 * ci + 1)
                    e.transpose(psum[4 + (2 * ci) % 4][0:K, 0:128],
                                p32[ci % 2][:],
                                id32[:]).then_inc(TC.h, 1)
                    TC.inc()
                    e.wait_ge(VC.h, s6_vc3 + 2 * ci + 2)
                    e.transpose(psum[4 + (2 * ci + 1) % 4][0:1, 0:128], cm32[:],
                                id32[:]).then_inc(TC.h, 1)
                    TC.inc()

            @block.scalar
            def _(e):
                for ci in range(16):
                    e.wait_ge(TC.h, s6_tc3 + 2 * ci + 1)
                    e.activation(out=slhsP[:, ci * 128:(ci + 1) * 128],
                                 in_=psum[4 + (2 * ci) % 4][0:K, 0:128],
                                 func=AF.Identity).then_inc(SC.h, 1)
                    SC.inc()
                    e.wait_ge(TC.h, s6_tc3 + 2 * ci + 2)
                    e.activation(out=cmrow[0:1, ci * 128:(ci + 1) * 128],
                                 in_=psum[4 + (2 * ci + 1) % 4][0:1, 0:128],
                                 func=AF.Identity).then_inc(SC.h, 1)
                    SC.inc()

            barrier(block)
            g6.close()
            g5.close()
            g_mid.close()
            tap(block, "stackrhs", oselr[:])
            tap(block, "slhs", slhsP[:])
            if stop_stage <= 6:
                return nc

            g7 = ExitStack(); root.enter_context(g7)
            sb7 = mk_sb(g7)
            xlf = sb7("xlf", [128, 4096])
            y0b = sb7("y0b", [128, 4096])
            ffb = sb7("ffb", [128, 4096])
            y1T = sb7("y1T", [128, 8 * 512], BF16)
            Hsb = sb7("Hsb", [128, NHC * 512], BF16)
            ffbb = ffb[:].bitcast(BF16)
            wst = [ffbb[:, 4096 + i * 1024:4096 + (i + 1) * 1024] for i in range(4)]
            lnm = sb7("lnm", [128, 4])
            lnv = sb7("lnv", [128, 4])
            lnr = sb7("lnr", [128, 4])
            glogT = sb7("glogT", [128, 32])
            gms = sb7("gms", [128, 4])
            grs = sb7("grs", [128, 4])
            gmx = sb7("gmx", [128, 4])
            gbm = sb7("gbm", [128, 4])
            gse = sb7("gse", [128, 4])
            gg = sb7("gg", [128, 4])
            gt2 = sb7("gt2", [128, 4])
            ng = sb7("ng", [128, 4])
            sgap = sb7("sgap", [128, 1])

            for t in range(NTILE):
                st = {"tc0": TC.n, "vc0": VC.n, "sc0": SC.n}

                @block.sync
                def _(e, t=t, st=st):
                    st["i_xl"] = MSC.issue(
                        e, xlf[:].rearrange("p (c f) -> p c f", f=1024),
                        xb[t * TT:(t + 1) * TT, :]
                        .rearrange("(c p) f -> p c f", p=128))

                @block.tensor
                def _(e, t=t, st=st):
                    for ci in range(4):
                        gci = t * 4 + ci
                        for nh in range(2):
                            pb = psum[2 * ci + nh]
                            e.matmul(pb[:], slhsP[:, gci * 128:(gci + 1) * 128],
                                     oselr[:, nh * 512:(nh + 1) * 512],
                                     start=True, stop=False)
                            e.matmul(pb[:], cmrow[0:1, gci * 128:(gci + 1) * 128],
                                     obmrow[:, nh * 512:(nh + 1) * 512],
                                     start=False, stop=False)
                            mm = e.matmul(pb[:], onesbfr[:],
                                          meanrow[:, nh * 512:(nh + 1) * 512],
                                          start=False, stop=True)
                    mm.then_inc(TC.h, 1)
                    TC.inc()
                    st["tc_sc"] = TC.n

                @block.vector
                def _(e, t=t, st=st):
                    e.wait_ge(TC.h, st["tc_sc"])
                    MSC.wait(e, st["i_xl"])
                    for ci in range(4):
                        e.tensor_tensor(y0b[:, ci * 1024:ci * 1024 + 512],
                                        psum[2 * ci][:],
                                        xlf[:, ci * 1024:ci * 1024 + 512], op=ALU.add)
                        e.tensor_tensor(y0b[:, ci * 1024 + 512:(ci + 1) * 1024],
                                        psum[2 * ci + 1][:],
                                        xlf[:, ci * 1024 + 512:(ci + 1) * 1024],
                                        op=ALU.add)
                    e.engine_nop().then_inc(VC.h, 1); VC.inc()
                    st["vc_y0"] = VC.n
                    for ci in range(4):
                        e.reduce_sum(lnm[:, ci:ci + 1],
                                     y0b[:, ci * 1024:(ci + 1) * 1024], axis=AX.X)
                    e.tensor_copy(gapA[:], gapB[:])
                    e.tensor_scalar(lnm[:], lnm[:], 1.0 / 1024, None, op0=ALU.mult)
                    e.tensor_copy(gapA[:], gapB[:])
                    for ci in range(4):
                        e.tensor_scalar(xlf[:, ci * 1024:(ci + 1) * 1024],
                                        y0b[:, ci * 1024:(ci + 1) * 1024],
                                        lnm[:, ci:ci + 1], None, op0=ALU.subtract)
                    e.engine_nop().then_inc(VC.h, 1); VC.inc()
                    st["vc_xc"] = VC.n

                @block.scalar
                def _(e, t=t, st=st):
                    e.wait_ge(VC.h, st["vc_xc"])
                    for ci in range(4):
                        e.activation(out=ffb[:, ci * 1024:(ci + 1) * 1024],
                                     in_=xlf[:, ci * 1024:(ci + 1) * 1024],
                                     func=AF.Square, accum_out=lnv[:, ci:ci + 1])
                    e.activation(out=sgap[:], in_=gapB[:], func=AF.Identity)
                    e.activation(out=lnr[:], in_=lnv[:], func=AF.Ln,
                                 scale=1.0 / 1024, bias=cst[:, 0:1])
                    e.activation(out=lnr[:], in_=lnr[:], func=AF.Exp, scale=-0.5
                                 ).then_inc(SC.h, 1)
                    SC.inc()
                    st["sc_ln1"] = SC.n

                @block.vector
                def _(e, t=t, st=st):
                    e.wait_ge(SC.h, st["sc_ln1"])
                    for ci in range(4):
                        e.tensor_scalar(y0b[:, ci * 1024:(ci + 1) * 1024],
                                        xlf[:, ci * 1024:(ci + 1) * 1024],
                                        lnr[:, ci:ci + 1], None, op0=ALU.mult)
                    e.engine_nop().then_inc(VC.h, 1); VC.inc()
                    st["vc_y1"] = VC.n

                if t == 0 and "t0y1" in dbg_out:
                    @block.sync
                    def _(e, st=st):
                        e.wait_ge(VC.h, st["vc_y1"])
                        MSC.issue(e, dbg_out["t0y1"], y0b[:])

                @block.tensor
                def _(e, t=t, st=st):
                    e.wait_ge(VC.h, st["vc_y1"])
                    for kc in range(8):
                        if kc >= 2:
                            e.wait_ge(YE.h, 8 * t + kc - 1)
                        for ci in range(4):
                            tr = e.transpose(psum[kc % 2][:, ci * 128:(ci + 1) * 128],
                                             y0b[:, ci * 1024 + kc * 128:
                                                 ci * 1024 + (kc + 1) * 128],
                                             id32[:])
                        tr.then_inc(YT.h, 1)
                        YT.inc()

                @block.vector
                def _(e, t=t, st=st):
                    for kc in range(8):
                        e.wait_ge(YT.h, 8 * t + kc + 1)
                        e.tensor_copy(y1T[:, kc * 512:(kc + 1) * 512],
                                      psum[kc % 2][:, 0:512]).then_inc(YE.h, 1)
                        YE.inc()
                    e.engine_nop().then_inc(VC.h, 1); VC.inc()
                    st["vc_y1t"] = VC.n

                @block.tensor
                def _(e, t=t, st=st):
                    e.wait_ge(VC.h, st["vc_y1t"])
                    for kc in range(8):
                        mm = e.matmul(psum[4][0:8, :], gwT[:, kc * 8:(kc + 1) * 8],
                                      y1T[:, kc * 512:(kc + 1) * 512],
                                      start=(kc == 0), stop=(kc == 7))
                    mm.then_inc(TC.h, 1)
                    TC.inc()
                    st["tc_gl"] = TC.n

                @block.vector
                def _(e, t=t, st=st):
                    e.wait_ge(TC.h, st["tc_gl"])
                    e.tensor_scalar(xlf[0:8, 2048:2560], psum[4][0:8, :], gbcol[:, 0:1], None,
                                    op0=ALU.add).then_inc(VC.h, 1)
                    VC.inc()
                    st["vc_gl"] = VC.n

                @block.tensor
                def _(e, t=t, st=st):
                    e.wait_ge(VC.h, st["vc_gl"])
                    for ci in range(4):
                        tr = e.transpose(psum[5][0:128, ci * 8:(ci + 1) * 8],
                                         xlf[0:8, 2048 + ci * 128:2048 + (ci + 1) * 128],
                                         id32[0:8, 0:8])
                    tr.then_inc(TC.h, 1)
                    TC.inc()
                    st["tc_glt"] = TC.n

                @block.vector
                def _(e, t=t, st=st):
                    e.wait_ge(TC.h, st["tc_glt"])
                    e.tensor_copy(glogT[:], psum[5][0:128, 0:32]).then_inc(VC.h, 1)
                    VC.inc()
                    for ci in range(4):
                        e.tensor_tensor(xlf[:, 2560 + ci * 8:2560 + (ci + 1) * 8],
                                        glogT[:, ci * 8:(ci + 1) * 8],
                                        glogT[:, ci * 8:(ci + 1) * 8], op=ALU.mult)
                    for ci in range(4):
                        e.reduce_sum(gms[:, ci:ci + 1], xlf[:, 2560 + ci * 8:2560 + (ci + 1) * 8],
                                     axis=AX.X)
                    e.engine_nop().then_inc(VC.h, 1); VC.inc()
                    st["vc_gms"] = VC.n

                @block.scalar
                def _(e, t=t, st=st):
                    e.wait_ge(VC.h, st["vc_gms"])
                    e.activation(out=grs[:], in_=gms[:], func=AF.Ln,
                                 scale=1.0 / 8, bias=cst[:, 1:2])
                    e.activation(out=grs[:], in_=grs[:], func=AF.Exp, scale=-0.5
                                 ).then_inc(SC.h, 1)
                    SC.inc()
                    st["sc_grs"] = SC.n

                @block.vector
                def _(e, t=t, st=st):
                    e.wait_ge(SC.h, st["sc_grs"])
                    for ci in range(4):
                        e.tensor_scalar(xlf[:, 2592 + ci * 8:2592 + (ci + 1) * 8],
                                        glogT[:, ci * 8:(ci + 1) * 8],
                                        grs[:, ci:ci + 1], None, op0=ALU.mult)
                    e.tensor_copy(gapA[:], gapB[:])
                    for ci in range(4):
                        e.tensor_tensor(xlf[:, 2592 + ci * 8:2592 + (ci + 1) * 8],
                                        xlf[:, 2592 + ci * 8:2592 + (ci + 1) * 8],
                                        rmsb[:], op=ALU.mult)
                    e.tensor_copy(gapA[:], gapB[:])
                    for ci in range(4):
                        e.reduce_max(gmx[:, ci:ci + 1],
                                     xlf[:, 2592 + ci * 8:2592 + (ci + 1) * 8],
                                     axis=AX.X)
                    e.tensor_copy(gapA[:], gapB[:])
                    e.tensor_scalar(gbm[:], gmx[:], -2.0, None, op0=ALU.mult
                                    ).then_inc(VC.h, 1)
                    VC.inc()
                    st["vc_gbm"] = VC.n

                @block.scalar
                def _(e, t=t, st=st):
                    e.wait_ge(VC.h, st["vc_gbm"])
                    for ci in range(3):
                        e.activation(out=xlf[:, 2560 + ci * 8:2560 + (ci + 1) * 8],
                                     in_=xlf[:, 2592 + ci * 8:2592 + (ci + 1) * 8], func=AF.Exp,
                                     scale=2.0, bias=gbm[:, ci:ci + 1],
                                     accum_out=gse[:, ci:ci + 1])
                    e.activation(out=xlf[:, 2584:2592],
                                 in_=xlf[:, 2616:2624], func=AF.Exp,
                                 scale=2.0, bias=gbm[:, 3:4],
                                 accum_out=gse[:, 3:4]).then_inc(SC.h, 1)
                    SC.inc()
                    st["sc_gse"] = SC.n

                @block.vector
                def _(e, t=t, st=st):
                    e.wait_ge(SC.h, st["sc_gse"])
                    e.reciprocal(gg[:], gse[:])
                    e.tensor_copy(gapA[:], gapB[:])
                    e.tensor_scalar(gt2[:], gg[:], 1e-6, None, op0=ALU.add)
                    e.tensor_copy(gapA[:], gapB[:])
                    e.reciprocal(gt2[:], gt2[:])
                    e.tensor_copy(gapA[:], gapB[:])
                    e.tensor_tensor(ng[:], gg[:], gt2[:], op=ALU.mult
                                    ).then_inc(VC.h, 1)
                    VC.inc()
                    st["vc_ng"] = VC.n

                if t == 0 and "t0ng" in dbg_out:
                    @block.sync
                    def _(e, st=st):
                        e.wait_ge(VC.h, st["vc_ng"])
                        MSC.issue(e, dbg_out["t0ng"], ng[:])

                @block.sync
                def _(e, t=t, st=st):
                    iw1 = [None] * NHC
                    iw2 = [None] * NHC
                    st["iw1"] = iw1
                    st["iw2"] = iw2
                    e.wait_ge(SC.h, st["sc_ln1"])
                    for hc in range(NHC):
                        g = t * NHC + hc
                        if g >= 4:
                            e.wait_ge(WA.h, g - 3)
                        iw1[hc] = SYN.issue(e, wst[hc % 4], w1scr[hc])
                    for hc in range(NHC):
                        if hc < 4:
                            e.wait_ge(WA.h, (t + 1) * NHC)
                        else:
                            e.wait_ge(WB.h, t * NHC + hc - 3)
                        iw2[hc] = SYN.issue(e, wst[hc % 4],
                                            w2scr[hc * 128:(hc + 1) * 128, :])

                @block.tensor
                def _(e, t=t, st=st):
                    e.wait_ge(VC.h, st["vc_y1t"])
                    for hc in range(NHC):
                        SYN.wait(e, st["iw1"][hc])
                        if hc >= 4:
                            e.wait_ge(GE.h, t * NHC + hc - 3)
                        for kc in range(8):
                            mm = e.matmul(psum[hc % 4][:],
                                          wst[hc % 4][:, kc * 128:(kc + 1) * 128],
                                          y1T[:, kc * 512:(kc + 1) * 512],
                                          start=(kc == 0), stop=(kc == 7))
                        mm.then_inc(WA.h, 1)
                        WA.inc()

                @block.scalar
                def _(e, t=t, st=st):
                    for hc in range(NHC):
                        e.wait_ge(WA.h, t * NHC + hc + 1)
                        e.activation(out=Hsb[:, hc * 512:(hc + 1) * 512],
                                     in_=psum[hc % 4][:], func=AF.Gelu,
                                     bias=b1col[:, hc:hc + 1]).then_inc(GE.h, 1)
                        GE.inc()

                @block.tensor
                def _(e, t=t, st=st):
                    e.wait_ge(GE.h, (t + 1) * NHC)
                    for m in range(4):
                        for n_ in range(2):
                            e.matmul(psum[m * 2 + n_][:], onesbfr[:],
                                     b2sumr[:, n_ * 512:(n_ + 1) * 512],
                                     start=True, stop=False)
                    for hc in range(NHC):
                        SYN.wait(e, st["iw2"][hc])
                        for m in range(4):
                            for n_ in range(2):
                                mm = e.matmul(psum[m * 2 + n_][:],
                                              Hsb[:, hc * 512 + m * 128:
                                                  hc * 512 + (m + 1) * 128],
                                              wst[hc % 4][:, n_ * 512:(n_ + 1) * 512],
                                              start=False, stop=(hc == NHC - 1))
                        mm.then_inc(WB.h, 1)
                        WB.inc()

                @block.vector
                def _(e, t=t, st=st):
                    e.wait_ge(WB.h, (t + 1) * NHC)
                    e.wait_ge(VC.h, st["vc_ng"])
                    for m in range(4):
                        for n_ in range(2):
                            e.tensor_scalar(ffb[:, m * 1024 + n_ * 512:
                                                m * 1024 + (n_ + 1) * 512],
                                            psum[m * 2 + n_][:], ng[:, m:m + 1],
                                            None, op0=ALU.mult)
                    e.engine_nop().then_inc(VC.h, 1); VC.inc()
                    for ci in range(4):
                        e.tensor_tensor(xlf[:, ci * 1024:(ci + 1) * 1024],
                                        y0b[:, ci * 1024:(ci + 1) * 1024],
                                        ffb[:, ci * 1024:(ci + 1) * 1024],
                                        op=ALU.add)
                    e.engine_nop().then_inc(VC.h, 1); VC.inc()
                    for ci in range(4):
                        e.reduce_sum(lnm[:, ci:ci + 1],
                                     xlf[:, ci * 1024:(ci + 1) * 1024], axis=AX.X)
                    e.tensor_copy(gapA[:], gapB[:])
                    e.tensor_scalar(lnm[:], lnm[:], 1.0 / 1024, None, op0=ALU.mult)
                    e.tensor_copy(gapA[:], gapB[:])
                    for ci in range(4):
                        e.tensor_scalar(y0b[:, ci * 1024:(ci + 1) * 1024],
                                        xlf[:, ci * 1024:(ci + 1) * 1024],
                                        lnm[:, ci:ci + 1], None, op0=ALU.subtract)
                    e.engine_nop().then_inc(VC.h, 1); VC.inc()
                    st["vc_xc2"] = VC.n

                if t == 0 and "t0ff" in dbg_out:
                    @block.sync
                    def _(e, st=st):
                        e.wait_ge(VC.h, st["vc_xc2"] - 2)
                        MSC.issue(e, dbg_out["t0ff"], ffb[:])

                if t == 0 and "t0H" in dbg_out:
                    @block.sync
                    def _(e, st=st):
                        e.wait_ge(GE.h, NHC)
                        MSC.issue(e, dbg_out["t0H"], Hsb[:, 0:2048])

                @block.scalar
                def _(e, t=t, st=st):
                    e.wait_ge(VC.h, st["vc_xc2"])
                    for ci in range(4):
                        e.activation(out=ffb[:, ci * 1024:(ci + 1) * 1024],
                                     in_=y0b[:, ci * 1024:(ci + 1) * 1024],
                                     func=AF.Square, accum_out=lnv[:, ci:ci + 1])
                    e.activation(out=sgap[:], in_=gapB[:], func=AF.Identity)
                    e.activation(out=lnr[:], in_=lnv[:], func=AF.Ln,
                                 scale=1.0 / 1024, bias=cst[:, 0:1])
                    e.activation(out=lnr[:], in_=lnr[:], func=AF.Exp, scale=-0.5
                                 ).then_inc(SC.h, 1)
                    SC.inc()
                    st["sc_ln2"] = SC.n

                @block.vector
                def _(e, t=t, st=st):
                    e.wait_ge(SC.h, st["sc_ln2"])
                    for ci in range(4):
                        e.tensor_scalar(ffb[:, ci * 1024:(ci + 1) * 1024],
                                        y0b[:, ci * 1024:(ci + 1) * 1024],
                                        lnr[:, ci:ci + 1], None, op0=ALU.mult)
                    e.engine_nop().then_inc(VC.h, 1); VC.inc()
                    st["vc_y2"] = VC.n

                @block.scalar
                def _(e, t=t, st=st):
                    e.wait_ge(VC.h, st["vc_y2"])
                    SCL.issue(e, yout[t * TT:(t + 1) * TT, :]
                              .rearrange("(c p) f -> p c f", p=128),
                              ffb[:].rearrange("p (c f) -> p c f", f=1024))

                barrier(block)

            g7.close()

            qf = [(s.h, s.n) for s in ALLQ]

            @block.gpsimd
            def _(e):
                for h, v in qf:
                    e.wait_ge(h, v)

    return nc


_CACHED = {}


def _prep_inputs(L0_attn_x, L0_attn_in_proj_w, L0_attn_in_proj_b, L0_attn_out_w,
                 L0_attn_out_b, L0_ffn_gate_w, L0_ffn_gate_b, L0_ffn_rms_w,
                 L0_ffn_w1, L0_ffn_b1, L0_ffn_w2, L0_ffn_b2):
    c = lambda a: np.ascontiguousarray(np.asarray(a, dtype=np.float32))
    x = c(L0_attn_x)
    w1r = c(np.transpose(np.asarray(L0_ffn_w1, np.float32)
                         .reshape(8, 8, 128, 16, 128), (0, 3, 2, 1, 4))
            .reshape(NHC, 128, 1024))
    w2r = c(np.asarray(L0_ffn_w2, np.float32).reshape(NHID, E))
    b1c = c(np.asarray(L0_ffn_b1, np.float32).reshape(128, 128).T)
    common = {
        "ipw": c(L0_attn_in_proj_w),
        "ipb": c(np.asarray(L0_attn_in_proj_b, np.float32).reshape(24, 128)),
        "outw": c(L0_attn_out_w),
        "outb": c(np.asarray(L0_attn_out_b, np.float32).reshape(1, E)),
        "gatew": c(L0_ffn_gate_w),
        "gateb": c(np.asarray(L0_ffn_gate_b, np.float32).reshape(1, NEXP)),
        "rmsw": c(np.asarray(L0_ffn_rms_w, np.float32).reshape(1, NEXP)),
        "w1r": w1r, "w2r": w2r, "b1c": b1c,
        "b2": c(L0_ffn_b2),
        "ident": np.eye(128, dtype=np.float32),
        "onesc": np.ones((128, 1), np.float32),
        "onesr": np.ones((1, 128), np.float32),
    }
    in_maps = []
    for core in range(8):
        b, part = core // 4, core % 4
        m = dict(common)
        m["xb"] = c(np.roll(x[b], -part * NTOK, axis=0))
        in_maps.append(m)
    return in_maps


def kernel(**inputs):
    in_maps = _prep_inputs(**inputs)
    if "nc" not in _CACHED:
        _CACHED["nc"] = build_program()
    nc = _CACHED["nc"]
    res = run_bass_kernel_spmd(nc, in_maps, core_ids=list(range(8)))
    out = np.empty((B, S, E), np.float32)
    for core in range(8):
        b, part = core // 4, core % 4
        out[b, part * NTOK:(part + 1) * NTOK] = res.results[core]["y"]
    return out


# version 10
# speedup vs baseline: 1.0685x; 1.0685x over previous
import numpy as np
from contextlib import ExitStack

import concourse.bass as bass
import concourse.mybir as mybir
from concourse.bass_utils import run_bass_kernel_spmd

F32 = mybir.dt.float32
BF16 = mybir.dt.bfloat16
AF = mybir.ActivationFunctionType
ALU = mybir.AluOpType
AX = mybir.AxisListType

B, S, E = 2, 8192, 1024
H, HD = 16, 64
K = 45
NEXP, HID = 8, 2048
NHID = NEXP * HID
NTOK = S // 4
ROWS = H * K
RPAD = 768
NRT = 6
TT = 512
NTILE = NTOK // TT
NHC = NHID // 128
EPS32 = float(np.finfo(np.float32).eps)
QB_N = H


class Ctr:
    def __init__(self, h):
        self.h = h
        self.n = 0

    def inc(self, k=1):
        self.n += k
        return self.n


class DmaStream:

    def __init__(self, sems):
        self.sems = sems
        self.marks = []

    def issue(self, e, out, in_, **kw):
        s = self.sems[len(self.marks) % len(self.sems)]
        e.dma_start(out=out, in_=in_, **kw).then_inc(s.h, 16)
        s.inc(16)
        self.marks.append((s.h, s.n))
        return len(self.marks) - 1

    def wait(self, e, i):
        h, v = self.marks[i]
        e.wait_ge(h, v)

    def wait_upto(self, e, i):
        seen = {}
        for j in range(i + 1):
            h, v = self.marks[j]
            seen[id(h)] = (h, v)
        for h, v in seen.values():
            e.wait_ge(h, v)


def build_program(dbg=None, stop_stage=99):
    dbg = dbg or {}
    nc = bass.Bass("TRN2", target_bir_lowering=False,
                   detect_race_conditions=False)

    xb = nc.dram_tensor("xb", [S, E], F32, kind="ExternalInput").ap()
    ipw = nc.dram_tensor("ipw", [3 * E, E], F32, kind="ExternalInput").ap()
    ipb = nc.dram_tensor("ipb", [24, 128], F32, kind="ExternalInput").ap()
    outw = nc.dram_tensor("outw", [E, E], F32, kind="ExternalInput").ap()
    outb = nc.dram_tensor("outb", [1, E], F32, kind="ExternalInput").ap()
    gatew = nc.dram_tensor("gatew", [NEXP, E], F32, kind="ExternalInput").ap()
    gateb = nc.dram_tensor("gateb", [1, NEXP], F32, kind="ExternalInput").ap()
    rmsw = nc.dram_tensor("rmsw", [1, NEXP], F32, kind="ExternalInput").ap()
    w1r = nc.dram_tensor("w1r", [NHC, 128, 1024], F32, kind="ExternalInput").ap()
    w2r = nc.dram_tensor("w2r", [NHID, E], F32, kind="ExternalInput").ap()
    b1c = nc.dram_tensor("b1c", [128, 128], F32, kind="ExternalInput").ap()
    b2 = nc.dram_tensor("b2", [NEXP, E], F32, kind="ExternalInput").ap()
    ident = nc.dram_tensor("ident", [128, 128], F32, kind="ExternalInput").ap()
    onesc = nc.dram_tensor("onesc", [128, 1], F32, kind="ExternalInput").ap()
    onesr = nc.dram_tensor("onesr", [1, 128], F32, kind="ExternalInput").ap()
    yout = nc.dram_tensor("y", [NTOK, E], F32, kind="ExternalOutput").ap()
    dbg_out = {k: nc.dram_tensor("dbg_" + k, list(v[0]), v[1], kind="ExternalOutput").ap()
               for k, v in dbg.items()}

    xscr = nc.dram_tensor("xscr", [S, E], BF16, kind="Internal").ap()
    w1scr = nc.dram_tensor("w1scr", [NHC, 128, 1024], BF16, kind="Internal").ap()
    w2scr = nc.dram_tensor("w2scr", [NHID, E], BF16, kind="Internal").ap()
    attnscr = nc.dram_tensor("attnscr", [RPAD, S], BF16, kind="Internal").ap()

    root = ExitStack()
    sem = lambda name: Ctr(root.enter_context(nc.semaphore(name)))

    with root:
        def mk_sb(stack):
            return lambda name, shape, dt=F32: stack.enter_context(
                nc.sbuf_tensor(name, shape, dt))

        sb = mk_sb(root)

        id32 = sb("id32", [128, 128])
        ones32c = sb("ones32c", [128, 1])
        ones32r = sb("ones32r", [1, 128])
        onesbfr = sb("onesbfr", [1, 128], BF16)
        onesbfc = sb("onesbfc", [128, 1], BF16)
        bcols = sb("bcols", [128, 24])
        b1col = sb("b1col", [128, 128])
        b2sumr = sb("b2sumr", [1, 1024], BF16)
        rmsb = sb("rmsb", [128, NEXP])
        gwT = sb("gwT", [128, 64], BF16)
        gbcol = sb("gbcol", [8, 1])
        cmax = sb("cmax", [128, 1])
        gmax1 = sb("gmax1", [1, 1])
        smax = sb("smax", [1, 1])
        gapA = sb("gapA", [128, 1])
        gapB = sb("gapB", [128, 1])
        bsc = sb("bsc", [1, 1])
        cst = sb("cst", [128, 2])
        rsum = sb("rsum", [128, NRT])
        sumex = sb("sumex", [128, NRT])
        oselr = sb("oselr", [K, 1024], BF16)
        obmrow = sb("obmrow", [1, 1024], BF16)
        meanrow = sb("meanrow", [1, 1024], BF16)
        slhsP = sb("slhsP", [K, 16 * 128], BF16)
        cmrow = sb("cmrow", [1, 16 * 128], BF16)

        LS0 = sem("LS0"); LS1 = sem("LS1"); LS2 = sem("LS2"); LS3 = sem("LS3")
        LC0 = sem("LC0"); LC1 = sem("LC1"); LC2 = sem("LC2"); LC3 = sem("LC3")
        LG0 = sem("LG0"); LG1 = sem("LG1")
        LM = sem("LM")
        LN_ = sem("LN_")
        TC = sem("TC"); VC = sem("VC"); SC = sem("SC")
        BAR = sem("BAR")
        WA = sem("WA"); GE = sem("GE"); WB = sem("WB")
        WCA = sem("WCA"); WCW = sem("WCW")
        YT = sem("YT"); YE = sem("YE"); RN = sem("RN")
        SYN = DmaStream([LS0, LS1, LS2, LS3])
        SCL = DmaStream([LC0, LC1, LC2, LC3])
        GPS = DmaStream([LG0, LG1])
        MSC = DmaStream([LM])
        MGP = DmaStream([LN_])
        ALLQ = [LS0, LS1, LS2, LS3, LC0, LC1, LC2, LC3, LG0, LG1, LM, LN_]

        psum = [root.enter_context(nc.psum_tensor(f"ps{i}", [128, 512], F32))
                for i in range(8)]

        g_mid = ExitStack(); root.enter_context(g_mid)
        sbm = mk_sb(g_mid)
        ss = sbm("ss", [128, 64])
        ssoh = sbm("ssoh", [128, 64])
        ptbf = sbm("ptbf", [128, 64 * K], BF16)
        ebias = sbm("ebias", [128, NRT])
        sump = sbm("sump", [128, 96])
        qtokb = sbm("qtokb", [K, 1024])
        qWf = sbm("qWf", [128, 8 * RPAD], BF16)
        wcb = [sbm(f"wcb{i}", [128, 4096], BF16) for i in range(2)]

        taps = {}

        def barrier(block):
            BAR.inc(5)
            tgt = BAR.n
            qvals = [(s.h, s.n) for s in ALLQ]

            @block.sync
            def _(e):
                for h, v in qvals:
                    e.wait_ge(h, v)
                e.sem_inc(BAR.h, 1)
                e.wait_ge(BAR.h, tgt)

            @block.scalar
            def _(e):
                e.activation(out=bsc[:], in_=ones32c[0:1, 0:1],
                             func=AF.Identity).then_inc(BAR.h, 1)
                e.wait_ge(BAR.h, tgt)

            @block.gpsimd
            def _(e):
                e.engine_nop().then_inc(BAR.h, 1)
                e.wait_ge(BAR.h, tgt)

            @block.tensor
            def _(e):
                e.nop().then_inc(BAR.h, 1)
                e.wait_ge(BAR.h, tgt)

            @block.vector
            def _(e):
                e.engine_nop().then_inc(BAR.h, 1)
                e.wait_ge(BAR.h, tgt)


        def emit_cast_slice(block, lo, hi):
            @block.gpsimd
            def _(e):
                for i in range(lo, hi):
                    if i >= 2:
                        e.wait_ge(WCW.h, 16 * (i - 1))
                    if i < 32:
                        e.dma_start(out=wcb[i % 2][:].rearrange("p (b f) -> p b f", f=1024),
                                    in_=w1r[i * 4:(i + 1) * 4].rearrange("b p f -> p b f")
                                    ).then_inc(WCA.h, 16)
                        WCA.inc(16)
                        e.wait_ge(WCA.h, WCA.n)
                        e.dma_start(out=w1scr[i * 4:(i + 1) * 4].rearrange("b p f -> p b f"),
                                    in_=wcb[i % 2][:].rearrange("p (b f) -> p b f", f=1024)
                                    ).then_inc(WCW.h, 16)
                        WCW.inc(16)
                    else:
                        j = i - 32
                        e.dma_start(out=wcb[i % 2][:].rearrange("p (b f) -> p b f", f=1024),
                                    in_=w2r[j * 512:(j + 1) * 512, :]
                                    .rearrange("(b p) f -> p b f", p=128)
                                    ).then_inc(WCA.h, 16)
                        WCA.inc(16)
                        e.wait_ge(WCA.h, WCA.n)
                        e.dma_start(out=w2scr[j * 512:(j + 1) * 512, :]
                                    .rearrange("(b p) f -> p b f", p=128),
                                    in_=wcb[i % 2][:].rearrange("p (b f) -> p b f", f=1024)
                                    ).then_inc(WCW.h, 16)
                        WCW.inc(16)

        def tap(block, name, ap):
            if name not in dbg_out:
                return
            @block.sync
            def _(e):
                MSC.issue(e, dbg_out[name], ap)
            barrier(block)

        with nc.Block() as block:
            g1 = ExitStack(); root.enter_context(g1)
            sb1 = mk_sb(g1)
            ipb_sb = sb1("ipb_sb", [24, 128])
            b2_sb = sb1("b2_sb", [NEXP, 1024])
            rms_sb = sb1("rms_sb", [1, NEXP])
            gb_sb = sb1("gb_sb", [1, NEXP])
            gw_sb = sb1("gw_sb", [NEXP, 1024])
            outb_sb = sb1("outb_sb", [1, 1024], BF16)
            xf = [sb1(f"xf{i}", [128, 1024]) for i in range(2)]
            xbf = [sb1(f"xbf{i}", [128, 1024], BF16) for i in range(2)]
            junk1 = sb1("junk1", [128, 1024])

            @block.sync
            def _(e):
                for t, src in [(id32, ident), (ones32c, onesc), (ones32r, onesr),
                               (b1col, b1c), (ipb_sb, ipb), (b2_sb, b2),
                               (rms_sb, rmsw), (gb_sb, gateb), (gw_sb, gatew)]:
                    MSC.issue(e, t[:], src)

            @block.gpsimd
            def _(e):
                for t, src in [(onesbfr, onesr), (onesbfc, onesc),
                               (outb_sb, outb)]:
                    MGP.issue(e, t[:], src)
            i_const = len(MSC.marks) - 1
            ig_const = len(MGP.marks) - 1

            tc0 = TC.n
            vc0 = VC.n
            @block.tensor
            def _(e):
                MSC.wait_upto(e, i_const)
                MGP.wait_upto(e, ig_const)
                e.transpose(psum[0][:, 0:24], ipb_sb[:], id32[0:24, 0:24]
                            ).then_inc(TC.h, 1); TC.inc()
                e.matmul(psum[4][0:1, :], ones32c[0:8, 0:1], b2_sb[:, 0:512],
                         start=True, stop=True)
                e.matmul(psum[5][0:1, :], ones32c[0:8, 0:1], b2_sb[:, 512:1024],
                         start=True, stop=True).then_inc(TC.h, 1); TC.inc()
                e.matmul(psum[6][:, 0:8], ones32r[:], rms_sb[:],
                         start=True, stop=True).then_inc(TC.h, 1); TC.inc()
                e.transpose(psum[7][0:8, 0:1], gb_sb[:], id32[0:1, 0:1]
                            ).then_inc(TC.h, 1); TC.inc()
                for kc in range(8):
                    if kc >= 2:
                        e.wait_ge(VC.h, vc0 + 6 + (kc - 2) + 1)
                    e.transpose(psum[1 + kc % 2][:, 0:8],
                                gw_sb[:, kc * 128:(kc + 1) * 128],
                                id32[0:8, 0:8]).then_inc(TC.h, 1); TC.inc()
            tc_tab = TC.n

            @block.vector
            def _(e):
                e.wait_ge(TC.h, tc0 + 4)
                e.tensor_copy(bcols[:], psum[0][:, 0:24])
                e.tensor_copy(b2sumr[:, 0:512], psum[4][0:1, :])
                e.tensor_copy(b2sumr[:, 512:1024], psum[5][0:1, :])
                e.tensor_copy(rmsb[:], psum[6][:, 0:8])
                e.tensor_copy(gbcol[:], psum[7][0:8, 0:1])
                MGP.wait_upto(e, ig_const)
                e.memset(slhsP[:], 0.0)
                e.memset(qWf[:], 0.0)
                e.memset(gapB[:], 0.0)
                e.memset(cst[:, 0:1], 1e-5)
                e.memset(cst[:, 1:2], EPS32)
                e.engine_nop().then_inc(VC.h, 6); VC.inc(6)
                for kc in range(8):
                    e.wait_ge(TC.h, tc0 + 4 + kc + 1)
                    e.tensor_copy(gwT[:, kc * 8:(kc + 1) * 8],
                                  psum[1 + kc % 2][:, 0:8]).then_inc(VC.h, 1)
                    VC.inc()
            vc_tab = VC.n

            NXT = S // 128
            s1_sc0, s1_vc0, s1_tc0 = SC.n, VC.n, TC.n
            ixf = [None] * NXT
            iws = [None] * NXT

            @block.sync
            def _(e):
                for i in range(NXT):
                    if i >= 2:
                        e.wait_ge(SC.h, s1_sc0 + (i - 2) + 1)
                        e.wait_ge(VC.h, s1_vc0 + (i - 2) + 1)
                    ixf[i] = SYN.issue(e, xf[i % 2][:],
                                       xb[i * 128:(i + 1) * 128, :])

            @block.scalar
            def _(e):
                for i in range(NXT):
                    SYN.wait(e, ixf[i])
                    e.activation(out=junk1[:], in_=xf[i % 2][:], func=AF.Square,
                                 accum_out=ss[:, i:i + 1]).then_inc(SC.h, 1)
                    SC.inc()
                    e.wait_ge(VC.h, s1_vc0 + i + 1)
                    iws[i] = SCL.issue(e, xscr[i * 128:(i + 1) * 128, :],
                                       xbf[i % 2][:])

            @block.vector
            def _(e):
                for i in range(NXT):
                    SYN.wait(e, ixf[i])
                    if i >= 2:
                        SCL.wait(e, iws[i - 2])
                        e.wait_ge(TC.h, s1_tc0 + (i - 1))
                    e.tensor_copy(xbf[i % 2][:], xf[i % 2][:]).then_inc(VC.h, 1)
                    VC.inc()
            s1_vc_end = VC.n

            @block.tensor
            def _(e):
                for i in range(NXT):
                    e.wait_ge(VC.h, s1_vc0 + i + 1)
                    e.matmul(psum[6][0:1, :], onesbfc[:], xbf[i % 2][:, 0:512],
                             start=(i == 0), stop=(i == NXT - 1))
                    e.matmul(psum[7][0:1, :], onesbfc[:], xbf[i % 2][:, 512:1024],
                             start=(i == 0), stop=(i == NXT - 1)).then_inc(TC.h, 1)
                    TC.inc()
            s1_tc_end = TC.n

            @block.vector
            def _(e):
                e.wait_ge(TC.h, s1_tc_end)
                e.tensor_scalar(meanrow[0:1, 0:512], psum[6][0:1, :], 1.0 / S,
                                None, op0=ALU.mult)
                e.tensor_scalar(meanrow[0:1, 512:1024], psum[7][0:1, :], 1.0 / S,
                                None, op0=ALU.mult).then_inc(VC.h, 1)
                VC.inc()
                e.tensor_copy(gapA[:], gapB[:])
                e.tensor_tensor(obmrow[:], outb_sb[:], meanrow[:],
                                op=ALU.subtract).then_inc(VC.h, 1)
                VC.inc()

            emit_cast_slice(block, 0, 7)

            barrier(block)
            g1.close()
            tap(block, "ss", ss[:])
            tap(block, "mean", meanrow[:])
            if stop_stage <= 1:
                return nc

            s2_tc0, s2_vc0 = TC.n, VC.n

            @block.vector
            def _(e):
                for it in range(K):
                    e.wait_ge(TC.h, s2_tc0 + 2 * it)
                    e.reduce_max(cmax[:], ss[:], axis=AX.X).then_inc(VC.h, 1)
                    VC.inc()
                    e.wait_ge(TC.h, s2_tc0 + 2 * it + 1)
                    e.reduce_max(gmax1[:], psum[2][0:1, 0:128], axis=AX.X
                                 ).then_inc(VC.h, 1)
                    VC.inc()
                    e.wait_ge(TC.h, s2_tc0 + 2 * it + 2)
                    e.tensor_scalar(ssoh[:], ss[:], psum[3][:, 0:1], None,
                                    op0=ALU.is_ge)
                    if it == 0:
                        e.tensor_copy(smax[:], gmax1[:])
                    else:
                        e.tensor_copy(gapA[:], gapB[:])
                    e.scalar_tensor_tensor(ss[:], ssoh[:], -1e30, ss[:],
                                           op0=ALU.mult, op1=ALU.add)
                    e.tensor_copy(ptbf[:, it::K], ssoh[:]).then_inc(VC.h, 1)
                    VC.inc()
            s2_vc_end = VC.n

            @block.tensor
            def _(e):
                for it in range(K):
                    e.wait_ge(VC.h, s2_vc0 + 3 * it + 1)
                    e.transpose(psum[2][0:1, 0:128], cmax[:], id32[:]
                                ).then_inc(TC.h, 1)
                    TC.inc()
                    e.wait_ge(VC.h, s2_vc0 + 3 * it + 2)
                    e.matmul(psum[3][:, 0:1], ones32r[:], gmax1[:],
                             start=True, stop=True).then_inc(TC.h, 1)
                    TC.inc()

            barrier(block)
            tap(block, "ptbf", ptbf[:])
            if stop_stage <= 2:
                return nc

            emit_cast_slice(block, 7, 18)

            g2 = ExitStack(); root.enter_context(g2)
            sb2 = mk_sb(g2)
            xq = [sb2(f"xq{i}", [128, 1024], BF16) for i in range(2)]
            s2b_tc0 = TC.n
            ixq = [None] * 64

            @block.sync
            def _(e):
                for c in range(64):
                    if c >= 2:
                        e.wait_ge(TC.h, s2b_tc0 + (c - 2) + 1)
                    ixq[c] = SYN.issue(e, xq[c % 2][:],
                                       xscr[c * 128:(c + 1) * 128, :])

            @block.tensor
            def _(e):
                for c in range(64):
                    SYN.wait(e, ixq[c])
                    e.matmul(psum[0][0:K, :], ptbf[:, c * K:(c + 1) * K],
                             xq[c % 2][:, 0:512], start=(c == 0), stop=(c == 63))
                    e.matmul(psum[1][0:K, :], ptbf[:, c * K:(c + 1) * K],
                             xq[c % 2][:, 512:1024], start=(c == 0), stop=(c == 63)
                             ).then_inc(TC.h, 1)
                    TC.inc()
            s2b_tc_end = TC.n

            @block.vector
            def _(e):
                e.wait_ge(TC.h, s2b_tc_end)
                e.tensor_copy(qtokb[:, 0:512], psum[0][0:K, :])
                e.tensor_copy(qtokb[:, 512:1024], psum[1][0:K, :]).then_inc(VC.h, 1)
                VC.inc()

            barrier(block)
            g2.close()
            tap(block, "qtok", qtokb[:])
            if stop_stage <= 2.5:
                return nc

            g3 = ExitStack(); root.enter_context(g3)
            sb3 = mk_sb(g3)
            wq32 = sb3("wq32", [128, 8192])
            wqT = sb3("wqT", [128, 8192], BF16)
            wkbf = sb3("wkbf", [128, 8192], BF16)
            qtokT = sb3("qtokT", [128, 8 * K], BF16)
            qTsb = sb3("qTsb", [128, 8 * K], BF16)
            tq32 = sb3("tq32", [128, 8 * K])
            sq720 = [sb3(f"sq720_{i}", [128, RPAD]) for i in range(2)]
            qb_row = sb3("qb_row", [1, RPAD])
            rn_row = sb3("rn_row", [1, RPAD])
            eb_row = sb3("eb_row", [1, RPAD])
            tmp_row = sb3("tmp_row", [1, RPAD])

            @block.gpsimd
            def _(e):
                MGP.issue(e, wkbf[:].rearrange("p (r f) -> p r f", f=1024),
                          ipw[1024:2048, :].rearrange("(r p) f -> p r f", p=128))
            ig_s3w = len(MGP.marks) - 1
            emit_cast_slice(block, 18, 26)

            @block.sync
            def _(e):
                MSC.issue(e, wq32[:].rearrange("p (r f) -> p r f", f=1024),
                          ipw[0:1024, :].rearrange("(r p) f -> p r f", p=128))
            i_s3w = len(MSC.marks) - 1

            s3_tc0, s3_vc0 = TC.n, VC.n

            @block.tensor
            def _(e):
                MSC.wait_upto(e, i_s3w)
                MGP.wait_upto(e, ig_s3w)
                n = 0
                for kc in range(8):
                    for oc in range(8):
                        if n >= 2:
                            e.wait_ge(VC.h, s3_vc0 + n - 1)
                        e.transpose(psum[n % 2][:, 0:128],
                                    wq32[:, oc * 1024 + kc * 128:oc * 1024 + (kc + 1) * 128],
                                    id32[:]).then_inc(TC.h, 1)
                        TC.inc(); n += 1
                for kc in range(8):
                    e.wait_ge(VC.h, s3_vc0 + n - 1)
                    e.transpose(psum[n % 2][0:128, 0:K],
                                qtokb[0:K, kc * 128:(kc + 1) * 128],
                                id32[0:K, 0:K]).then_inc(TC.h, 1)
                    TC.inc(); n += 1
            s3_tc_tr = TC.n

            @block.vector
            def _(e):
                n = 0
                for kc in range(8):
                    for oc in range(8):
                        e.wait_ge(TC.h, s3_tc0 + n + 1)
                        e.tensor_copy(wqT[:, kc * 1024 + oc * 128:kc * 1024 + (oc + 1) * 128],
                                      psum[n % 2][:, 0:128]).then_inc(VC.h, 1)
                        VC.inc(); n += 1
                for kc in range(8):
                    e.wait_ge(TC.h, s3_tc0 + n + 1)
                    e.tensor_copy(qtokT[:, kc * K:(kc + 1) * K],
                                  psum[n % 2][0:128, 0:K]).then_inc(VC.h, 1)
                    VC.inc(); n += 1
            s3_vc_tr = VC.n

            if stop_stage <= 2.6:
                barrier(block)
                return nc

            s3_tc1, s3_vc1 = TC.n, VC.n

            @block.tensor
            def _(e):
                e.wait_ge(VC.h, s3_vc_tr)
                for oc in range(8):
                    if oc >= 2:
                        e.wait_ge(VC.h, s3_vc1 + (oc - 2) + 1)
                    for kc in range(8):
                        mm = e.matmul(psum[oc % 2][:, 0:K],
                                      wqT[:, kc * 1024 + oc * 128:kc * 1024 + (oc + 1) * 128],
                                      qtokT[:, kc * K:(kc + 1) * K],
                                      start=(kc == 0), stop=(kc == 7))
                    mm.then_inc(TC.h, 1)
                    TC.inc()
            s3_tc_q = TC.n

            @block.vector
            def _(e):
                for oc in range(8):
                    e.wait_ge(TC.h, s3_tc1 + oc + 1)
                    e.tensor_scalar(qTsb[:, oc * K:(oc + 1) * K], psum[oc % 2][:, 0:K],
                                    bcols[:, oc:oc + 1], None, op0=ALU.add
                                    ).then_inc(VC.h, 1)
                    VC.inc()
                e.tensor_copy(gapA[:], gapB[:])
                for oc in range(8):
                    e.tensor_scalar(tq32[:, oc * K:(oc + 1) * K],
                                    qTsb[:, oc * K:(oc + 1) * K],
                                    bcols[:, 8 + oc:9 + oc], None, op0=ALU.mult)
                e.engine_nop().then_inc(VC.h, 1)
                VC.inc()
            s3_vc_q = VC.n

            if stop_stage <= 2.7:
                barrier(block)
                return nc

            s3_tc2, s3_vc2 = TC.n, VC.n

            @block.tensor
            def _(e):
                e.wait_ge(VC.h, s3_vc_q)
                n = 0
                for h in range(H):
                    r, po = h // 2, (h % 2) * 64
                    for fc in range(8):
                        if n >= 4:
                            e.wait_ge(VC.h, s3_vc2 + n - 3)
                        e.matmul(psum[4 + n % 4][0:128, 0:K],
                                 wkbf[po:po + 64, r * 1024 + fc * 128:r * 1024 + (fc + 1) * 128],
                                 qTsb[po:po + 64, r * K:(r + 1) * K],
                                 start=True, stop=True,
                                 tile_position=(po, 0)).then_inc(TC.h, 1)
                        TC.inc(); n += 1
                if stop_stage > 2.75:
                    for h in range(QB_N):
                        po = (h % 2) * 64
                        dst = (psum[2] if h % 2 == 0 else psum[3])[
                            0:1, (h // 2) * 64:(h // 2) * 64 + K]
                        e.matmul(dst, ones32c[po:po + 64, 0:1],
                                 tq32[po:po + 64, (h // 2) * K:(h // 2 + 1) * K],
                                 start=True, stop=True, tile_position=(po, 0))
                e.nop().then_inc(TC.h, 1)
                TC.inc()
            s3_tc_qb = TC.n

            @block.vector
            def _(e):
                n = 0
                for h in range(H):
                    for fc in range(8):
                        e.wait_ge(TC.h, s3_tc2 + n + 1)
                        e.tensor_copy(qWf[:, fc * RPAD + h * K:fc * RPAD + (h + 1) * K],
                                      psum[4 + n % 4][0:128, 0:K]).then_inc(VC.h, 1)
                        VC.inc(); n += 1
            s3_vc_qwf = VC.n

            if stop_stage <= 2.8:
                barrier(block)
                return nc

            s3_vc3 = VC.n

            @block.vector
            def _(e):
                e.wait_ge(VC.h, s3_vc_qwf)
                for fc in range(8):
                    if fc >= 2:
                        e.wait_ge(RN.h, fc - 1)
                    e.tensor_tensor(sq720[fc % 2][:],
                                    qWf[:, fc * RPAD:(fc + 1) * RPAD],
                                    qWf[:, fc * RPAD:(fc + 1) * RPAD],
                                    op=ALU.mult).then_inc(VC.h, 1)
                    VC.inc()

            @block.tensor
            def _(e):
                for fc in range(8):
                    e.wait_ge(VC.h, s3_vc3 + fc + 1)
                    e.matmul(psum[6][0:1, :], ones32c[:], sq720[fc % 2][:, 0:512],
                             start=(fc == 0), stop=(fc == 7))
                    e.matmul(psum[7][0:1, 0:256], ones32c[:], sq720[fc % 2][:, 512:768],
                             start=(fc == 0), stop=(fc == 7)).then_inc(RN.h, 1)
                    RN.inc()
                e.nop().then_inc(TC.h, 1)
                TC.inc()
            s3_tc_rn = TC.n

            if stop_stage <= 2.9:
                barrier(block)
                return nc

            s3_sc0 = SC.n
            @block.scalar
            def _(e):
                e.wait_ge(TC.h, s3_tc_rn)
                e.activation(out=rn_row[0:1, 0:512], in_=psum[6][0:1, :], func=AF.Sqrt)
                e.activation(out=rn_row[0:1, 512:768], in_=psum[7][0:1, 0:256],
                             func=AF.Sqrt)
                for h in range(H):
                    srcp = (psum[2] if h % 2 == 0 else psum[3])[
                        0:1, (h // 2) * 64:(h // 2) * 64 + K]
                    e.activation(out=qb_row[0:1, h * K:(h + 1) * K], in_=srcp,
                                 func=AF.Identity)
                e.activation(out=qb_row[0:1, 720:768], in_=psum[3][0:1, 464:512],
                             func=AF.Identity, scale=0.0)
                e.activation(out=tmp_row[:], in_=qb_row[:], func=AF.Abs)
                e.activation(out=smax[:], in_=smax[:], func=AF.Sqrt
                             ).then_inc(SC.h, 1)
                SC.inc()
            s3_sc_rows = SC.n

            s3_vc4 = VC.n
            @block.vector
            def _(e):
                e.wait_ge(SC.h, s3_sc_rows)
                e.tensor_scalar(rn_row[:], rn_row[:], smax[0:1, 0:1], None,
                                op0=ALU.mult)
                e.tensor_tensor(eb_row[:], qb_row[:], tmp_row[:], op=ALU.subtract)
                e.tensor_copy(gapA[:], gapB[:])
                e.tensor_tensor(eb_row[:], eb_row[:], rn_row[:], op=ALU.subtract)
                e.tensor_copy(gapA[:], gapB[:])
                e.tensor_scalar(eb_row[:], eb_row[:], 0.125, None, op0=ALU.mult)
                e.memset(eb_row[0:1, ROWS:RPAD], 0.0)
                e.engine_nop().then_inc(VC.h, 1)
                VC.inc()
            s3_vc_eb = VC.n

            s3_tc5, s3_vc5 = TC.n, VC.n
            @block.tensor
            def _(e):
                e.wait_ge(VC.h, s3_vc_eb)
                for rt in range(NRT):
                    if rt >= 2:
                        e.wait_ge(VC.h, s3_vc5 + (rt - 2) + 1)
                    e.transpose(psum[rt % 2][0:128, 0:1],
                                eb_row[0:1, rt * 128:(rt + 1) * 128],
                                id32[0:1, 0:1]).then_inc(TC.h, 1)
                    TC.inc()
            s3_tc_ebt = TC.n

            @block.vector
            def _(e):
                for rt in range(NRT):
                    e.wait_ge(TC.h, s3_tc5 + rt + 1)
                    e.tensor_copy(ebias[:, rt:rt + 1], psum[rt % 2][0:128, 0:1]
                                  ).then_inc(VC.h, 1)
                    VC.inc()

            barrier(block)
            tap(block, "wq32", wq32[:])
            tap(block, "wkbf", wkbf[:])
            tap(block, "wqT", wqT[:])
            tap(block, "qTsb", qTsb[:])
            tap(block, "tq32", tq32[:])
            tap(block, "qbrow", qb_row[:])
            g3.close()
            tap(block, "qWf", qWf[:])
            tap(block, "ebias", ebias[:])
            if stop_stage <= 3:
                return nc

            emit_cast_slice(block, 26, 48)

            g4 = ExitStack(); root.enter_context(g4)
            sb4 = mk_sb(g4)
            SLICE = 2048
            NSL = S // SLICE
            xT = sb4("xT", [128, 8 * SLICE], BF16)
            attst = [sb4(f"attst{i}", [128, 512], BF16) for i in range(2)]
            s4_tc0, s4_sc0 = TC.n, SC.n
            ixT = [[None] * 8 for _ in range(NSL)]
            iaw = [None] * (NSL * NRT * 4)

            @block.sync
            def _(e):
                for sl in range(NSL):
                    if sl >= 1:
                        e.wait_ge(TC.h, s4_tc0 + sl * 24)
                    for fc in range(8):
                        ixT[sl][fc] = SYN.issue(
                            e, xT[:, fc * SLICE:(fc + 1) * SLICE],
                            xscr[sl * SLICE:(sl + 1) * SLICE,
                                 fc * 128:(fc + 1) * 128],
                            transpose=True)

            @block.tensor
            def _(e):
                n = 0
                for sl in range(NSL):
                    SYN.wait_upto(e, ixT[sl][7])
                    for rt in range(NRT):
                        for sub in range(4):
                            if n >= 2:
                                e.wait_ge(SC.h, s4_sc0 + (n - 2) + 1)
                            for kc in range(8):
                                mm = e.matmul(psum[n % 2][:],
                                              qWf[:, kc * RPAD + rt * 128:
                                                  kc * RPAD + (rt + 1) * 128],
                                              xT[:, kc * SLICE + sub * 512:
                                                 kc * SLICE + (sub + 1) * 512],
                                              start=(kc == 0), stop=(kc == 7))
                            mm.then_inc(TC.h, 1)
                            TC.inc(); n += 1

            @block.scalar
            def _(e):
                n = 0
                for sl in range(NSL):
                    for rt in range(NRT):
                        for sub in range(4):
                            e.wait_ge(TC.h, s4_tc0 + n + 1)
                            if n >= 2:
                                SCL.wait(e, iaw[n - 2])
                            e.activation(out=attst[n % 2][:], in_=psum[n % 2][:],
                                         func=AF.Exp, bias=ebias[:, rt:rt + 1],
                                         scale=0.125,
                                         accum_out=sump[:, rt * 16 + sl * 4 + sub:
                                                        rt * 16 + sl * 4 + sub + 1]
                                         ).then_inc(SC.h, 1)
                            SC.inc()
                            e.wait_ge(SC.h, SC.n)
                            iaw[n] = SCL.issue(
                                e, attnscr[rt * 128:(rt + 1) * 128,
                                           sl * SLICE + sub * 512:
                                           sl * SLICE + (sub + 1) * 512],
                                attst[n % 2][:])
                            n += 1
            s4_sc_end = SC.n

            @block.vector
            def _(e):
                e.wait_ge(SC.h, s4_sc_end)
                for rt in range(NRT):
                    e.reduce_sum(sumex[:, rt:rt + 1], sump[:, rt * 16:(rt + 1) * 16],
                                 axis=AX.X)
                e.tensor_copy(gapA[:], gapB[:])
                e.reciprocal(rsum[:], sumex[:])
                e.engine_nop().then_inc(VC.h, 1)
                VC.inc()

            barrier(block)
            g4.close()
            tap(block, "sumex", sumex[:])
            if stop_stage <= 4:
                return nc

            emit_cast_slice(block, 48, 64)

            g5 = ExitStack(); root.enter_context(g5)
            sb5 = mk_sb(g5)
            atT = [sb5(f"atT{i}", [128, RPAD], BF16) for i in range(2)]
            xh = [sb5(f"xh{i}", [128, 512], BF16) for i in range(2)]
            ctxxn = sb5("ctxxn", [128, NRT * 1024])
            s5_tc0, s5_vc0 = TC.n, VC.n
            iat = [None] * 128
            ixh = [None] * 128

            @block.sync
            def _(e):
                for i in range(128):
                    c = i % 64
                    if i >= 2:
                        e.wait_ge(TC.h, s5_tc0 + (i - 1))
                    iat[i] = SYN.issue(e, atT[i % 2][:],
                                       attnscr[:, c * 128:(c + 1) * 128],
                                       transpose=True)

            @block.scalar
            def _(e):
                for i in range(128):
                    fh, c = i // 64, i % 64
                    if i >= 2:
                        e.wait_ge(TC.h, s5_tc0 + (i - 1))
                    ixh[i] = SCL.issue(e, xh[i % 2][:],
                                       xscr[c * 128:(c + 1) * 128,
                                            fh * 512:(fh + 1) * 512])

            @block.tensor
            def _(e):
                for i in range(128):
                    fh, c = i // 64, i % 64
                    SYN.wait(e, iat[i])
                    SCL.wait(e, ixh[i])
                    if fh == 1 and c == 0:
                        e.wait_ge(VC.h, s5_vc0 + NRT)
                    for rt in range(NRT):
                        mm = e.matmul(psum[rt][:],
                                      atT[i % 2][:, rt * 128:(rt + 1) * 128],
                                      xh[i % 2][:], start=(c == 0), stop=(c == 63))
                    mm.then_inc(TC.h, 1)
                    TC.inc()
            s5_tc_end = TC.n

            @block.vector
            def _(e):
                for fh in range(2):
                    e.wait_ge(TC.h, s5_tc0 + 64 * (fh + 1))
                    for rt in range(NRT):
                        e.tensor_scalar(ctxxn[:, rt * 1024 + fh * 512:
                                              rt * 1024 + (fh + 1) * 512],
                                        psum[rt][:], rsum[:, rt:rt + 1], None,
                                        op0=ALU.mult).then_inc(VC.h, 1)
                        VC.inc()

            barrier(block)
            tap(block, "ctxxn", ctxxn[:])
            if stop_stage <= 5:
                return nc

            g6 = ExitStack(); root.enter_context(g6)
            sb6 = mk_sb(g6)
            wv32 = sb6("wv32", [128, 8192])
            wvT = sb6("wvT", [128, H * 8 * 64], BF16)
            wo32 = sb6("wo32", [128, 8192])
            woT = sb6("woT", [128, 8192], BF16)
            ctxxT = sb6("ctxxT", [128, 8 * RPAD], BF16)
            ctxT = sb6("ctxT", [128, 8 * K], BF16)
            cm32 = sb6("cm32", [128, 1])
            p32 = [sb6(f"p32_{i}", [128, K]) for i in range(2)]

            @block.sync
            def _(e):
                MSC.issue(e, wv32[:].rearrange("p (r f) -> p r f", f=1024),
                          ipw[2048:3072, :].rearrange("(r p) f -> p r f", p=128))
                MSC.issue(e, wo32[:].rearrange("p (r f) -> p r f", f=1024),
                          outw.rearrange("(r p) f -> p r f", p=128))
            i_s6w = len(MSC.marks) - 1

            s6_tc0, s6_vc0 = TC.n, VC.n

            @block.tensor
            def _(e):
                MSC.wait_upto(e, i_s6w)
                n = 0
                for kc in range(8):
                    for oc in range(8):
                        if n >= 2:
                            e.wait_ge(VC.h, s6_vc0 + n - 1)
                        e.transpose(psum[n % 2][:, 0:128],
                                    wo32[:, oc * 1024 + kc * 128:oc * 1024 + (kc + 1) * 128],
                                    id32[:]).then_inc(TC.h, 1)
                        TC.inc(); n += 1
                for h in range(H):
                    r, po = h // 2, (h % 2) * 64
                    for fc in range(8):
                        e.wait_ge(VC.h, s6_vc0 + n - 1)
                        e.transpose(psum[n % 2][:, 0:64],
                                    wv32[po:po + 64, r * 1024 + fc * 128:r * 1024 + (fc + 1) * 128],
                                    id32[po:po + 64, po:po + 64],
                                    tile_position=(po, 0)).then_inc(TC.h, 1)
                        TC.inc(); n += 1
                for rt in range(NRT):
                    for fc in range(8):
                        e.wait_ge(VC.h, s6_vc0 + n - 1)
                        e.transpose(psum[n % 2][:, 0:128],
                                    ctxxn[:, rt * 1024 + fc * 128:rt * 1024 + (fc + 1) * 128],
                                    id32[:]).then_inc(TC.h, 1)
                        TC.inc(); n += 1
            s6_tc_tr = TC.n

            @block.vector
            def _(e):
                n = 0
                for kc in range(8):
                    for oc in range(8):
                        e.wait_ge(TC.h, s6_tc0 + n + 1)
                        e.tensor_copy(woT[:, kc * 1024 + oc * 128:kc * 1024 + (oc + 1) * 128],
                                      psum[n % 2][:, 0:128]).then_inc(VC.h, 1)
                        VC.inc(); n += 1
                for h in range(H):
                    for fc in range(8):
                        e.wait_ge(TC.h, s6_tc0 + n + 1)
                        e.tensor_copy(wvT[:, (h * 8 + fc) * 64:(h * 8 + fc + 1) * 64],
                                      psum[n % 2][:, 0:64]).then_inc(VC.h, 1)
                        VC.inc(); n += 1
                for rt in range(NRT):
                    for fc in range(8):
                        e.wait_ge(TC.h, s6_tc0 + n + 1)
                        e.tensor_copy(ctxxT[:, fc * RPAD + rt * 128:
                                            fc * RPAD + (rt + 1) * 128],
                                      psum[n % 2][:, 0:128]).then_inc(VC.h, 1)
                        VC.inc(); n += 1
            s6_vc_tr = VC.n

            s6_tc1, s6_vc1 = TC.n, VC.n

            @block.tensor
            def _(e):
                e.wait_ge(VC.h, s6_vc_tr)
                for h in range(H):
                    po = (h % 2) * 64
                    if h >= 2:
                        e.wait_ge(VC.h, s6_vc1 + (h - 2) + 1)
                    for fc in range(8):
                        mm = e.matmul(psum[h % 2][po:po + 64, 0:K],
                                      wvT[:, (h * 8 + fc) * 64:(h * 8 + fc + 1) * 64],
                                      ctxxT[:, fc * RPAD + h * K:fc * RPAD + (h + 1) * K],
                                      start=(fc == 0), stop=(fc == 7),
                                      tile_position=(0, po))
                    mm.then_inc(TC.h, 1)
                    TC.inc()
            s6_tc_ctx = TC.n

            @block.vector
            def _(e):
                for h in range(H):
                    po = (h % 2) * 64
                    e.wait_ge(TC.h, s6_tc1 + h + 1)
                    e.tensor_scalar(ctxT[po:po + 64, (h // 2) * K:(h // 2 + 1) * K],
                                    psum[h % 2][po:po + 64, 0:K],
                                    bcols[po:po + 64, 16 + h // 2:17 + h // 2],
                                    None, op0=ALU.add).then_inc(VC.h, 1)
                    VC.inc()
            s6_vc_ctx = VC.n

            @block.tensor
            def _(e):
                e.wait_ge(VC.h, s6_vc_ctx)
                for ec in range(8):
                    e.matmul(psum[2][0:K, :], ctxT[:, ec * K:(ec + 1) * K],
                             woT[:, ec * 1024:ec * 1024 + 512],
                             start=(ec == 0), stop=(ec == 7))
                    mm = e.matmul(psum[3][0:K, :], ctxT[:, ec * K:(ec + 1) * K],
                                  woT[:, ec * 1024 + 512:(ec + 1) * 1024],
                                  start=(ec == 0), stop=(ec == 7))
                mm.then_inc(TC.h, 1)
                TC.inc()
            s6_tc_os = TC.n

            @block.vector
            def _(e):
                e.wait_ge(TC.h, s6_tc_os)
                e.tensor_copy(oselr[:, 0:512], psum[2][0:K, :])
                e.tensor_copy(oselr[:, 512:1024], psum[3][0:K, :]
                              ).then_inc(VC.h, 1)
                VC.inc()
            s6_vc_os = VC.n

            s6_tc3, s6_vc3, s6_sc3 = TC.n, VC.n, SC.n

            @block.vector
            def _(e):
                for ci in range(16):
                    if ci >= 2:
                        e.wait_ge(TC.h, s6_tc3 + 2 * (ci - 1))
                    e.tensor_copy(p32[ci % 2][:], ptbf[:, ci * K:(ci + 1) * K]
                                  ).then_inc(VC.h, 1)
                    VC.inc()
                    if ci >= 1:
                        e.wait_ge(TC.h, s6_tc3 + 2 * ci)
                    e.reduce_sum(cm32[:], ptbf[:, ci * K:(ci + 1) * K], axis=AX.X
                                 ).then_inc(VC.h, 1)
                    VC.inc()

            @block.tensor
            def _(e):
                for ci in range(16):
                    if ci >= 2:
                        e.wait_ge(SC.h, s6_sc3 + 2 * (ci - 2) + 2)
                    e.wait_ge(VC.h, s6_vc3 + 2 * ci + 1)
                    e.transpose(psum[4 + (2 * ci) % 4][0:K, 0:128],
                                p32[ci % 2][:],
                                id32[:]).then_inc(TC.h, 1)
                    TC.inc()
                    e.wait_ge(VC.h, s6_vc3 + 2 * ci + 2)
                    e.transpose(psum[4 + (2 * ci + 1) % 4][0:1, 0:128], cm32[:],
                                id32[:]).then_inc(TC.h, 1)
                    TC.inc()

            @block.scalar
            def _(e):
                for ci in range(16):
                    e.wait_ge(TC.h, s6_tc3 + 2 * ci + 1)
                    e.activation(out=slhsP[:, ci * 128:(ci + 1) * 128],
                                 in_=psum[4 + (2 * ci) % 4][0:K, 0:128],
                                 func=AF.Identity).then_inc(SC.h, 1)
                    SC.inc()
                    e.wait_ge(TC.h, s6_tc3 + 2 * ci + 2)
                    e.activation(out=cmrow[0:1, ci * 128:(ci + 1) * 128],
                                 in_=psum[4 + (2 * ci + 1) % 4][0:1, 0:128],
                                 func=AF.Identity).then_inc(SC.h, 1)
                    SC.inc()

            barrier(block)
            g6.close()
            g5.close()
            g_mid.close()
            tap(block, "stackrhs", oselr[:])
            tap(block, "slhs", slhsP[:])
            if stop_stage <= 6:
                return nc

            g7 = ExitStack(); root.enter_context(g7)
            sb7 = mk_sb(g7)
            xlf = sb7("xlf", [128, 4096])
            y0b = sb7("y0b", [128, 4096])
            ffb = sb7("ffb", [128, 4096])
            y1T = sb7("y1T", [128, 8 * 512], BF16)
            Hsb = sb7("Hsb", [128, NHC * 512], BF16)
            ffbb = ffb[:].bitcast(BF16)
            wst = [ffbb[:, 4096 + i * 1024:4096 + (i + 1) * 1024] for i in range(4)]
            lnm = sb7("lnm", [128, 4])
            lnv = sb7("lnv", [128, 4])
            lnr = sb7("lnr", [128, 4])
            glogT = sb7("glogT", [128, 32])
            gms = sb7("gms", [128, 4])
            grs = sb7("grs", [128, 4])
            gmx = sb7("gmx", [128, 4])
            gbm = sb7("gbm", [128, 4])
            gse = sb7("gse", [128, 4])
            gg = sb7("gg", [128, 4])
            gt2 = sb7("gt2", [128, 4])
            ng = sb7("ng", [128, 4])
            sgap = sb7("sgap", [128, 1])

            for t in range(NTILE):
                st = {"tc0": TC.n, "vc0": VC.n, "sc0": SC.n}

                @block.sync
                def _(e, t=t, st=st):
                    st["i_xl"] = MSC.issue(
                        e, xlf[:].rearrange("p (c f) -> p c f", f=1024),
                        xb[t * TT:(t + 1) * TT, :]
                        .rearrange("(c p) f -> p c f", p=128))

                @block.tensor
                def _(e, t=t, st=st):
                    for ci in range(4):
                        gci = t * 4 + ci
                        for nh in range(2):
                            pb = psum[2 * ci + nh]
                            e.matmul(pb[:], slhsP[:, gci * 128:(gci + 1) * 128],
                                     oselr[:, nh * 512:(nh + 1) * 512],
                                     start=True, stop=False)
                            e.matmul(pb[:], cmrow[0:1, gci * 128:(gci + 1) * 128],
                                     obmrow[:, nh * 512:(nh + 1) * 512],
                                     start=False, stop=False)
                            mm = e.matmul(pb[:], onesbfr[:],
                                          meanrow[:, nh * 512:(nh + 1) * 512],
                                          start=False, stop=True)
                    mm.then_inc(TC.h, 1)
                    TC.inc()
                    st["tc_sc"] = TC.n

                @block.vector
                def _(e, t=t, st=st):
                    e.wait_ge(TC.h, st["tc_sc"])
                    MSC.wait(e, st["i_xl"])
                    for ci in range(4):
                        e.tensor_tensor(y0b[:, ci * 1024:ci * 1024 + 512],
                                        psum[2 * ci][:],
                                        xlf[:, ci * 1024:ci * 1024 + 512], op=ALU.add)
                        e.tensor_tensor(y0b[:, ci * 1024 + 512:(ci + 1) * 1024],
                                        psum[2 * ci + 1][:],
                                        xlf[:, ci * 1024 + 512:(ci + 1) * 1024],
                                        op=ALU.add)
                    e.engine_nop().then_inc(VC.h, 1); VC.inc()
                    st["vc_y0"] = VC.n
                    for ci in range(4):
                        e.reduce_sum(lnm[:, ci:ci + 1],
                                     y0b[:, ci * 1024:(ci + 1) * 1024], axis=AX.X)
                    e.tensor_copy(gapA[:], gapB[:])
                    e.tensor_scalar(lnm[:], lnm[:], 1.0 / 1024, None, op0=ALU.mult)
                    e.tensor_copy(gapA[:], gapB[:])
                    for ci in range(4):
                        e.tensor_scalar(xlf[:, ci * 1024:(ci + 1) * 1024],
                                        y0b[:, ci * 1024:(ci + 1) * 1024],
                                        lnm[:, ci:ci + 1], None, op0=ALU.subtract)
                    e.engine_nop().then_inc(VC.h, 1); VC.inc()
                    st["vc_xc"] = VC.n

                @block.scalar
                def _(e, t=t, st=st):
                    e.wait_ge(VC.h, st["vc_xc"])
                    for ci in range(4):
                        e.activation(out=ffb[:, ci * 1024:(ci + 1) * 1024],
                                     in_=xlf[:, ci * 1024:(ci + 1) * 1024],
                                     func=AF.Square, accum_out=lnv[:, ci:ci + 1])
                    e.activation(out=sgap[:], in_=gapB[:], func=AF.Identity)
                    e.activation(out=lnr[:], in_=lnv[:], func=AF.Ln,
                                 scale=1.0 / 1024, bias=cst[:, 0:1])
                    e.activation(out=lnr[:], in_=lnr[:], func=AF.Exp, scale=-0.5
                                 ).then_inc(SC.h, 1)
                    SC.inc()
                    st["sc_ln1"] = SC.n

                @block.vector
                def _(e, t=t, st=st):
                    e.wait_ge(SC.h, st["sc_ln1"])
                    for ci in range(4):
                        e.tensor_scalar(y0b[:, ci * 1024:(ci + 1) * 1024],
                                        xlf[:, ci * 1024:(ci + 1) * 1024],
                                        lnr[:, ci:ci + 1], None, op0=ALU.mult)
                    e.engine_nop().then_inc(VC.h, 1); VC.inc()
                    st["vc_y1"] = VC.n

                if t == 0 and "t0y1" in dbg_out:
                    @block.sync
                    def _(e, st=st):
                        e.wait_ge(VC.h, st["vc_y1"])
                        MSC.issue(e, dbg_out["t0y1"], y0b[:])

                @block.tensor
                def _(e, t=t, st=st):
                    e.wait_ge(VC.h, st["vc_y1"])
                    for kc in range(8):
                        if kc >= 2:
                            e.wait_ge(YE.h, 8 * t + kc - 1)
                        for ci in range(4):
                            tr = e.transpose(psum[kc % 2][:, ci * 128:(ci + 1) * 128],
                                             y0b[:, ci * 1024 + kc * 128:
                                                 ci * 1024 + (kc + 1) * 128],
                                             id32[:])
                        tr.then_inc(YT.h, 1)
                        YT.inc()

                @block.vector
                def _(e, t=t, st=st):
                    for kc in range(8):
                        e.wait_ge(YT.h, 8 * t + kc + 1)
                        e.tensor_copy(y1T[:, kc * 512:(kc + 1) * 512],
                                      psum[kc % 2][:, 0:512]).then_inc(YE.h, 1)
                        YE.inc()
                    e.engine_nop().then_inc(VC.h, 1); VC.inc()
                    st["vc_y1t"] = VC.n

                @block.tensor
                def _(e, t=t, st=st):
                    e.wait_ge(VC.h, st["vc_y1t"])
                    for kc in range(8):
                        mm = e.matmul(psum[4][0:8, :], gwT[:, kc * 8:(kc + 1) * 8],
                                      y1T[:, kc * 512:(kc + 1) * 512],
                                      start=(kc == 0), stop=(kc == 7))
                    mm.then_inc(TC.h, 1)
                    TC.inc()
                    st["tc_gl"] = TC.n

                @block.vector
                def _(e, t=t, st=st):
                    e.wait_ge(TC.h, st["tc_gl"])
                    e.tensor_scalar(xlf[0:8, 2048:2560], psum[4][0:8, :], gbcol[:, 0:1], None,
                                    op0=ALU.add).then_inc(VC.h, 1)
                    VC.inc()
                    st["vc_gl"] = VC.n

                @block.tensor
                def _(e, t=t, st=st):
                    e.wait_ge(VC.h, st["vc_gl"])
                    for ci in range(4):
                        tr = e.transpose(psum[5][0:128, ci * 8:(ci + 1) * 8],
                                         xlf[0:8, 2048 + ci * 128:2048 + (ci + 1) * 128],
                                         id32[0:8, 0:8])
                    tr.then_inc(TC.h, 1)
                    TC.inc()
                    st["tc_glt"] = TC.n

                @block.vector
                def _(e, t=t, st=st):
                    e.wait_ge(TC.h, st["tc_glt"])
                    e.tensor_copy(glogT[:], psum[5][0:128, 0:32]).then_inc(VC.h, 1)
                    VC.inc()
                    for ci in range(4):
                        e.tensor_tensor(xlf[:, 2560 + ci * 8:2560 + (ci + 1) * 8],
                                        glogT[:, ci * 8:(ci + 1) * 8],
                                        glogT[:, ci * 8:(ci + 1) * 8], op=ALU.mult)
                    for ci in range(4):
                        e.reduce_sum(gms[:, ci:ci + 1], xlf[:, 2560 + ci * 8:2560 + (ci + 1) * 8],
                                     axis=AX.X)
                    e.engine_nop().then_inc(VC.h, 1); VC.inc()
                    st["vc_gms"] = VC.n

                @block.scalar
                def _(e, t=t, st=st):
                    e.wait_ge(VC.h, st["vc_gms"])
                    e.activation(out=grs[:], in_=gms[:], func=AF.Ln,
                                 scale=1.0 / 8, bias=cst[:, 1:2])
                    e.activation(out=grs[:], in_=grs[:], func=AF.Exp, scale=-0.5
                                 ).then_inc(SC.h, 1)
                    SC.inc()
                    st["sc_grs"] = SC.n

                @block.vector
                def _(e, t=t, st=st):
                    e.wait_ge(SC.h, st["sc_grs"])
                    for ci in range(4):
                        e.tensor_scalar(xlf[:, 2592 + ci * 8:2592 + (ci + 1) * 8],
                                        glogT[:, ci * 8:(ci + 1) * 8],
                                        grs[:, ci:ci + 1], None, op0=ALU.mult)
                    e.tensor_copy(gapA[:], gapB[:])
                    for ci in range(4):
                        e.tensor_tensor(xlf[:, 2592 + ci * 8:2592 + (ci + 1) * 8],
                                        xlf[:, 2592 + ci * 8:2592 + (ci + 1) * 8],
                                        rmsb[:], op=ALU.mult)
                    e.tensor_copy(gapA[:], gapB[:])
                    for ci in range(4):
                        e.reduce_max(gmx[:, ci:ci + 1],
                                     xlf[:, 2592 + ci * 8:2592 + (ci + 1) * 8],
                                     axis=AX.X)
                    e.tensor_copy(gapA[:], gapB[:])
                    e.tensor_scalar(gbm[:], gmx[:], -2.0, None, op0=ALU.mult
                                    ).then_inc(VC.h, 1)
                    VC.inc()
                    st["vc_gbm"] = VC.n

                @block.scalar
                def _(e, t=t, st=st):
                    e.wait_ge(VC.h, st["vc_gbm"])
                    for ci in range(3):
                        e.activation(out=xlf[:, 2560 + ci * 8:2560 + (ci + 1) * 8],
                                     in_=xlf[:, 2592 + ci * 8:2592 + (ci + 1) * 8], func=AF.Exp,
                                     scale=2.0, bias=gbm[:, ci:ci + 1],
                                     accum_out=gse[:, ci:ci + 1])
                    e.activation(out=xlf[:, 2584:2592],
                                 in_=xlf[:, 2616:2624], func=AF.Exp,
                                 scale=2.0, bias=gbm[:, 3:4],
                                 accum_out=gse[:, 3:4]).then_inc(SC.h, 1)
                    SC.inc()
                    st["sc_gse"] = SC.n

                @block.vector
                def _(e, t=t, st=st):
                    e.wait_ge(SC.h, st["sc_gse"])
                    e.reciprocal(gg[:], gse[:])
                    e.tensor_copy(gapA[:], gapB[:])
                    e.tensor_scalar(gt2[:], gg[:], 1e-6, None, op0=ALU.add)
                    e.tensor_copy(gapA[:], gapB[:])
                    e.reciprocal(gt2[:], gt2[:])
                    e.tensor_copy(gapA[:], gapB[:])
                    e.tensor_tensor(ng[:], gg[:], gt2[:], op=ALU.mult
                                    ).then_inc(VC.h, 1)
                    VC.inc()
                    st["vc_ng"] = VC.n

                if t == 0 and "t0ng" in dbg_out:
                    @block.sync
                    def _(e, st=st):
                        e.wait_ge(VC.h, st["vc_ng"])
                        MSC.issue(e, dbg_out["t0ng"], ng[:])

                @block.sync
                def _(e, t=t, st=st):
                    iw1 = [None] * NHC
                    iw2 = [None] * NHC
                    st["iw1"] = iw1
                    st["iw2"] = iw2
                    e.wait_ge(SC.h, st["sc_ln1"])
                    for hc in range(NHC):
                        g = t * NHC + hc
                        if g >= 4:
                            e.wait_ge(WA.h, g - 3)
                        iw1[hc] = SYN.issue(e, wst[hc % 4], w1scr[hc])
                    for hc in range(NHC):
                        if hc < 4:
                            e.wait_ge(WA.h, (t + 1) * NHC)
                        else:
                            e.wait_ge(WB.h, t * NHC + hc - 3)
                        iw2[hc] = SYN.issue(e, wst[hc % 4],
                                            w2scr[hc * 128:(hc + 1) * 128, :])

                @block.tensor
                def _(e, t=t, st=st):
                    e.wait_ge(VC.h, st["vc_y1t"])
                    for hc in range(NHC):
                        SYN.wait(e, st["iw1"][hc])
                        if hc >= 4:
                            e.wait_ge(GE.h, t * NHC + hc - 3)
                        for kc in range(8):
                            mm = e.matmul(psum[hc % 4][:],
                                          wst[hc % 4][:, kc * 128:(kc + 1) * 128],
                                          y1T[:, kc * 512:(kc + 1) * 512],
                                          start=(kc == 0), stop=(kc == 7))
                        mm.then_inc(WA.h, 1)
                        WA.inc()

                @block.scalar
                def _(e, t=t, st=st):
                    for hc in range(NHC):
                        e.wait_ge(WA.h, t * NHC + hc + 1)
                        e.activation(out=Hsb[:, hc * 512:(hc + 1) * 512],
                                     in_=psum[hc % 4][:], func=AF.Gelu,
                                     bias=b1col[:, hc:hc + 1]).then_inc(GE.h, 1)
                        GE.inc()

                @block.tensor
                def _(e, t=t, st=st):
                    e.wait_ge(GE.h, (t + 1) * NHC)
                    for m in range(4):
                        for n_ in range(2):
                            e.matmul(psum[m * 2 + n_][:], onesbfr[:],
                                     b2sumr[:, n_ * 512:(n_ + 1) * 512],
                                     start=True, stop=False)
                    for hc in range(NHC):
                        SYN.wait(e, st["iw2"][hc])
                        for m in range(4):
                            for n_ in range(2):
                                mm = e.matmul(psum[m * 2 + n_][:],
                                              Hsb[:, hc * 512 + m * 128:
                                                  hc * 512 + (m + 1) * 128],
                                              wst[hc % 4][:, n_ * 512:(n_ + 1) * 512],
                                              start=False, stop=(hc == NHC - 1))
                        mm.then_inc(WB.h, 1)
                        WB.inc()

                @block.vector
                def _(e, t=t, st=st):
                    e.wait_ge(WB.h, (t + 1) * NHC)
                    e.wait_ge(VC.h, st["vc_ng"])
                    for m in range(4):
                        for n_ in range(2):
                            e.tensor_scalar(ffb[:, m * 1024 + n_ * 512:
                                                m * 1024 + (n_ + 1) * 512],
                                            psum[m * 2 + n_][:], ng[:, m:m + 1],
                                            None, op0=ALU.mult)
                    e.engine_nop().then_inc(VC.h, 1); VC.inc()
                    for ci in range(4):
                        e.tensor_tensor(xlf[:, ci * 1024:(ci + 1) * 1024],
                                        y0b[:, ci * 1024:(ci + 1) * 1024],
                                        ffb[:, ci * 1024:(ci + 1) * 1024],
                                        op=ALU.add)
                    e.engine_nop().then_inc(VC.h, 1); VC.inc()
                    for ci in range(4):
                        e.reduce_sum(lnm[:, ci:ci + 1],
                                     xlf[:, ci * 1024:(ci + 1) * 1024], axis=AX.X)
                    e.tensor_copy(gapA[:], gapB[:])
                    e.tensor_scalar(lnm[:], lnm[:], 1.0 / 1024, None, op0=ALU.mult)
                    e.tensor_copy(gapA[:], gapB[:])
                    for ci in range(4):
                        e.tensor_scalar(y0b[:, ci * 1024:(ci + 1) * 1024],
                                        xlf[:, ci * 1024:(ci + 1) * 1024],
                                        lnm[:, ci:ci + 1], None, op0=ALU.subtract)
                    e.engine_nop().then_inc(VC.h, 1); VC.inc()
                    st["vc_xc2"] = VC.n

                if t == 0 and "t0ff" in dbg_out:
                    @block.sync
                    def _(e, st=st):
                        e.wait_ge(VC.h, st["vc_xc2"] - 2)
                        MSC.issue(e, dbg_out["t0ff"], ffb[:])

                if t == 0 and "t0H" in dbg_out:
                    @block.sync
                    def _(e, st=st):
                        e.wait_ge(GE.h, NHC)
                        MSC.issue(e, dbg_out["t0H"], Hsb[:, 0:2048])

                @block.scalar
                def _(e, t=t, st=st):
                    e.wait_ge(VC.h, st["vc_xc2"])
                    for ci in range(4):
                        e.activation(out=ffb[:, ci * 1024:(ci + 1) * 1024],
                                     in_=y0b[:, ci * 1024:(ci + 1) * 1024],
                                     func=AF.Square, accum_out=lnv[:, ci:ci + 1])
                    e.activation(out=sgap[:], in_=gapB[:], func=AF.Identity)
                    e.activation(out=lnr[:], in_=lnv[:], func=AF.Ln,
                                 scale=1.0 / 1024, bias=cst[:, 0:1])
                    e.activation(out=lnr[:], in_=lnr[:], func=AF.Exp, scale=-0.5
                                 ).then_inc(SC.h, 1)
                    SC.inc()
                    st["sc_ln2"] = SC.n

                @block.vector
                def _(e, t=t, st=st):
                    e.wait_ge(SC.h, st["sc_ln2"])
                    for ci in range(4):
                        e.tensor_scalar(ffb[:, ci * 1024:(ci + 1) * 1024],
                                        y0b[:, ci * 1024:(ci + 1) * 1024],
                                        lnr[:, ci:ci + 1], None, op0=ALU.mult)
                    e.engine_nop().then_inc(VC.h, 1); VC.inc()
                    st["vc_y2"] = VC.n

                @block.scalar
                def _(e, t=t, st=st):
                    e.wait_ge(VC.h, st["vc_y2"])
                    SCL.issue(e, yout[t * TT:(t + 1) * TT, :]
                              .rearrange("(c p) f -> p c f", p=128),
                              ffb[:].rearrange("p (c f) -> p c f", f=1024))

                barrier(block)

            g7.close()

            qf = [(s.h, s.n) for s in ALLQ]

            @block.gpsimd
            def _(e):
                for h, v in qf:
                    e.wait_ge(h, v)

    return nc


_CACHED = {}


def _prep_inputs(L0_attn_x, L0_attn_in_proj_w, L0_attn_in_proj_b, L0_attn_out_w,
                 L0_attn_out_b, L0_ffn_gate_w, L0_ffn_gate_b, L0_ffn_rms_w,
                 L0_ffn_w1, L0_ffn_b1, L0_ffn_w2, L0_ffn_b2):
    c = lambda a: np.ascontiguousarray(np.asarray(a, dtype=np.float32))
    x = c(L0_attn_x)
    w1r = c(np.transpose(np.asarray(L0_ffn_w1, np.float32)
                         .reshape(8, 8, 128, 16, 128), (0, 3, 2, 1, 4))
            .reshape(NHC, 128, 1024))
    w2r = c(np.asarray(L0_ffn_w2, np.float32).reshape(NHID, E))
    b1c = c(np.asarray(L0_ffn_b1, np.float32).reshape(128, 128).T)
    common = {
        "ipw": c(L0_attn_in_proj_w),
        "ipb": c(np.asarray(L0_attn_in_proj_b, np.float32).reshape(24, 128)),
        "outw": c(L0_attn_out_w),
        "outb": c(np.asarray(L0_attn_out_b, np.float32).reshape(1, E)),
        "gatew": c(L0_ffn_gate_w),
        "gateb": c(np.asarray(L0_ffn_gate_b, np.float32).reshape(1, NEXP)),
        "rmsw": c(np.asarray(L0_ffn_rms_w, np.float32).reshape(1, NEXP)),
        "w1r": w1r, "w2r": w2r, "b1c": b1c,
        "b2": c(L0_ffn_b2),
        "ident": np.eye(128, dtype=np.float32),
        "onesc": np.ones((128, 1), np.float32),
        "onesr": np.ones((1, 128), np.float32),
    }
    in_maps = []
    for core in range(8):
        b, part = core // 4, core % 4
        m = dict(common)
        m["xb"] = c(np.roll(x[b], -part * NTOK, axis=0))
        in_maps.append(m)
    return in_maps


def kernel(**inputs):
    in_maps = _prep_inputs(**inputs)
    if "nc" not in _CACHED:
        _CACHED["nc"] = build_program()
    nc = _CACHED["nc"]
    res = run_bass_kernel_spmd(nc, in_maps, core_ids=list(range(8)))
    out = np.empty((B, S, E), np.float32)
    for core in range(8):
        b, part = core // 4, core % 4
        out[b, part * NTOK:(part + 1) * NTOK] = res.results[core]["y"]
    return out


# version 11
# speedup vs baseline: 1.1557x; 1.0816x over previous
import numpy as np
from contextlib import ExitStack

import concourse.bass as bass
import concourse.mybir as mybir
from concourse.bass_utils import run_bass_kernel_spmd

F32 = mybir.dt.float32
BF16 = mybir.dt.bfloat16
AF = mybir.ActivationFunctionType
ALU = mybir.AluOpType
AX = mybir.AxisListType

B, S, E = 2, 8192, 1024
H, HD = 16, 64
K = 45
NEXP, HID = 8, 2048
NHID = NEXP * HID
NTOK = S // 4
ROWS = H * K
RPAD = 768
NRT = 6
TT = 512
NTILE = NTOK // TT
NHC = NHID // 128
EPS32 = float(np.finfo(np.float32).eps)
QB_N = H


class Ctr:
    def __init__(self, h):
        self.h = h
        self.n = 0

    def inc(self, k=1):
        self.n += k
        return self.n


class DmaStream:

    def __init__(self, sems):
        self.sems = sems
        self.marks = []

    def issue(self, e, out, in_, **kw):
        s = self.sems[len(self.marks) % len(self.sems)]
        e.dma_start(out=out, in_=in_, **kw).then_inc(s.h, 16)
        s.inc(16)
        self.marks.append((s.h, s.n))
        return len(self.marks) - 1

    def wait(self, e, i):
        h, v = self.marks[i]
        e.wait_ge(h, v)

    def wait_upto(self, e, i):
        seen = {}
        for j in range(i + 1):
            h, v = self.marks[j]
            seen[id(h)] = (h, v)
        for h, v in seen.values():
            e.wait_ge(h, v)


def build_program(dbg=None, stop_stage=99):
    dbg = dbg or {}
    nc = bass.Bass("TRN2", target_bir_lowering=False,
                   detect_race_conditions=False)

    xb = nc.dram_tensor("xb", [S, E], F32, kind="ExternalInput").ap()
    ipw = nc.dram_tensor("ipw", [3 * E, E], F32, kind="ExternalInput").ap()
    ipb = nc.dram_tensor("ipb", [24, 128], F32, kind="ExternalInput").ap()
    outw = nc.dram_tensor("outw", [E, E], F32, kind="ExternalInput").ap()
    outb = nc.dram_tensor("outb", [1, E], F32, kind="ExternalInput").ap()
    gatew = nc.dram_tensor("gatew", [NEXP, E], F32, kind="ExternalInput").ap()
    gateb = nc.dram_tensor("gateb", [1, NEXP], F32, kind="ExternalInput").ap()
    rmsw = nc.dram_tensor("rmsw", [1, NEXP], F32, kind="ExternalInput").ap()
    w1r = nc.dram_tensor("w1r", [NHC, 128, 1024], F32, kind="ExternalInput").ap()
    w2r = nc.dram_tensor("w2r", [NHID, E], F32, kind="ExternalInput").ap()
    b1c = nc.dram_tensor("b1c", [128, 128], F32, kind="ExternalInput").ap()
    b2 = nc.dram_tensor("b2", [NEXP, E], F32, kind="ExternalInput").ap()
    ident = nc.dram_tensor("ident", [128, 128], F32, kind="ExternalInput").ap()
    onesc = nc.dram_tensor("onesc", [128, 1], F32, kind="ExternalInput").ap()
    onesr = nc.dram_tensor("onesr", [1, 128], F32, kind="ExternalInput").ap()
    yout = nc.dram_tensor("y", [NTOK, E], F32, kind="ExternalOutput").ap()
    dbg_out = {k: nc.dram_tensor("dbg_" + k, list(v[0]), v[1], kind="ExternalOutput").ap()
               for k, v in dbg.items()}

    xscr = nc.dram_tensor("xscr", [S, E], BF16, kind="Internal").ap()
    w1scr = nc.dram_tensor("w1scr", [NHC, 128, 1024], BF16, kind="Internal").ap()
    w2scr = nc.dram_tensor("w2scr", [NHID, E], BF16, kind="Internal").ap()
    attnscr = nc.dram_tensor("attnscr", [RPAD, S], BF16, kind="Internal").ap()

    root = ExitStack()
    sem = lambda name: Ctr(root.enter_context(nc.semaphore(name)))

    with root:
        def mk_sb(stack):
            return lambda name, shape, dt=F32: stack.enter_context(
                nc.sbuf_tensor(name, shape, dt))

        sb = mk_sb(root)

        id32 = sb("id32", [128, 128])
        ones32c = sb("ones32c", [128, 1])
        ones32r = sb("ones32r", [1, 128])
        onesbfr = sb("onesbfr", [1, 128], BF16)
        onesbfc = sb("onesbfc", [128, 1], BF16)
        bcols = sb("bcols", [128, 24])
        b1col = sb("b1col", [128, 128])
        b2sumr = sb("b2sumr", [1, 1024], BF16)
        rmsb = sb("rmsb", [128, NEXP])
        gwT = sb("gwT", [128, 64], BF16)
        gbcol = sb("gbcol", [8, 1])
        cmax = sb("cmax", [128, 1])
        gmax1 = sb("gmax1", [1, 1])
        smax = sb("smax", [1, 1])
        gapA = sb("gapA", [128, 1])
        gapB = sb("gapB", [128, 1])
        bsc = sb("bsc", [1, 1])
        cst = sb("cst", [128, 2])
        rsum = sb("rsum", [128, NRT])
        sumex = sb("sumex", [128, NRT])
        oselr = sb("oselr", [K, 1024], BF16)
        obmrow = sb("obmrow", [1, 1024], BF16)
        meanrow = sb("meanrow", [1, 1024], BF16)
        slhsP = sb("slhsP", [K, 16 * 128], BF16)
        cmrow = sb("cmrow", [1, 16 * 128], BF16)

        LS0 = sem("LS0"); LS1 = sem("LS1"); LS2 = sem("LS2"); LS3 = sem("LS3")
        LC0 = sem("LC0"); LC1 = sem("LC1"); LC2 = sem("LC2"); LC3 = sem("LC3")
        LG0 = sem("LG0"); LG1 = sem("LG1")
        LM = sem("LM")
        LN_ = sem("LN_")
        TC = sem("TC"); VC = sem("VC"); SC = sem("SC")
        BAR = sem("BAR")
        WA = sem("WA"); GE = sem("GE"); WB = sem("WB")
        WCA = sem("WCA"); WCW = sem("WCW")
        YT = sem("YT"); YE = sem("YE"); RN = sem("RN")
        SYN = DmaStream([LS0, LS1, LS2, LS3])
        SCL = DmaStream([LC0, LC1, LC2, LC3])
        GPS = DmaStream([LG0, LG1])
        MSC = DmaStream([LM])
        MGP = DmaStream([LN_])
        ALLQ = [LS0, LS1, LS2, LS3, LC0, LC1, LC2, LC3, LG0, LG1, LM, LN_]

        psum = [root.enter_context(nc.psum_tensor(f"ps{i}", [128, 512], F32))
                for i in range(8)]

        g_mid = ExitStack(); root.enter_context(g_mid)
        sbm = mk_sb(g_mid)
        ss = sbm("ss", [128, 64])
        ssoh = sbm("ssoh", [128, 64])
        ptbf = sbm("ptbf", [128, 64 * K], BF16)
        ebias = sbm("ebias", [128, NRT])
        sump = sbm("sump", [128, 96])
        qtokb = sbm("qtokb", [K, 1024])
        qWf = sbm("qWf", [128, 8 * RPAD], BF16)
        wcb = [sbm(f"wcb{i}", [128, 4096], BF16) for i in range(2)]

        taps = {}

        def barrier(block):
            BAR.inc(5)
            tgt = BAR.n
            qvals = [(s.h, s.n) for s in ALLQ]

            @block.sync
            def _(e):
                for h, v in qvals:
                    e.wait_ge(h, v)
                e.sem_inc(BAR.h, 1)
                e.wait_ge(BAR.h, tgt)

            @block.scalar
            def _(e):
                e.activation(out=bsc[:], in_=ones32c[0:1, 0:1],
                             func=AF.Identity).then_inc(BAR.h, 1)
                e.wait_ge(BAR.h, tgt)

            @block.gpsimd
            def _(e):
                e.engine_nop().then_inc(BAR.h, 1)
                e.wait_ge(BAR.h, tgt)

            @block.tensor
            def _(e):
                e.nop().then_inc(BAR.h, 1)
                e.wait_ge(BAR.h, tgt)

            @block.vector
            def _(e):
                e.engine_nop().then_inc(BAR.h, 1)
                e.wait_ge(BAR.h, tgt)


        def emit_cast_slice(block, lo, hi):
            @block.gpsimd
            def _(e):
                for i in range(lo, hi):
                    if i >= 2:
                        e.wait_ge(WCW.h, 16 * (i - 1))
                    if i < 32:
                        e.dma_start(out=wcb[i % 2][:].rearrange("p (b f) -> p b f", f=1024),
                                    in_=w1r[i * 4:(i + 1) * 4].rearrange("b p f -> p b f")
                                    ).then_inc(WCA.h, 16)
                        WCA.inc(16)
                        e.wait_ge(WCA.h, WCA.n)
                        e.dma_start(out=w1scr[i * 4:(i + 1) * 4].rearrange("b p f -> p b f"),
                                    in_=wcb[i % 2][:].rearrange("p (b f) -> p b f", f=1024)
                                    ).then_inc(WCW.h, 16)
                        WCW.inc(16)
                    else:
                        j = i - 32
                        e.dma_start(out=wcb[i % 2][:].rearrange("p (b f) -> p b f", f=1024),
                                    in_=w2r[j * 512:(j + 1) * 512, :]
                                    .rearrange("(b p) f -> p b f", p=128)
                                    ).then_inc(WCA.h, 16)
                        WCA.inc(16)
                        e.wait_ge(WCA.h, WCA.n)
                        e.dma_start(out=w2scr[j * 512:(j + 1) * 512, :]
                                    .rearrange("(b p) f -> p b f", p=128),
                                    in_=wcb[i % 2][:].rearrange("p (b f) -> p b f", f=1024)
                                    ).then_inc(WCW.h, 16)
                        WCW.inc(16)

        def tap(block, name, ap):
            if name not in dbg_out:
                return
            @block.sync
            def _(e):
                MSC.issue(e, dbg_out[name], ap)
            barrier(block)

        with nc.Block() as block:
            g1 = ExitStack(); root.enter_context(g1)
            sb1 = mk_sb(g1)
            ipb_sb = sb1("ipb_sb", [24, 128])
            b2_sb = sb1("b2_sb", [NEXP, 1024])
            rms_sb = sb1("rms_sb", [1, NEXP])
            gb_sb = sb1("gb_sb", [1, NEXP])
            gw_sb = sb1("gw_sb", [NEXP, 1024])
            outb_sb = sb1("outb_sb", [1, 1024], BF16)
            xf = [sb1(f"xf{i}", [128, 1024]) for i in range(4)]
            xbf = [sb1(f"xbf{i}", [128, 1024], BF16) for i in range(4)]
            junk1 = sb1("junk1", [128, 1024])

            @block.sync
            def _(e):
                for t, src in [(id32, ident), (ones32c, onesc), (ones32r, onesr),
                               (b1col, b1c), (ipb_sb, ipb), (b2_sb, b2),
                               (rms_sb, rmsw), (gb_sb, gateb), (gw_sb, gatew)]:
                    MSC.issue(e, t[:], src)

            @block.gpsimd
            def _(e):
                for t, src in [(onesbfr, onesr), (onesbfc, onesc),
                               (outb_sb, outb)]:
                    MGP.issue(e, t[:], src)
            i_const = len(MSC.marks) - 1
            ig_const = len(MGP.marks) - 1

            tc0 = TC.n
            vc0 = VC.n
            @block.tensor
            def _(e):
                MSC.wait_upto(e, i_const)
                MGP.wait_upto(e, ig_const)
                e.transpose(psum[0][:, 0:24], ipb_sb[:], id32[0:24, 0:24]
                            ).then_inc(TC.h, 1); TC.inc()
                e.matmul(psum[4][0:1, :], ones32c[0:8, 0:1], b2_sb[:, 0:512],
                         start=True, stop=True)
                e.matmul(psum[5][0:1, :], ones32c[0:8, 0:1], b2_sb[:, 512:1024],
                         start=True, stop=True).then_inc(TC.h, 1); TC.inc()
                e.matmul(psum[6][:, 0:8], ones32r[:], rms_sb[:],
                         start=True, stop=True).then_inc(TC.h, 1); TC.inc()
                e.transpose(psum[7][0:8, 0:1], gb_sb[:], id32[0:1, 0:1]
                            ).then_inc(TC.h, 1); TC.inc()
                for kc in range(8):
                    if kc >= 2:
                        e.wait_ge(VC.h, vc0 + 6 + (kc - 2) + 1)
                    e.transpose(psum[1 + kc % 2][:, 0:8],
                                gw_sb[:, kc * 128:(kc + 1) * 128],
                                id32[0:8, 0:8]).then_inc(TC.h, 1); TC.inc()
            tc_tab = TC.n

            @block.vector
            def _(e):
                e.wait_ge(TC.h, tc0 + 4)
                e.tensor_copy(bcols[:], psum[0][:, 0:24])
                e.tensor_copy(b2sumr[:, 0:512], psum[4][0:1, :])
                e.tensor_copy(b2sumr[:, 512:1024], psum[5][0:1, :])
                e.tensor_copy(rmsb[:], psum[6][:, 0:8])
                e.tensor_copy(gbcol[:], psum[7][0:8, 0:1])
                MGP.wait_upto(e, ig_const)
                e.memset(slhsP[:], 0.0)
                e.memset(qWf[:], 0.0)
                e.memset(gapB[:], 0.0)
                e.memset(cst[:, 0:1], 1e-5)
                e.memset(cst[:, 1:2], EPS32)
                e.engine_nop().then_inc(VC.h, 6); VC.inc(6)
                for kc in range(8):
                    e.wait_ge(TC.h, tc0 + 4 + kc + 1)
                    e.tensor_copy(gwT[:, kc * 8:(kc + 1) * 8],
                                  psum[1 + kc % 2][:, 0:8]).then_inc(VC.h, 1)
                    VC.inc()
            vc_tab = VC.n

            NXT = S // 128
            s1_sc0, s1_vc0, s1_tc0 = SC.n, VC.n, TC.n
            ixf = [None] * NXT
            iws = [None] * NXT

            @block.sync
            def _(e):
                for i in range(NXT):
                    if i >= 4:
                        e.wait_ge(SC.h, s1_sc0 + (i - 4) + 1)
                        e.wait_ge(VC.h, s1_vc0 + (i - 4) + 1)
                    ixf[i] = SYN.issue(e, xf[i % 4][:],
                                       xb[i * 128:(i + 1) * 128, :])

            @block.scalar
            def _(e):
                for i in range(NXT):
                    SYN.wait(e, ixf[i])
                    e.activation(out=junk1[:], in_=xf[i % 4][:], func=AF.Square,
                                 accum_out=ss[:, i:i + 1]).then_inc(SC.h, 1)
                    SC.inc()
                    e.wait_ge(VC.h, s1_vc0 + i + 1)
                    iws[i] = SCL.issue(e, xscr[i * 128:(i + 1) * 128, :],
                                       xbf[i % 4][:])

            @block.vector
            def _(e):
                for i in range(NXT):
                    SYN.wait(e, ixf[i])
                    if i >= 4:
                        SCL.wait(e, iws[i - 4])
                        e.wait_ge(TC.h, s1_tc0 + (i - 3))
                    e.tensor_copy(xbf[i % 4][:], xf[i % 4][:]).then_inc(VC.h, 1)
                    VC.inc()
            s1_vc_end = VC.n

            @block.tensor
            def _(e):
                for i in range(NXT):
                    e.wait_ge(VC.h, s1_vc0 + i + 1)
                    e.matmul(psum[6][0:1, :], onesbfc[:], xbf[i % 4][:, 0:512],
                             start=(i == 0), stop=(i == NXT - 1))
                    e.matmul(psum[7][0:1, :], onesbfc[:], xbf[i % 4][:, 512:1024],
                             start=(i == 0), stop=(i == NXT - 1)).then_inc(TC.h, 1)
                    TC.inc()
            s1_tc_end = TC.n

            @block.vector
            def _(e):
                e.wait_ge(TC.h, s1_tc_end)
                e.tensor_scalar(meanrow[0:1, 0:512], psum[6][0:1, :], 1.0 / S,
                                None, op0=ALU.mult)
                e.tensor_scalar(meanrow[0:1, 512:1024], psum[7][0:1, :], 1.0 / S,
                                None, op0=ALU.mult).then_inc(VC.h, 1)
                VC.inc()
                e.tensor_copy(gapA[:], gapB[:])
                e.tensor_tensor(obmrow[:], outb_sb[:], meanrow[:],
                                op=ALU.subtract).then_inc(VC.h, 1)
                VC.inc()

            emit_cast_slice(block, 0, 7)

            barrier(block)
            g1.close()
            tap(block, "ss", ss[:])
            tap(block, "mean", meanrow[:])
            if stop_stage <= 1:
                return nc

            s2_tc0, s2_vc0 = TC.n, VC.n

            @block.vector
            def _(e):
                for it in range(K):
                    e.wait_ge(TC.h, s2_tc0 + 2 * it)
                    e.reduce_max(cmax[:], ss[:], axis=AX.X).then_inc(VC.h, 1)
                    VC.inc()
                    e.wait_ge(TC.h, s2_tc0 + 2 * it + 1)
                    e.reduce_max(gmax1[:], psum[2][0:1, 0:128], axis=AX.X
                                 ).then_inc(VC.h, 1)
                    VC.inc()
                    e.wait_ge(TC.h, s2_tc0 + 2 * it + 2)
                    e.tensor_scalar(ssoh[:], ss[:], psum[3][:, 0:1], None,
                                    op0=ALU.is_ge)
                    if it == 0:
                        e.tensor_copy(smax[:], gmax1[:])
                    else:
                        e.tensor_copy(gapA[:], gapB[:])
                    e.scalar_tensor_tensor(ss[:], ssoh[:], -1e30, ss[:],
                                           op0=ALU.mult, op1=ALU.add)
                    e.tensor_copy(ptbf[:, it::K], ssoh[:]).then_inc(VC.h, 1)
                    VC.inc()
            s2_vc_end = VC.n

            @block.tensor
            def _(e):
                for it in range(K):
                    e.wait_ge(VC.h, s2_vc0 + 3 * it + 1)
                    e.transpose(psum[2][0:1, 0:128], cmax[:], id32[:]
                                ).then_inc(TC.h, 1)
                    TC.inc()
                    e.wait_ge(VC.h, s2_vc0 + 3 * it + 2)
                    e.matmul(psum[3][:, 0:1], ones32r[:], gmax1[:],
                             start=True, stop=True).then_inc(TC.h, 1)
                    TC.inc()

            barrier(block)
            tap(block, "ptbf", ptbf[:])
            if stop_stage <= 2:
                return nc

            emit_cast_slice(block, 7, 18)

            g2 = ExitStack(); root.enter_context(g2)
            sb2 = mk_sb(g2)
            xq = [sb2(f"xq{i}", [128, 1024], BF16) for i in range(2)]
            s2b_tc0 = TC.n
            ixq = [None] * 64

            @block.sync
            def _(e):
                for c in range(64):
                    if c >= 2:
                        e.wait_ge(TC.h, s2b_tc0 + (c - 2) + 1)
                    ixq[c] = SYN.issue(e, xq[c % 2][:],
                                       xscr[c * 128:(c + 1) * 128, :])

            @block.tensor
            def _(e):
                for c in range(64):
                    SYN.wait(e, ixq[c])
                    e.matmul(psum[0][0:K, :], ptbf[:, c * K:(c + 1) * K],
                             xq[c % 2][:, 0:512], start=(c == 0), stop=(c == 63))
                    e.matmul(psum[1][0:K, :], ptbf[:, c * K:(c + 1) * K],
                             xq[c % 2][:, 512:1024], start=(c == 0), stop=(c == 63)
                             ).then_inc(TC.h, 1)
                    TC.inc()
            s2b_tc_end = TC.n

            @block.vector
            def _(e):
                e.wait_ge(TC.h, s2b_tc_end)
                e.tensor_copy(qtokb[:, 0:512], psum[0][0:K, :])
                e.tensor_copy(qtokb[:, 512:1024], psum[1][0:K, :]).then_inc(VC.h, 1)
                VC.inc()

            barrier(block)
            g2.close()
            tap(block, "qtok", qtokb[:])
            if stop_stage <= 2.5:
                return nc

            g3 = ExitStack(); root.enter_context(g3)
            sb3 = mk_sb(g3)
            wq32 = sb3("wq32", [128, 8192])
            wqT = sb3("wqT", [128, 8192], BF16)
            wkbf = sb3("wkbf", [128, 8192], BF16)
            qtokT = sb3("qtokT", [128, 8 * K], BF16)
            qTsb = sb3("qTsb", [128, 8 * K], BF16)
            tq32 = sb3("tq32", [128, 8 * K])
            sq720 = [sb3(f"sq720_{i}", [128, RPAD]) for i in range(2)]
            qb_row = sb3("qb_row", [1, RPAD])
            rn_row = sb3("rn_row", [1, RPAD])
            eb_row = sb3("eb_row", [1, RPAD])
            tmp_row = sb3("tmp_row", [1, RPAD])

            @block.gpsimd
            def _(e):
                MGP.issue(e, wkbf[:].rearrange("p (r f) -> p r f", f=1024),
                          ipw[1024:2048, :].rearrange("(r p) f -> p r f", p=128))
            ig_s3w = len(MGP.marks) - 1
            emit_cast_slice(block, 18, 26)

            @block.sync
            def _(e):
                MSC.issue(e, wq32[:].rearrange("p (r f) -> p r f", f=1024),
                          ipw[0:1024, :].rearrange("(r p) f -> p r f", p=128))
            i_s3w = len(MSC.marks) - 1

            s3_tc0, s3_vc0 = TC.n, VC.n

            @block.tensor
            def _(e):
                MSC.wait_upto(e, i_s3w)
                MGP.wait_upto(e, ig_s3w)
                n = 0
                for kc in range(8):
                    for oc in range(8):
                        if n >= 2:
                            e.wait_ge(VC.h, s3_vc0 + n - 1)
                        e.transpose(psum[n % 2][:, 0:128],
                                    wq32[:, oc * 1024 + kc * 128:oc * 1024 + (kc + 1) * 128],
                                    id32[:]).then_inc(TC.h, 1)
                        TC.inc(); n += 1
                for kc in range(8):
                    e.wait_ge(VC.h, s3_vc0 + n - 1)
                    e.transpose(psum[n % 2][0:128, 0:K],
                                qtokb[0:K, kc * 128:(kc + 1) * 128],
                                id32[0:K, 0:K]).then_inc(TC.h, 1)
                    TC.inc(); n += 1
            s3_tc_tr = TC.n

            @block.vector
            def _(e):
                n = 0
                for kc in range(8):
                    for oc in range(8):
                        e.wait_ge(TC.h, s3_tc0 + n + 1)
                        e.tensor_copy(wqT[:, kc * 1024 + oc * 128:kc * 1024 + (oc + 1) * 128],
                                      psum[n % 2][:, 0:128]).then_inc(VC.h, 1)
                        VC.inc(); n += 1
                for kc in range(8):
                    e.wait_ge(TC.h, s3_tc0 + n + 1)
                    e.tensor_copy(qtokT[:, kc * K:(kc + 1) * K],
                                  psum[n % 2][0:128, 0:K]).then_inc(VC.h, 1)
                    VC.inc(); n += 1
            s3_vc_tr = VC.n

            if stop_stage <= 2.6:
                barrier(block)
                return nc

            s3_tc1, s3_vc1 = TC.n, VC.n

            @block.tensor
            def _(e):
                e.wait_ge(VC.h, s3_vc_tr)
                for oc in range(8):
                    if oc >= 2:
                        e.wait_ge(VC.h, s3_vc1 + (oc - 2) + 1)
                    for kc in range(8):
                        mm = e.matmul(psum[oc % 2][:, 0:K],
                                      wqT[:, kc * 1024 + oc * 128:kc * 1024 + (oc + 1) * 128],
                                      qtokT[:, kc * K:(kc + 1) * K],
                                      start=(kc == 0), stop=(kc == 7))
                    mm.then_inc(TC.h, 1)
                    TC.inc()
            s3_tc_q = TC.n

            @block.vector
            def _(e):
                for oc in range(8):
                    e.wait_ge(TC.h, s3_tc1 + oc + 1)
                    e.tensor_scalar(qTsb[:, oc * K:(oc + 1) * K], psum[oc % 2][:, 0:K],
                                    bcols[:, oc:oc + 1], None, op0=ALU.add
                                    ).then_inc(VC.h, 1)
                    VC.inc()
                e.tensor_copy(gapA[:], gapB[:])
                for oc in range(8):
                    e.tensor_scalar(tq32[:, oc * K:(oc + 1) * K],
                                    qTsb[:, oc * K:(oc + 1) * K],
                                    bcols[:, 8 + oc:9 + oc], None, op0=ALU.mult)
                e.engine_nop().then_inc(VC.h, 1)
                VC.inc()
            s3_vc_q = VC.n

            if stop_stage <= 2.7:
                barrier(block)
                return nc

            s3_tc2, s3_vc2 = TC.n, VC.n

            @block.tensor
            def _(e):
                e.wait_ge(VC.h, s3_vc_q)
                n = 0
                for h in range(H):
                    r, po = h // 2, (h % 2) * 64
                    for fc in range(8):
                        if n >= 4:
                            e.wait_ge(VC.h, s3_vc2 + n - 3)
                        e.matmul(psum[4 + n % 4][0:128, 0:K],
                                 wkbf[po:po + 64, r * 1024 + fc * 128:r * 1024 + (fc + 1) * 128],
                                 qTsb[po:po + 64, r * K:(r + 1) * K],
                                 start=True, stop=True,
                                 tile_position=(po, 0)).then_inc(TC.h, 1)
                        TC.inc(); n += 1
                if stop_stage > 2.75:
                    for h in range(QB_N):
                        po = (h % 2) * 64
                        dst = (psum[2] if h % 2 == 0 else psum[3])[
                            0:1, (h // 2) * 64:(h // 2) * 64 + K]
                        e.matmul(dst, ones32c[po:po + 64, 0:1],
                                 tq32[po:po + 64, (h // 2) * K:(h // 2 + 1) * K],
                                 start=True, stop=True, tile_position=(po, 0))
                e.nop().then_inc(TC.h, 1)
                TC.inc()
            s3_tc_qb = TC.n

            @block.vector
            def _(e):
                n = 0
                for h in range(H):
                    for fc in range(8):
                        e.wait_ge(TC.h, s3_tc2 + n + 1)
                        e.tensor_copy(qWf[:, fc * RPAD + h * K:fc * RPAD + (h + 1) * K],
                                      psum[4 + n % 4][0:128, 0:K]).then_inc(VC.h, 1)
                        VC.inc(); n += 1
            s3_vc_qwf = VC.n

            if stop_stage <= 2.8:
                barrier(block)
                return nc

            s3_vc3 = VC.n

            @block.vector
            def _(e):
                e.wait_ge(VC.h, s3_vc_qwf)
                for fc in range(8):
                    if fc >= 2:
                        e.wait_ge(RN.h, fc - 1)
                    e.tensor_tensor(sq720[fc % 2][:],
                                    qWf[:, fc * RPAD:(fc + 1) * RPAD],
                                    qWf[:, fc * RPAD:(fc + 1) * RPAD],
                                    op=ALU.mult).then_inc(VC.h, 1)
                    VC.inc()

            @block.tensor
            def _(e):
                for fc in range(8):
                    e.wait_ge(VC.h, s3_vc3 + fc + 1)
                    e.matmul(psum[6][0:1, :], ones32c[:], sq720[fc % 2][:, 0:512],
                             start=(fc == 0), stop=(fc == 7))
                    e.matmul(psum[7][0:1, 0:256], ones32c[:], sq720[fc % 2][:, 512:768],
                             start=(fc == 0), stop=(fc == 7)).then_inc(RN.h, 1)
                    RN.inc()
                e.nop().then_inc(TC.h, 1)
                TC.inc()
            s3_tc_rn = TC.n

            if stop_stage <= 2.9:
                barrier(block)
                return nc

            s3_sc0 = SC.n
            @block.scalar
            def _(e):
                e.wait_ge(TC.h, s3_tc_rn)
                e.activation(out=rn_row[0:1, 0:512], in_=psum[6][0:1, :], func=AF.Sqrt)
                e.activation(out=rn_row[0:1, 512:768], in_=psum[7][0:1, 0:256],
                             func=AF.Sqrt)
                for h in range(H):
                    srcp = (psum[2] if h % 2 == 0 else psum[3])[
                        0:1, (h // 2) * 64:(h // 2) * 64 + K]
                    e.activation(out=qb_row[0:1, h * K:(h + 1) * K], in_=srcp,
                                 func=AF.Identity)
                e.activation(out=qb_row[0:1, 720:768], in_=psum[3][0:1, 464:512],
                             func=AF.Identity, scale=0.0)
                e.activation(out=tmp_row[:], in_=qb_row[:], func=AF.Abs)
                e.activation(out=smax[:], in_=smax[:], func=AF.Sqrt
                             ).then_inc(SC.h, 1)
                SC.inc()
            s3_sc_rows = SC.n

            s3_vc4 = VC.n
            @block.vector
            def _(e):
                e.wait_ge(SC.h, s3_sc_rows)
                e.tensor_scalar(rn_row[:], rn_row[:], smax[0:1, 0:1], None,
                                op0=ALU.mult)
                e.tensor_tensor(eb_row[:], qb_row[:], tmp_row[:], op=ALU.subtract)
                e.tensor_copy(gapA[:], gapB[:])
                e.tensor_tensor(eb_row[:], eb_row[:], rn_row[:], op=ALU.subtract)
                e.tensor_copy(gapA[:], gapB[:])
                e.tensor_scalar(eb_row[:], eb_row[:], 0.125, None, op0=ALU.mult)
                e.memset(eb_row[0:1, ROWS:RPAD], 0.0)
                e.engine_nop().then_inc(VC.h, 1)
                VC.inc()
            s3_vc_eb = VC.n

            s3_tc5, s3_vc5 = TC.n, VC.n
            @block.tensor
            def _(e):
                e.wait_ge(VC.h, s3_vc_eb)
                for rt in range(NRT):
                    if rt >= 2:
                        e.wait_ge(VC.h, s3_vc5 + (rt - 2) + 1)
                    e.transpose(psum[rt % 2][0:128, 0:1],
                                eb_row[0:1, rt * 128:(rt + 1) * 128],
                                id32[0:1, 0:1]).then_inc(TC.h, 1)
                    TC.inc()
            s3_tc_ebt = TC.n

            @block.vector
            def _(e):
                for rt in range(NRT):
                    e.wait_ge(TC.h, s3_tc5 + rt + 1)
                    e.tensor_copy(ebias[:, rt:rt + 1], psum[rt % 2][0:128, 0:1]
                                  ).then_inc(VC.h, 1)
                    VC.inc()

            barrier(block)
            tap(block, "wq32", wq32[:])
            tap(block, "wkbf", wkbf[:])
            tap(block, "wqT", wqT[:])
            tap(block, "qTsb", qTsb[:])
            tap(block, "tq32", tq32[:])
            tap(block, "qbrow", qb_row[:])
            g3.close()
            tap(block, "qWf", qWf[:])
            tap(block, "ebias", ebias[:])
            if stop_stage <= 3:
                return nc

            emit_cast_slice(block, 26, 48)

            g4 = ExitStack(); root.enter_context(g4)
            sb4 = mk_sb(g4)
            SLICE = 2048
            NSL = S // SLICE
            xT = sb4("xT", [128, 8 * SLICE], BF16)
            attst = [sb4(f"attst{i}", [128, 512], BF16) for i in range(4)]
            s4_tc0, s4_sc0 = TC.n, SC.n
            ixT = [[None] * 8 for _ in range(NSL)]
            iaw = [None] * (NSL * NRT * 4)

            @block.sync
            def _(e):
                for sl in range(NSL):
                    if sl >= 1:
                        e.wait_ge(TC.h, s4_tc0 + sl * 24)
                    for fc in range(8):
                        ixT[sl][fc] = SYN.issue(
                            e, xT[:, fc * SLICE:(fc + 1) * SLICE],
                            xscr[sl * SLICE:(sl + 1) * SLICE,
                                 fc * 128:(fc + 1) * 128],
                            transpose=True)

            @block.tensor
            def _(e):
                n = 0
                for sl in range(NSL):
                    SYN.wait_upto(e, ixT[sl][7])
                    for rt in range(NRT):
                        for sub in range(4):
                            if n >= 4:
                                e.wait_ge(SC.h, s4_sc0 + (n - 4) + 1)
                            for kc in range(8):
                                mm = e.matmul(psum[n % 4][:],
                                              qWf[:, kc * RPAD + rt * 128:
                                                  kc * RPAD + (rt + 1) * 128],
                                              xT[:, kc * SLICE + sub * 512:
                                                 kc * SLICE + (sub + 1) * 512],
                                              start=(kc == 0), stop=(kc == 7))
                            mm.then_inc(TC.h, 1)
                            TC.inc(); n += 1

            @block.scalar
            def _(e):
                n = 0
                for sl in range(NSL):
                    for rt in range(NRT):
                        for sub in range(4):
                            e.wait_ge(TC.h, s4_tc0 + n + 1)
                            if n >= 4:
                                SCL.wait(e, iaw[n - 4])
                            e.activation(out=attst[n % 4][:], in_=psum[n % 4][:],
                                         func=AF.Exp, bias=ebias[:, rt:rt + 1],
                                         scale=0.125,
                                         accum_out=sump[:, rt * 16 + sl * 4 + sub:
                                                        rt * 16 + sl * 4 + sub + 1]
                                         ).then_inc(SC.h, 1)
                            SC.inc()
                            e.wait_ge(SC.h, SC.n)
                            iaw[n] = SCL.issue(
                                e, attnscr[rt * 128:(rt + 1) * 128,
                                           sl * SLICE + sub * 512:
                                           sl * SLICE + (sub + 1) * 512],
                                attst[n % 4][:])
                            n += 1
            s4_sc_end = SC.n

            @block.vector
            def _(e):
                e.wait_ge(SC.h, s4_sc_end)
                for rt in range(NRT):
                    e.reduce_sum(sumex[:, rt:rt + 1], sump[:, rt * 16:(rt + 1) * 16],
                                 axis=AX.X)
                e.tensor_copy(gapA[:], gapB[:])
                e.reciprocal(rsum[:], sumex[:])
                e.engine_nop().then_inc(VC.h, 1)
                VC.inc()

            barrier(block)
            g4.close()
            tap(block, "sumex", sumex[:])
            if stop_stage <= 4:
                return nc

            emit_cast_slice(block, 48, 64)

            g5 = ExitStack(); root.enter_context(g5)
            sb5 = mk_sb(g5)
            atT = [sb5(f"atT{i}", [128, RPAD], BF16) for i in range(4)]
            xh = [sb5(f"xh{i}", [128, 512], BF16) for i in range(4)]
            ctxxn = sb5("ctxxn", [128, NRT * 1024])
            s5_tc0, s5_vc0 = TC.n, VC.n
            iat = [None] * 128
            ixh = [None] * 128

            @block.sync
            def _(e):
                for i in range(128):
                    c = i % 64
                    if i >= 4:
                        e.wait_ge(TC.h, s5_tc0 + (i - 3))
                    iat[i] = SYN.issue(e, atT[i % 4][:],
                                       attnscr[:, c * 128:(c + 1) * 128],
                                       transpose=True)

            @block.scalar
            def _(e):
                for i in range(128):
                    fh, c = i // 64, i % 64
                    if i >= 4:
                        e.wait_ge(TC.h, s5_tc0 + (i - 3))
                    ixh[i] = SCL.issue(e, xh[i % 4][:],
                                       xscr[c * 128:(c + 1) * 128,
                                            fh * 512:(fh + 1) * 512])

            @block.tensor
            def _(e):
                for i in range(128):
                    fh, c = i // 64, i % 64
                    SYN.wait(e, iat[i])
                    SCL.wait(e, ixh[i])
                    if fh == 1 and c == 0:
                        e.wait_ge(VC.h, s5_vc0 + NRT)
                    for rt in range(NRT):
                        mm = e.matmul(psum[rt][:],
                                      atT[i % 4][:, rt * 128:(rt + 1) * 128],
                                      xh[i % 4][:], start=(c == 0), stop=(c == 63))
                    mm.then_inc(TC.h, 1)
                    TC.inc()
            s5_tc_end = TC.n

            @block.vector
            def _(e):
                for fh in range(2):
                    e.wait_ge(TC.h, s5_tc0 + 64 * (fh + 1))
                    for rt in range(NRT):
                        e.tensor_scalar(ctxxn[:, rt * 1024 + fh * 512:
                                              rt * 1024 + (fh + 1) * 512],
                                        psum[rt][:], rsum[:, rt:rt + 1], None,
                                        op0=ALU.mult).then_inc(VC.h, 1)
                        VC.inc()

            barrier(block)
            tap(block, "ctxxn", ctxxn[:])
            if stop_stage <= 5:
                return nc

            g6 = ExitStack(); root.enter_context(g6)
            sb6 = mk_sb(g6)
            wv32 = sb6("wv32", [128, 8192])
            wvT = sb6("wvT", [128, H * 8 * 64], BF16)
            wo32 = sb6("wo32", [128, 8192])
            woT = sb6("woT", [128, 8192], BF16)
            ctxxT = sb6("ctxxT", [128, 8 * RPAD], BF16)
            ctxT = sb6("ctxT", [128, 8 * K], BF16)
            cm32 = sb6("cm32", [128, 1])
            p32 = [sb6(f"p32_{i}", [128, K]) for i in range(2)]

            @block.sync
            def _(e):
                MSC.issue(e, wv32[:].rearrange("p (r f) -> p r f", f=1024),
                          ipw[2048:3072, :].rearrange("(r p) f -> p r f", p=128))
                MSC.issue(e, wo32[:].rearrange("p (r f) -> p r f", f=1024),
                          outw.rearrange("(r p) f -> p r f", p=128))
            i_s6w = len(MSC.marks) - 1

            s6_tc0, s6_vc0 = TC.n, VC.n

            @block.tensor
            def _(e):
                MSC.wait_upto(e, i_s6w)
                n = 0
                for kc in range(8):
                    for oc in range(8):
                        if n >= 2:
                            e.wait_ge(VC.h, s6_vc0 + n - 1)
                        e.transpose(psum[n % 2][:, 0:128],
                                    wo32[:, oc * 1024 + kc * 128:oc * 1024 + (kc + 1) * 128],
                                    id32[:]).then_inc(TC.h, 1)
                        TC.inc(); n += 1
                for h in range(H):
                    r, po = h // 2, (h % 2) * 64
                    for fc in range(8):
                        e.wait_ge(VC.h, s6_vc0 + n - 1)
                        e.transpose(psum[n % 2][:, 0:64],
                                    wv32[po:po + 64, r * 1024 + fc * 128:r * 1024 + (fc + 1) * 128],
                                    id32[po:po + 64, po:po + 64],
                                    tile_position=(po, 0)).then_inc(TC.h, 1)
                        TC.inc(); n += 1
                for rt in range(NRT):
                    for fc in range(8):
                        e.wait_ge(VC.h, s6_vc0 + n - 1)
                        e.transpose(psum[n % 2][:, 0:128],
                                    ctxxn[:, rt * 1024 + fc * 128:rt * 1024 + (fc + 1) * 128],
                                    id32[:]).then_inc(TC.h, 1)
                        TC.inc(); n += 1
            s6_tc_tr = TC.n

            @block.vector
            def _(e):
                n = 0
                for kc in range(8):
                    for oc in range(8):
                        e.wait_ge(TC.h, s6_tc0 + n + 1)
                        e.tensor_copy(woT[:, kc * 1024 + oc * 128:kc * 1024 + (oc + 1) * 128],
                                      psum[n % 2][:, 0:128]).then_inc(VC.h, 1)
                        VC.inc(); n += 1
                for h in range(H):
                    for fc in range(8):
                        e.wait_ge(TC.h, s6_tc0 + n + 1)
                        e.tensor_copy(wvT[:, (h * 8 + fc) * 64:(h * 8 + fc + 1) * 64],
                                      psum[n % 2][:, 0:64]).then_inc(VC.h, 1)
                        VC.inc(); n += 1
                for rt in range(NRT):
                    for fc in range(8):
                        e.wait_ge(TC.h, s6_tc0 + n + 1)
                        e.tensor_copy(ctxxT[:, fc * RPAD + rt * 128:
                                            fc * RPAD + (rt + 1) * 128],
                                      psum[n % 2][:, 0:128]).then_inc(VC.h, 1)
                        VC.inc(); n += 1
            s6_vc_tr = VC.n

            s6_tc1, s6_vc1 = TC.n, VC.n

            @block.tensor
            def _(e):
                e.wait_ge(VC.h, s6_vc_tr)
                for h in range(H):
                    po = (h % 2) * 64
                    if h >= 2:
                        e.wait_ge(VC.h, s6_vc1 + (h - 2) + 1)
                    for fc in range(8):
                        mm = e.matmul(psum[h % 2][po:po + 64, 0:K],
                                      wvT[:, (h * 8 + fc) * 64:(h * 8 + fc + 1) * 64],
                                      ctxxT[:, fc * RPAD + h * K:fc * RPAD + (h + 1) * K],
                                      start=(fc == 0), stop=(fc == 7),
                                      tile_position=(0, po))
                    mm.then_inc(TC.h, 1)
                    TC.inc()
            s6_tc_ctx = TC.n

            @block.vector
            def _(e):
                for h in range(H):
                    po = (h % 2) * 64
                    e.wait_ge(TC.h, s6_tc1 + h + 1)
                    e.tensor_scalar(ctxT[po:po + 64, (h // 2) * K:(h // 2 + 1) * K],
                                    psum[h % 2][po:po + 64, 0:K],
                                    bcols[po:po + 64, 16 + h // 2:17 + h // 2],
                                    None, op0=ALU.add).then_inc(VC.h, 1)
                    VC.inc()
            s6_vc_ctx = VC.n

            @block.tensor
            def _(e):
                e.wait_ge(VC.h, s6_vc_ctx)
                for ec in range(8):
                    e.matmul(psum[2][0:K, :], ctxT[:, ec * K:(ec + 1) * K],
                             woT[:, ec * 1024:ec * 1024 + 512],
                             start=(ec == 0), stop=(ec == 7))
                    mm = e.matmul(psum[3][0:K, :], ctxT[:, ec * K:(ec + 1) * K],
                                  woT[:, ec * 1024 + 512:(ec + 1) * 1024],
                                  start=(ec == 0), stop=(ec == 7))
                mm.then_inc(TC.h, 1)
                TC.inc()
            s6_tc_os = TC.n

            @block.vector
            def _(e):
                e.wait_ge(TC.h, s6_tc_os)
                e.tensor_copy(oselr[:, 0:512], psum[2][0:K, :])
                e.tensor_copy(oselr[:, 512:1024], psum[3][0:K, :]
                              ).then_inc(VC.h, 1)
                VC.inc()
            s6_vc_os = VC.n

            s6_tc3, s6_vc3, s6_sc3 = TC.n, VC.n, SC.n

            @block.vector
            def _(e):
                for ci in range(16):
                    if ci >= 2:
                        e.wait_ge(TC.h, s6_tc3 + 2 * (ci - 1))
                    e.tensor_copy(p32[ci % 2][:], ptbf[:, ci * K:(ci + 1) * K]
                                  ).then_inc(VC.h, 1)
                    VC.inc()
                    if ci >= 1:
                        e.wait_ge(TC.h, s6_tc3 + 2 * ci)
                    e.reduce_sum(cm32[:], ptbf[:, ci * K:(ci + 1) * K], axis=AX.X
                                 ).then_inc(VC.h, 1)
                    VC.inc()

            @block.tensor
            def _(e):
                for ci in range(16):
                    if ci >= 2:
                        e.wait_ge(SC.h, s6_sc3 + 2 * (ci - 2) + 2)
                    e.wait_ge(VC.h, s6_vc3 + 2 * ci + 1)
                    e.transpose(psum[4 + (2 * ci) % 4][0:K, 0:128],
                                p32[ci % 2][:],
                                id32[:]).then_inc(TC.h, 1)
                    TC.inc()
                    e.wait_ge(VC.h, s6_vc3 + 2 * ci + 2)
                    e.transpose(psum[4 + (2 * ci + 1) % 4][0:1, 0:128], cm32[:],
                                id32[:]).then_inc(TC.h, 1)
                    TC.inc()

            @block.scalar
            def _(e):
                for ci in range(16):
                    e.wait_ge(TC.h, s6_tc3 + 2 * ci + 1)
                    e.activation(out=slhsP[:, ci * 128:(ci + 1) * 128],
                                 in_=psum[4 + (2 * ci) % 4][0:K, 0:128],
                                 func=AF.Identity).then_inc(SC.h, 1)
                    SC.inc()
                    e.wait_ge(TC.h, s6_tc3 + 2 * ci + 2)
                    e.activation(out=cmrow[0:1, ci * 128:(ci + 1) * 128],
                                 in_=psum[4 + (2 * ci + 1) % 4][0:1, 0:128],
                                 func=AF.Identity).then_inc(SC.h, 1)
                    SC.inc()

            barrier(block)
            g6.close()
            g5.close()
            g_mid.close()
            tap(block, "stackrhs", oselr[:])
            tap(block, "slhs", slhsP[:])
            if stop_stage <= 6:
                return nc

            g7 = ExitStack(); root.enter_context(g7)
            sb7 = mk_sb(g7)
            xlf = sb7("xlf", [128, 4096])
            y0b = sb7("y0b", [128, 4096])
            ffb = sb7("ffb", [128, 4096])
            y1T = sb7("y1T", [128, 8 * 512], BF16)
            Hsb = sb7("Hsb", [128, NHC * 512], BF16)
            ffbb = ffb[:].bitcast(BF16)
            wst = [ffbb[:, 4096 + i * 1024:4096 + (i + 1) * 1024] for i in range(4)]
            lnm = sb7("lnm", [128, 4])
            lnv = sb7("lnv", [128, 4])
            lnr = sb7("lnr", [128, 4])
            glogT = sb7("glogT", [128, 32])
            gms = sb7("gms", [128, 4])
            grs = sb7("grs", [128, 4])
            gmx = sb7("gmx", [128, 4])
            gbm = sb7("gbm", [128, 4])
            gse = sb7("gse", [128, 4])
            gg = sb7("gg", [128, 4])
            gt2 = sb7("gt2", [128, 4])
            ng = sb7("ng", [128, 4])
            sgap = sb7("sgap", [128, 1])

            for t in range(NTILE):
                st = {"tc0": TC.n, "vc0": VC.n, "sc0": SC.n}

                @block.sync
                def _(e, t=t, st=st):
                    st["i_xl"] = MSC.issue(
                        e, xlf[:].rearrange("p (c f) -> p c f", f=1024),
                        xb[t * TT:(t + 1) * TT, :]
                        .rearrange("(c p) f -> p c f", p=128))

                @block.tensor
                def _(e, t=t, st=st):
                    for ci in range(4):
                        gci = t * 4 + ci
                        for nh in range(2):
                            pb = psum[2 * ci + nh]
                            e.matmul(pb[:], slhsP[:, gci * 128:(gci + 1) * 128],
                                     oselr[:, nh * 512:(nh + 1) * 512],
                                     start=True, stop=False)
                            e.matmul(pb[:], cmrow[0:1, gci * 128:(gci + 1) * 128],
                                     obmrow[:, nh * 512:(nh + 1) * 512],
                                     start=False, stop=False)
                            mm = e.matmul(pb[:], onesbfr[:],
                                          meanrow[:, nh * 512:(nh + 1) * 512],
                                          start=False, stop=True)
                    mm.then_inc(TC.h, 1)
                    TC.inc()
                    st["tc_sc"] = TC.n

                @block.vector
                def _(e, t=t, st=st):
                    e.wait_ge(TC.h, st["tc_sc"])
                    MSC.wait(e, st["i_xl"])
                    for ci in range(4):
                        e.tensor_tensor(y0b[:, ci * 1024:ci * 1024 + 512],
                                        psum[2 * ci][:],
                                        xlf[:, ci * 1024:ci * 1024 + 512], op=ALU.add)
                        e.tensor_tensor(y0b[:, ci * 1024 + 512:(ci + 1) * 1024],
                                        psum[2 * ci + 1][:],
                                        xlf[:, ci * 1024 + 512:(ci + 1) * 1024],
                                        op=ALU.add)
                    e.engine_nop().then_inc(VC.h, 1); VC.inc()
                    st["vc_y0"] = VC.n
                    for ci in range(4):
                        e.reduce_sum(lnm[:, ci:ci + 1],
                                     y0b[:, ci * 1024:(ci + 1) * 1024], axis=AX.X)
                    e.tensor_copy(gapA[:], gapB[:])
                    e.tensor_scalar(lnm[:], lnm[:], 1.0 / 1024, None, op0=ALU.mult)
                    e.tensor_copy(gapA[:], gapB[:])
                    for ci in range(4):
                        e.tensor_scalar(xlf[:, ci * 1024:(ci + 1) * 1024],
                                        y0b[:, ci * 1024:(ci + 1) * 1024],
                                        lnm[:, ci:ci + 1], None, op0=ALU.subtract)
                    e.engine_nop().then_inc(VC.h, 1); VC.inc()
                    st["vc_xc"] = VC.n

                @block.scalar
                def _(e, t=t, st=st):
                    e.wait_ge(VC.h, st["vc_xc"])
                    for ci in range(4):
                        e.activation(out=ffb[:, ci * 1024:(ci + 1) * 1024],
                                     in_=xlf[:, ci * 1024:(ci + 1) * 1024],
                                     func=AF.Square, accum_out=lnv[:, ci:ci + 1])
                    e.activation(out=sgap[:], in_=gapB[:], func=AF.Identity)
                    e.activation(out=lnr[:], in_=lnv[:], func=AF.Ln,
                                 scale=1.0 / 1024, bias=cst[:, 0:1])
                    e.activation(out=lnr[:], in_=lnr[:], func=AF.Exp, scale=-0.5
                                 ).then_inc(SC.h, 1)
                    SC.inc()
                    st["sc_ln1"] = SC.n

                @block.vector
                def _(e, t=t, st=st):
                    e.wait_ge(SC.h, st["sc_ln1"])
                    for ci in range(4):
                        e.tensor_scalar(y0b[:, ci * 1024:(ci + 1) * 1024],
                                        xlf[:, ci * 1024:(ci + 1) * 1024],
                                        lnr[:, ci:ci + 1], None, op0=ALU.mult)
                    e.engine_nop().then_inc(VC.h, 1); VC.inc()
                    st["vc_y1"] = VC.n

                if t == 0 and "t0y1" in dbg_out:
                    @block.sync
                    def _(e, st=st):
                        e.wait_ge(VC.h, st["vc_y1"])
                        MSC.issue(e, dbg_out["t0y1"], y0b[:])

                @block.tensor
                def _(e, t=t, st=st):
                    e.wait_ge(VC.h, st["vc_y1"])
                    for kc in range(8):
                        if kc >= 2:
                            e.wait_ge(YE.h, 8 * t + kc - 1)
                        for ci in range(4):
                            tr = e.transpose(psum[kc % 2][:, ci * 128:(ci + 1) * 128],
                                             y0b[:, ci * 1024 + kc * 128:
                                                 ci * 1024 + (kc + 1) * 128],
                                             id32[:])
                        tr.then_inc(YT.h, 1)
                        YT.inc()

                @block.vector
                def _(e, t=t, st=st):
                    for kc in range(8):
                        e.wait_ge(YT.h, 8 * t + kc + 1)
                        e.tensor_copy(y1T[:, kc * 512:(kc + 1) * 512],
                                      psum[kc % 2][:, 0:512]).then_inc(YE.h, 1)
                        YE.inc()
                    e.engine_nop().then_inc(VC.h, 1); VC.inc()
                    st["vc_y1t"] = VC.n

                @block.tensor
                def _(e, t=t, st=st):
                    e.wait_ge(VC.h, st["vc_y1t"])
                    for kc in range(8):
                        mm = e.matmul(psum[4][0:8, :], gwT[:, kc * 8:(kc + 1) * 8],
                                      y1T[:, kc * 512:(kc + 1) * 512],
                                      start=(kc == 0), stop=(kc == 7))
                    mm.then_inc(TC.h, 1)
                    TC.inc()
                    st["tc_gl"] = TC.n

                @block.vector
                def _(e, t=t, st=st):
                    e.wait_ge(TC.h, st["tc_gl"])
                    e.tensor_scalar(xlf[0:8, 2048:2560], psum[4][0:8, :], gbcol[:, 0:1], None,
                                    op0=ALU.add).then_inc(VC.h, 1)
                    VC.inc()
                    st["vc_gl"] = VC.n

                @block.tensor
                def _(e, t=t, st=st):
                    e.wait_ge(VC.h, st["vc_gl"])
                    for ci in range(4):
                        tr = e.transpose(psum[5][0:128, ci * 8:(ci + 1) * 8],
                                         xlf[0:8, 2048 + ci * 128:2048 + (ci + 1) * 128],
                                         id32[0:8, 0:8])
                    tr.then_inc(TC.h, 1)
                    TC.inc()
                    st["tc_glt"] = TC.n

                @block.vector
                def _(e, t=t, st=st):
                    e.wait_ge(TC.h, st["tc_glt"])
                    e.tensor_copy(glogT[:], psum[5][0:128, 0:32]).then_inc(VC.h, 1)
                    VC.inc()
                    for ci in range(4):
                        e.tensor_tensor(xlf[:, 2560 + ci * 8:2560 + (ci + 1) * 8],
                                        glogT[:, ci * 8:(ci + 1) * 8],
                                        glogT[:, ci * 8:(ci + 1) * 8], op=ALU.mult)
                    for ci in range(4):
                        e.reduce_sum(gms[:, ci:ci + 1], xlf[:, 2560 + ci * 8:2560 + (ci + 1) * 8],
                                     axis=AX.X)
                    e.engine_nop().then_inc(VC.h, 1); VC.inc()
                    st["vc_gms"] = VC.n

                @block.scalar
                def _(e, t=t, st=st):
                    e.wait_ge(VC.h, st["vc_gms"])
                    e.activation(out=grs[:], in_=gms[:], func=AF.Ln,
                                 scale=1.0 / 8, bias=cst[:, 1:2])
                    e.activation(out=grs[:], in_=grs[:], func=AF.Exp, scale=-0.5
                                 ).then_inc(SC.h, 1)
                    SC.inc()
                    st["sc_grs"] = SC.n

                @block.vector
                def _(e, t=t, st=st):
                    e.wait_ge(SC.h, st["sc_grs"])
                    for ci in range(4):
                        e.tensor_scalar(xlf[:, 2592 + ci * 8:2592 + (ci + 1) * 8],
                                        glogT[:, ci * 8:(ci + 1) * 8],
                                        grs[:, ci:ci + 1], None, op0=ALU.mult)
                    e.tensor_copy(gapA[:], gapB[:])
                    for ci in range(4):
                        e.tensor_tensor(xlf[:, 2592 + ci * 8:2592 + (ci + 1) * 8],
                                        xlf[:, 2592 + ci * 8:2592 + (ci + 1) * 8],
                                        rmsb[:], op=ALU.mult)
                    e.tensor_copy(gapA[:], gapB[:])
                    for ci in range(4):
                        e.reduce_max(gmx[:, ci:ci + 1],
                                     xlf[:, 2592 + ci * 8:2592 + (ci + 1) * 8],
                                     axis=AX.X)
                    e.tensor_copy(gapA[:], gapB[:])
                    e.tensor_scalar(gbm[:], gmx[:], -2.0, None, op0=ALU.mult
                                    ).then_inc(VC.h, 1)
                    VC.inc()
                    st["vc_gbm"] = VC.n

                @block.scalar
                def _(e, t=t, st=st):
                    e.wait_ge(VC.h, st["vc_gbm"])
                    for ci in range(3):
                        e.activation(out=xlf[:, 2560 + ci * 8:2560 + (ci + 1) * 8],
                                     in_=xlf[:, 2592 + ci * 8:2592 + (ci + 1) * 8], func=AF.Exp,
                                     scale=2.0, bias=gbm[:, ci:ci + 1],
                                     accum_out=gse[:, ci:ci + 1])
                    e.activation(out=xlf[:, 2584:2592],
                                 in_=xlf[:, 2616:2624], func=AF.Exp,
                                 scale=2.0, bias=gbm[:, 3:4],
                                 accum_out=gse[:, 3:4]).then_inc(SC.h, 1)
                    SC.inc()
                    st["sc_gse"] = SC.n

                @block.vector
                def _(e, t=t, st=st):
                    e.wait_ge(SC.h, st["sc_gse"])
                    e.reciprocal(gg[:], gse[:])
                    e.tensor_copy(gapA[:], gapB[:])
                    e.tensor_scalar(gt2[:], gg[:], 1e-6, None, op0=ALU.add)
                    e.tensor_copy(gapA[:], gapB[:])
                    e.reciprocal(gt2[:], gt2[:])
                    e.tensor_copy(gapA[:], gapB[:])
                    e.tensor_tensor(ng[:], gg[:], gt2[:], op=ALU.mult
                                    ).then_inc(VC.h, 1)
                    VC.inc()
                    st["vc_ng"] = VC.n

                if t == 0 and "t0ng" in dbg_out:
                    @block.sync
                    def _(e, st=st):
                        e.wait_ge(VC.h, st["vc_ng"])
                        MSC.issue(e, dbg_out["t0ng"], ng[:])

                @block.sync
                def _(e, t=t, st=st):
                    iw1 = [None] * NHC
                    iw2 = [None] * NHC
                    st["iw1"] = iw1
                    st["iw2"] = iw2
                    e.wait_ge(SC.h, st["sc_ln1"])
                    for hc in range(NHC):
                        g = t * NHC + hc
                        if g >= 4:
                            e.wait_ge(WA.h, g - 3)
                        iw1[hc] = SYN.issue(e, wst[hc % 4], w1scr[hc])
                    for hc in range(NHC):
                        if hc < 4:
                            e.wait_ge(WA.h, (t + 1) * NHC)
                        else:
                            e.wait_ge(WB.h, t * NHC + hc - 3)
                        iw2[hc] = SYN.issue(e, wst[hc % 4],
                                            w2scr[hc * 128:(hc + 1) * 128, :])

                @block.tensor
                def _(e, t=t, st=st):
                    e.wait_ge(VC.h, st["vc_y1t"])
                    for hc in range(NHC):
                        SYN.wait(e, st["iw1"][hc])
                        if hc >= 4:
                            e.wait_ge(GE.h, t * NHC + hc - 3)
                        for kc in range(8):
                            mm = e.matmul(psum[hc % 4][:],
                                          wst[hc % 4][:, kc * 128:(kc + 1) * 128],
                                          y1T[:, kc * 512:(kc + 1) * 512],
                                          start=(kc == 0), stop=(kc == 7))
                        mm.then_inc(WA.h, 1)
                        WA.inc()

                @block.scalar
                def _(e, t=t, st=st):
                    for hc in range(NHC):
                        e.wait_ge(WA.h, t * NHC + hc + 1)
                        e.activation(out=Hsb[:, hc * 512:(hc + 1) * 512],
                                     in_=psum[hc % 4][:], func=AF.Gelu,
                                     bias=b1col[:, hc:hc + 1]).then_inc(GE.h, 1)
                        GE.inc()

                @block.tensor
                def _(e, t=t, st=st):
                    e.wait_ge(GE.h, (t + 1) * NHC)
                    for m in range(4):
                        for n_ in range(2):
                            e.matmul(psum[m * 2 + n_][:], onesbfr[:],
                                     b2sumr[:, n_ * 512:(n_ + 1) * 512],
                                     start=True, stop=False)
                    for hc in range(NHC):
                        SYN.wait(e, st["iw2"][hc])
                        for m in range(4):
                            for n_ in range(2):
                                mm = e.matmul(psum[m * 2 + n_][:],
                                              Hsb[:, hc * 512 + m * 128:
                                                  hc * 512 + (m + 1) * 128],
                                              wst[hc % 4][:, n_ * 512:(n_ + 1) * 512],
                                              start=False, stop=(hc == NHC - 1))
                        mm.then_inc(WB.h, 1)
                        WB.inc()

                @block.vector
                def _(e, t=t, st=st):
                    e.wait_ge(WB.h, (t + 1) * NHC)
                    e.wait_ge(VC.h, st["vc_ng"])
                    for m in range(4):
                        for n_ in range(2):
                            e.tensor_scalar(ffb[:, m * 1024 + n_ * 512:
                                                m * 1024 + (n_ + 1) * 512],
                                            psum[m * 2 + n_][:], ng[:, m:m + 1],
                                            None, op0=ALU.mult)
                    e.engine_nop().then_inc(VC.h, 1); VC.inc()
                    for ci in range(4):
                        e.tensor_tensor(xlf[:, ci * 1024:(ci + 1) * 1024],
                                        y0b[:, ci * 1024:(ci + 1) * 1024],
                                        ffb[:, ci * 1024:(ci + 1) * 1024],
                                        op=ALU.add)
                    e.engine_nop().then_inc(VC.h, 1); VC.inc()
                    for ci in range(4):
                        e.reduce_sum(lnm[:, ci:ci + 1],
                                     xlf[:, ci * 1024:(ci + 1) * 1024], axis=AX.X)
                    e.tensor_copy(gapA[:], gapB[:])
                    e.tensor_scalar(lnm[:], lnm[:], 1.0 / 1024, None, op0=ALU.mult)
                    e.tensor_copy(gapA[:], gapB[:])
                    for ci in range(4):
                        e.tensor_scalar(y0b[:, ci * 1024:(ci + 1) * 1024],
                                        xlf[:, ci * 1024:(ci + 1) * 1024],
                                        lnm[:, ci:ci + 1], None, op0=ALU.subtract)
                    e.engine_nop().then_inc(VC.h, 1); VC.inc()
                    st["vc_xc2"] = VC.n

                if t == 0 and "t0ff" in dbg_out:
                    @block.sync
                    def _(e, st=st):
                        e.wait_ge(VC.h, st["vc_xc2"] - 2)
                        MSC.issue(e, dbg_out["t0ff"], ffb[:])

                if t == 0 and "t0H" in dbg_out:
                    @block.sync
                    def _(e, st=st):
                        e.wait_ge(GE.h, NHC)
                        MSC.issue(e, dbg_out["t0H"], Hsb[:, 0:2048])

                @block.scalar
                def _(e, t=t, st=st):
                    e.wait_ge(VC.h, st["vc_xc2"])
                    for ci in range(4):
                        e.activation(out=ffb[:, ci * 1024:(ci + 1) * 1024],
                                     in_=y0b[:, ci * 1024:(ci + 1) * 1024],
                                     func=AF.Square, accum_out=lnv[:, ci:ci + 1])
                    e.activation(out=sgap[:], in_=gapB[:], func=AF.Identity)
                    e.activation(out=lnr[:], in_=lnv[:], func=AF.Ln,
                                 scale=1.0 / 1024, bias=cst[:, 0:1])
                    e.activation(out=lnr[:], in_=lnr[:], func=AF.Exp, scale=-0.5
                                 ).then_inc(SC.h, 1)
                    SC.inc()
                    st["sc_ln2"] = SC.n

                @block.vector
                def _(e, t=t, st=st):
                    e.wait_ge(SC.h, st["sc_ln2"])
                    for ci in range(4):
                        e.tensor_scalar(ffb[:, ci * 1024:(ci + 1) * 1024],
                                        y0b[:, ci * 1024:(ci + 1) * 1024],
                                        lnr[:, ci:ci + 1], None, op0=ALU.mult)
                    e.engine_nop().then_inc(VC.h, 1); VC.inc()
                    st["vc_y2"] = VC.n

                @block.scalar
                def _(e, t=t, st=st):
                    e.wait_ge(VC.h, st["vc_y2"])
                    SCL.issue(e, yout[t * TT:(t + 1) * TT, :]
                              .rearrange("(c p) f -> p c f", p=128),
                              ffb[:].rearrange("p (c f) -> p c f", f=1024))

                barrier(block)

            g7.close()

            qf = [(s.h, s.n) for s in ALLQ]

            @block.gpsimd
            def _(e):
                for h, v in qf:
                    e.wait_ge(h, v)

    return nc


_CACHED = {}


def _prep_inputs(L0_attn_x, L0_attn_in_proj_w, L0_attn_in_proj_b, L0_attn_out_w,
                 L0_attn_out_b, L0_ffn_gate_w, L0_ffn_gate_b, L0_ffn_rms_w,
                 L0_ffn_w1, L0_ffn_b1, L0_ffn_w2, L0_ffn_b2):
    c = lambda a: np.ascontiguousarray(np.asarray(a, dtype=np.float32))
    x = c(L0_attn_x)
    w1r = c(np.transpose(np.asarray(L0_ffn_w1, np.float32)
                         .reshape(8, 8, 128, 16, 128), (0, 3, 2, 1, 4))
            .reshape(NHC, 128, 1024))
    w2r = c(np.asarray(L0_ffn_w2, np.float32).reshape(NHID, E))
    b1c = c(np.asarray(L0_ffn_b1, np.float32).reshape(128, 128).T)
    common = {
        "ipw": c(L0_attn_in_proj_w),
        "ipb": c(np.asarray(L0_attn_in_proj_b, np.float32).reshape(24, 128)),
        "outw": c(L0_attn_out_w),
        "outb": c(np.asarray(L0_attn_out_b, np.float32).reshape(1, E)),
        "gatew": c(L0_ffn_gate_w),
        "gateb": c(np.asarray(L0_ffn_gate_b, np.float32).reshape(1, NEXP)),
        "rmsw": c(np.asarray(L0_ffn_rms_w, np.float32).reshape(1, NEXP)),
        "w1r": w1r, "w2r": w2r, "b1c": b1c,
        "b2": c(L0_ffn_b2),
        "ident": np.eye(128, dtype=np.float32),
        "onesc": np.ones((128, 1), np.float32),
        "onesr": np.ones((1, 128), np.float32),
    }
    in_maps = []
    for core in range(8):
        b, part = core // 4, core % 4
        m = dict(common)
        m["xb"] = c(np.roll(x[b], -part * NTOK, axis=0))
        in_maps.append(m)
    return in_maps


def kernel(**inputs):
    in_maps = _prep_inputs(**inputs)
    if "nc" not in _CACHED:
        _CACHED["nc"] = build_program()
    nc = _CACHED["nc"]
    res = run_bass_kernel_spmd(nc, in_maps, core_ids=list(range(8)))
    out = np.empty((B, S, E), np.float32)
    for core in range(8):
        b, part = core // 4, core % 4
        out[b, part * NTOK:(part + 1) * NTOK] = res.results[core]["y"]
    return out


# version 12 (final)
# speedup vs baseline: 1.1707x; 1.0130x over previous
import numpy as np
from contextlib import ExitStack

import concourse.bass as bass
import concourse.mybir as mybir
from concourse.bass_utils import run_bass_kernel_spmd

F32 = mybir.dt.float32
BF16 = mybir.dt.bfloat16
AF = mybir.ActivationFunctionType
ALU = mybir.AluOpType
AX = mybir.AxisListType

B, S, E = 2, 8192, 1024
H, HD = 16, 64
K = 45
NEXP, HID = 8, 2048
NHID = NEXP * HID
NTOK = S // 4
ROWS = H * K
RPAD = 768
NRT = 6
TT = 512
NTILE = NTOK // TT
NHC = NHID // 128
EPS32 = float(np.finfo(np.float32).eps)
QB_N = H


class Ctr:
    def __init__(self, h):
        self.h = h
        self.n = 0

    def inc(self, k=1):
        self.n += k
        return self.n


class DmaStream:

    def __init__(self, sems):
        self.sems = sems
        self.marks = []

    def issue(self, e, out, in_, **kw):
        s = self.sems[len(self.marks) % len(self.sems)]
        e.dma_start(out=out, in_=in_, **kw).then_inc(s.h, 16)
        s.inc(16)
        self.marks.append((s.h, s.n))
        return len(self.marks) - 1

    def wait(self, e, i):
        h, v = self.marks[i]
        e.wait_ge(h, v)

    def wait_upto(self, e, i):
        seen = {}
        for j in range(i + 1):
            h, v = self.marks[j]
            seen[id(h)] = (h, v)
        for h, v in seen.values():
            e.wait_ge(h, v)


def build_program(dbg=None, stop_stage=99):
    dbg = dbg or {}
    nc = bass.Bass("TRN2", target_bir_lowering=False,
                   detect_race_conditions=False)

    xb = nc.dram_tensor("xb", [S, E], F32, kind="ExternalInput").ap()
    ipw = nc.dram_tensor("ipw", [3 * E, E], F32, kind="ExternalInput").ap()
    ipb = nc.dram_tensor("ipb", [24, 128], F32, kind="ExternalInput").ap()
    outw = nc.dram_tensor("outw", [E, E], F32, kind="ExternalInput").ap()
    outb = nc.dram_tensor("outb", [1, E], F32, kind="ExternalInput").ap()
    gatew = nc.dram_tensor("gatew", [NEXP, E], F32, kind="ExternalInput").ap()
    gateb = nc.dram_tensor("gateb", [1, NEXP], F32, kind="ExternalInput").ap()
    rmsw = nc.dram_tensor("rmsw", [1, NEXP], F32, kind="ExternalInput").ap()
    w1r = nc.dram_tensor("w1r", [NHC, 128, 1024], F32, kind="ExternalInput").ap()
    w2r = nc.dram_tensor("w2r", [NHID, E], F32, kind="ExternalInput").ap()
    b1c = nc.dram_tensor("b1c", [128, 128], F32, kind="ExternalInput").ap()
    b2 = nc.dram_tensor("b2", [NEXP, E], F32, kind="ExternalInput").ap()
    ident = nc.dram_tensor("ident", [128, 128], F32, kind="ExternalInput").ap()
    onesc = nc.dram_tensor("onesc", [128, 1], F32, kind="ExternalInput").ap()
    onesr = nc.dram_tensor("onesr", [1, 128], F32, kind="ExternalInput").ap()
    yout = nc.dram_tensor("y", [NTOK, E], F32, kind="ExternalOutput").ap()
    dbg_out = {k: nc.dram_tensor("dbg_" + k, list(v[0]), v[1], kind="ExternalOutput").ap()
               for k, v in dbg.items()}

    xscr = nc.dram_tensor("xscr", [S, E], BF16, kind="Internal").ap()
    w1scr = nc.dram_tensor("w1scr", [NHC, 128, 1024], BF16, kind="Internal").ap()
    w2scr = nc.dram_tensor("w2scr", [NHID, E], BF16, kind="Internal").ap()
    attnscr = nc.dram_tensor("attnscr", [RPAD, S], BF16, kind="Internal").ap()

    root = ExitStack()
    sem = lambda name: Ctr(root.enter_context(nc.semaphore(name)))

    with root:
        def mk_sb(stack):
            return lambda name, shape, dt=F32: stack.enter_context(
                nc.sbuf_tensor(name, shape, dt))

        sb = mk_sb(root)

        id32 = sb("id32", [128, 128])
        ones32c = sb("ones32c", [128, 1])
        ones32r = sb("ones32r", [1, 128])
        onesbfr = sb("onesbfr", [1, 128], BF16)
        onesbfc = sb("onesbfc", [128, 1], BF16)
        bcols = sb("bcols", [128, 24])
        b1col = sb("b1col", [128, 128])
        b2sumr = sb("b2sumr", [1, 1024], BF16)
        rmsb = sb("rmsb", [128, NEXP])
        gwT = sb("gwT", [128, 64], BF16)
        gbcol = sb("gbcol", [8, 1])
        cmax = sb("cmax", [128, 1])
        gmax1 = sb("gmax1", [1, 1])
        smax = sb("smax", [1, 1])
        gapA = sb("gapA", [128, 1])
        gapB = sb("gapB", [128, 1])
        bsc = sb("bsc", [1, 1])
        cst = sb("cst", [128, 2])
        rsum = sb("rsum", [128, NRT])
        sumex = sb("sumex", [128, NRT])
        oselr = sb("oselr", [K, 1024], BF16)
        obmrow = sb("obmrow", [1, 1024], BF16)
        meanrow = sb("meanrow", [1, 1024], BF16)
        slhsP = sb("slhsP", [K, 16 * 128], BF16)
        cmrow = sb("cmrow", [1, 16 * 128], BF16)

        LS0 = sem("LS0"); LS1 = sem("LS1"); LS2 = sem("LS2"); LS3 = sem("LS3")
        LC0 = sem("LC0"); LC1 = sem("LC1"); LC2 = sem("LC2"); LC3 = sem("LC3")
        LG0 = sem("LG0"); LG1 = sem("LG1")
        LM = sem("LM")
        LN_ = sem("LN_")
        TC = sem("TC"); VC = sem("VC"); SC = sem("SC")
        BAR = sem("BAR")
        WA = sem("WA"); GE = sem("GE"); WB = sem("WB")
        WCA = sem("WCA"); WCW = sem("WCW")
        YT = sem("YT"); YE = sem("YE"); RN = sem("RN")
        SYN = DmaStream([LS0, LS1, LS2, LS3])
        SCL = DmaStream([LC0, LC1, LC2, LC3])
        GPS = DmaStream([LG0, LG1])
        MSC = DmaStream([LM])
        MGP = DmaStream([LN_])
        ALLQ = [LS0, LS1, LS2, LS3, LC0, LC1, LC2, LC3, LG0, LG1, LM, LN_]

        psum = [root.enter_context(nc.psum_tensor(f"ps{i}", [128, 512], F32))
                for i in range(8)]

        g_mid = ExitStack(); root.enter_context(g_mid)
        sbm = mk_sb(g_mid)
        ss = sbm("ss", [128, 64])
        ssoh = sbm("ssoh", [128, 64])
        ptbf = sbm("ptbf", [128, 64 * K], BF16)
        ebias = sbm("ebias", [128, NRT])
        sump = sbm("sump", [128, 96])
        qtokb = sbm("qtokb", [K, 1024])
        qWf = sbm("qWf", [128, 8 * RPAD], BF16)
        wcb = [sbm(f"wcb{i}", [128, 4096], BF16) for i in range(2)]

        taps = {}

        def barrier(block):
            BAR.inc(5)
            tgt = BAR.n
            qvals = [(s.h, s.n) for s in ALLQ]

            @block.sync
            def _(e):
                for h, v in qvals:
                    e.wait_ge(h, v)
                e.sem_inc(BAR.h, 1)
                e.wait_ge(BAR.h, tgt)

            @block.scalar
            def _(e):
                e.activation(out=bsc[:], in_=ones32c[0:1, 0:1],
                             func=AF.Identity).then_inc(BAR.h, 1)
                e.wait_ge(BAR.h, tgt)

            @block.gpsimd
            def _(e):
                e.engine_nop().then_inc(BAR.h, 1)
                e.wait_ge(BAR.h, tgt)

            @block.tensor
            def _(e):
                e.nop().then_inc(BAR.h, 1)
                e.wait_ge(BAR.h, tgt)

            @block.vector
            def _(e):
                e.engine_nop().then_inc(BAR.h, 1)
                e.wait_ge(BAR.h, tgt)


        def emit_cast_slice(block, lo, hi):
            @block.gpsimd
            def _(e):
                for i in range(lo, hi):
                    if i >= 2:
                        e.wait_ge(WCW.h, 16 * (i - 1))
                    if i < 32:
                        e.dma_start(out=wcb[i % 2][:].rearrange("p (b f) -> p b f", f=1024),
                                    in_=w1r[i * 4:(i + 1) * 4].rearrange("b p f -> p b f")
                                    ).then_inc(WCA.h, 16)
                        WCA.inc(16)
                        e.wait_ge(WCA.h, WCA.n)
                        e.dma_start(out=w1scr[i * 4:(i + 1) * 4].rearrange("b p f -> p b f"),
                                    in_=wcb[i % 2][:].rearrange("p (b f) -> p b f", f=1024)
                                    ).then_inc(WCW.h, 16)
                        WCW.inc(16)
                    else:
                        j = i - 32
                        e.dma_start(out=wcb[i % 2][:].rearrange("p (b f) -> p b f", f=1024),
                                    in_=w2r[j * 512:(j + 1) * 512, :]
                                    .rearrange("(b p) f -> p b f", p=128)
                                    ).then_inc(WCA.h, 16)
                        WCA.inc(16)
                        e.wait_ge(WCA.h, WCA.n)
                        e.dma_start(out=w2scr[j * 512:(j + 1) * 512, :]
                                    .rearrange("(b p) f -> p b f", p=128),
                                    in_=wcb[i % 2][:].rearrange("p (b f) -> p b f", f=1024)
                                    ).then_inc(WCW.h, 16)
                        WCW.inc(16)

        def tap(block, name, ap):
            if name not in dbg_out:
                return
            @block.sync
            def _(e):
                MSC.issue(e, dbg_out[name], ap)
            barrier(block)

        with nc.Block() as block:
            g1 = ExitStack(); root.enter_context(g1)
            sb1 = mk_sb(g1)
            ipb_sb = sb1("ipb_sb", [24, 128])
            b2_sb = sb1("b2_sb", [NEXP, 1024])
            rms_sb = sb1("rms_sb", [1, NEXP])
            gb_sb = sb1("gb_sb", [1, NEXP])
            gw_sb = sb1("gw_sb", [NEXP, 1024])
            outb_sb = sb1("outb_sb", [1, 1024], BF16)
            xf = [sb1(f"xf{i}", [128, 1024]) for i in range(4)]
            xbf = [sb1(f"xbf{i}", [128, 1024], BF16) for i in range(4)]
            junk1 = sb1("junk1", [128, 1024])

            @block.sync
            def _(e):
                for t, src in [(id32, ident), (ones32c, onesc), (ones32r, onesr),
                               (b1col, b1c), (ipb_sb, ipb), (b2_sb, b2),
                               (rms_sb, rmsw), (gb_sb, gateb), (gw_sb, gatew)]:
                    MSC.issue(e, t[:], src)

            @block.gpsimd
            def _(e):
                for t, src in [(onesbfr, onesr), (onesbfc, onesc),
                               (outb_sb, outb)]:
                    MGP.issue(e, t[:], src)
            i_const = len(MSC.marks) - 1
            ig_const = len(MGP.marks) - 1

            tc0 = TC.n
            vc0 = VC.n
            @block.tensor
            def _(e):
                MSC.wait_upto(e, i_const)
                MGP.wait_upto(e, ig_const)
                e.transpose(psum[0][:, 0:24], ipb_sb[:], id32[0:24, 0:24]
                            ).then_inc(TC.h, 1); TC.inc()
                e.matmul(psum[4][0:1, :], ones32c[0:8, 0:1], b2_sb[:, 0:512],
                         start=True, stop=True)
                e.matmul(psum[5][0:1, :], ones32c[0:8, 0:1], b2_sb[:, 512:1024],
                         start=True, stop=True).then_inc(TC.h, 1); TC.inc()
                e.matmul(psum[6][:, 0:8], ones32r[:], rms_sb[:],
                         start=True, stop=True).then_inc(TC.h, 1); TC.inc()
                e.transpose(psum[7][0:8, 0:1], gb_sb[:], id32[0:1, 0:1]
                            ).then_inc(TC.h, 1); TC.inc()
                for kc in range(8):
                    if kc >= 2:
                        e.wait_ge(VC.h, vc0 + 6 + (kc - 2) + 1)
                    e.transpose(psum[1 + kc % 2][:, 0:8],
                                gw_sb[:, kc * 128:(kc + 1) * 128],
                                id32[0:8, 0:8]).then_inc(TC.h, 1); TC.inc()
            tc_tab = TC.n

            @block.vector
            def _(e):
                e.wait_ge(TC.h, tc0 + 4)
                e.tensor_copy(bcols[:], psum[0][:, 0:24])
                e.tensor_copy(b2sumr[:, 0:512], psum[4][0:1, :])
                e.tensor_copy(b2sumr[:, 512:1024], psum[5][0:1, :])
                e.tensor_copy(rmsb[:], psum[6][:, 0:8])
                e.tensor_copy(gbcol[:], psum[7][0:8, 0:1])
                MGP.wait_upto(e, ig_const)
                e.memset(slhsP[:], 0.0)
                e.memset(qWf[:], 0.0)
                e.memset(gapB[:], 0.0)
                e.memset(cst[:, 0:1], 1e-5)
                e.memset(cst[:, 1:2], EPS32)
                e.engine_nop().then_inc(VC.h, 6); VC.inc(6)
                for kc in range(8):
                    e.wait_ge(TC.h, tc0 + 4 + kc + 1)
                    e.tensor_copy(gwT[:, kc * 8:(kc + 1) * 8],
                                  psum[1 + kc % 2][:, 0:8]).then_inc(VC.h, 1)
                    VC.inc()
            vc_tab = VC.n

            NXT = S // 128
            s1_sc0, s1_vc0, s1_tc0 = SC.n, VC.n, TC.n
            ixf = [None] * NXT
            iws = [None] * NXT

            @block.sync
            def _(e):
                for i in range(NXT):
                    if i >= 4:
                        e.wait_ge(SC.h, s1_sc0 + (i - 4) + 1)
                        e.wait_ge(VC.h, s1_vc0 + (i - 4) + 1)
                    ixf[i] = SYN.issue(e, xf[i % 4][:],
                                       xb[i * 128:(i + 1) * 128, :])

            @block.scalar
            def _(e):
                for i in range(NXT):
                    SYN.wait(e, ixf[i])
                    e.activation(out=junk1[:], in_=xf[i % 4][:], func=AF.Square,
                                 accum_out=ss[:, i:i + 1]).then_inc(SC.h, 1)
                    SC.inc()
                    e.wait_ge(VC.h, s1_vc0 + i + 1)
                    iws[i] = SCL.issue(e, xscr[i * 128:(i + 1) * 128, :],
                                       xbf[i % 4][:])

            @block.vector
            def _(e):
                for i in range(NXT):
                    SYN.wait(e, ixf[i])
                    if i >= 4:
                        SCL.wait(e, iws[i - 4])
                        e.wait_ge(TC.h, s1_tc0 + (i - 3))
                    e.tensor_copy(xbf[i % 4][:], xf[i % 4][:]).then_inc(VC.h, 1)
                    VC.inc()
            s1_vc_end = VC.n

            @block.tensor
            def _(e):
                for i in range(NXT):
                    e.wait_ge(VC.h, s1_vc0 + i + 1)
                    e.matmul(psum[6][0:1, :], onesbfc[:], xbf[i % 4][:, 0:512],
                             start=(i == 0), stop=(i == NXT - 1))
                    e.matmul(psum[7][0:1, :], onesbfc[:], xbf[i % 4][:, 512:1024],
                             start=(i == 0), stop=(i == NXT - 1)).then_inc(TC.h, 1)
                    TC.inc()
            s1_tc_end = TC.n

            @block.vector
            def _(e):
                e.wait_ge(TC.h, s1_tc_end)
                e.tensor_scalar(meanrow[0:1, 0:512], psum[6][0:1, :], 1.0 / S,
                                None, op0=ALU.mult)
                e.tensor_scalar(meanrow[0:1, 512:1024], psum[7][0:1, :], 1.0 / S,
                                None, op0=ALU.mult).then_inc(VC.h, 1)
                VC.inc()
                e.tensor_copy(gapA[:], gapB[:])
                e.tensor_tensor(obmrow[:], outb_sb[:], meanrow[:],
                                op=ALU.subtract).then_inc(VC.h, 1)
                VC.inc()

            emit_cast_slice(block, 0, 7)

            barrier(block)
            g1.close()
            tap(block, "ss", ss[:])
            tap(block, "mean", meanrow[:])
            if stop_stage <= 1:
                return nc

            s2_tc0, s2_vc0 = TC.n, VC.n

            @block.vector
            def _(e):
                for it in range(K):
                    e.wait_ge(TC.h, s2_tc0 + 2 * it)
                    e.reduce_max(cmax[:], ss[:], axis=AX.X).then_inc(VC.h, 1)
                    VC.inc()
                    e.wait_ge(TC.h, s2_tc0 + 2 * it + 1)
                    e.reduce_max(gmax1[:], psum[2][0:1, 0:128], axis=AX.X
                                 ).then_inc(VC.h, 1)
                    VC.inc()
                    e.wait_ge(TC.h, s2_tc0 + 2 * it + 2)
                    e.tensor_scalar(ssoh[:], ss[:], psum[3][:, 0:1], None,
                                    op0=ALU.is_ge)
                    if it == 0:
                        e.tensor_copy(smax[:], gmax1[:])
                    else:
                        e.tensor_copy(gapA[:], gapB[:])
                    e.scalar_tensor_tensor(ss[:], ssoh[:], -1e30, ss[:],
                                           op0=ALU.mult, op1=ALU.add)
                    e.tensor_copy(ptbf[:, it::K], ssoh[:]).then_inc(VC.h, 1)
                    VC.inc()
            s2_vc_end = VC.n

            @block.tensor
            def _(e):
                for it in range(K):
                    e.wait_ge(VC.h, s2_vc0 + 3 * it + 1)
                    e.transpose(psum[2][0:1, 0:128], cmax[:], id32[:]
                                ).then_inc(TC.h, 1)
                    TC.inc()
                    e.wait_ge(VC.h, s2_vc0 + 3 * it + 2)
                    e.matmul(psum[3][:, 0:1], ones32r[:], gmax1[:],
                             start=True, stop=True).then_inc(TC.h, 1)
                    TC.inc()

            barrier(block)
            tap(block, "ptbf", ptbf[:])
            if stop_stage <= 2:
                return nc

            emit_cast_slice(block, 7, 18)

            g2 = ExitStack(); root.enter_context(g2)
            sb2 = mk_sb(g2)
            xq = [sb2(f"xq{i}", [128, 1024], BF16) for i in range(4)]
            s2b_tc0 = TC.n
            ixq = [None] * 64

            @block.sync
            def _(e):
                for c in range(64):
                    if c >= 4:
                        e.wait_ge(TC.h, s2b_tc0 + (c - 4) + 1)
                    ixq[c] = SYN.issue(e, xq[c % 4][:],
                                       xscr[c * 128:(c + 1) * 128, :])

            @block.tensor
            def _(e):
                for c in range(64):
                    SYN.wait(e, ixq[c])
                    e.matmul(psum[0][0:K, :], ptbf[:, c * K:(c + 1) * K],
                             xq[c % 4][:, 0:512], start=(c == 0), stop=(c == 63))
                    e.matmul(psum[1][0:K, :], ptbf[:, c * K:(c + 1) * K],
                             xq[c % 4][:, 512:1024], start=(c == 0), stop=(c == 63)
                             ).then_inc(TC.h, 1)
                    TC.inc()
            s2b_tc_end = TC.n

            @block.vector
            def _(e):
                e.wait_ge(TC.h, s2b_tc_end)
                e.tensor_copy(qtokb[:, 0:512], psum[0][0:K, :])
                e.tensor_copy(qtokb[:, 512:1024], psum[1][0:K, :]).then_inc(VC.h, 1)
                VC.inc()

            barrier(block)
            g2.close()
            tap(block, "qtok", qtokb[:])
            if stop_stage <= 2.5:
                return nc

            g3 = ExitStack(); root.enter_context(g3)
            sb3 = mk_sb(g3)
            wq32 = sb3("wq32", [128, 8192])
            wqT = sb3("wqT", [128, 8192], BF16)
            wkbf = sb3("wkbf", [128, 8192], BF16)
            qtokT = sb3("qtokT", [128, 8 * K], BF16)
            qTsb = sb3("qTsb", [128, 8 * K], BF16)
            tq32 = sb3("tq32", [128, 8 * K])
            sq720 = [sb3(f"sq720_{i}", [128, RPAD]) for i in range(2)]
            qb_row = sb3("qb_row", [1, RPAD])
            rn_row = sb3("rn_row", [1, RPAD])
            eb_row = sb3("eb_row", [1, RPAD])
            tmp_row = sb3("tmp_row", [1, RPAD])

            @block.gpsimd
            def _(e):
                MGP.issue(e, wkbf[:].rearrange("p (r f) -> p r f", f=1024),
                          ipw[1024:2048, :].rearrange("(r p) f -> p r f", p=128))
            ig_s3w = len(MGP.marks) - 1
            emit_cast_slice(block, 18, 26)

            @block.sync
            def _(e):
                MSC.issue(e, wq32[:].rearrange("p (r f) -> p r f", f=1024),
                          ipw[0:1024, :].rearrange("(r p) f -> p r f", p=128))
            i_s3w = len(MSC.marks) - 1

            s3_tc0, s3_vc0 = TC.n, VC.n

            @block.tensor
            def _(e):
                MSC.wait_upto(e, i_s3w)
                MGP.wait_upto(e, ig_s3w)
                n = 0
                for kc in range(8):
                    for oc in range(8):
                        if n >= 2:
                            e.wait_ge(VC.h, s3_vc0 + n - 1)
                        e.transpose(psum[n % 2][:, 0:128],
                                    wq32[:, oc * 1024 + kc * 128:oc * 1024 + (kc + 1) * 128],
                                    id32[:]).then_inc(TC.h, 1)
                        TC.inc(); n += 1
                for kc in range(8):
                    e.wait_ge(VC.h, s3_vc0 + n - 1)
                    e.transpose(psum[n % 2][0:128, 0:K],
                                qtokb[0:K, kc * 128:(kc + 1) * 128],
                                id32[0:K, 0:K]).then_inc(TC.h, 1)
                    TC.inc(); n += 1
            s3_tc_tr = TC.n

            @block.vector
            def _(e):
                n = 0
                for kc in range(8):
                    for oc in range(8):
                        e.wait_ge(TC.h, s3_tc0 + n + 1)
                        e.tensor_copy(wqT[:, kc * 1024 + oc * 128:kc * 1024 + (oc + 1) * 128],
                                      psum[n % 2][:, 0:128]).then_inc(VC.h, 1)
                        VC.inc(); n += 1
                for kc in range(8):
                    e.wait_ge(TC.h, s3_tc0 + n + 1)
                    e.tensor_copy(qtokT[:, kc * K:(kc + 1) * K],
                                  psum[n % 2][0:128, 0:K]).then_inc(VC.h, 1)
                    VC.inc(); n += 1
            s3_vc_tr = VC.n

            if stop_stage <= 2.6:
                barrier(block)
                return nc

            s3_tc1, s3_vc1 = TC.n, VC.n

            @block.tensor
            def _(e):
                e.wait_ge(VC.h, s3_vc_tr)
                for oc in range(8):
                    if oc >= 2:
                        e.wait_ge(VC.h, s3_vc1 + (oc - 2) + 1)
                    for kc in range(8):
                        mm = e.matmul(psum[oc % 2][:, 0:K],
                                      wqT[:, kc * 1024 + oc * 128:kc * 1024 + (oc + 1) * 128],
                                      qtokT[:, kc * K:(kc + 1) * K],
                                      start=(kc == 0), stop=(kc == 7))
                    mm.then_inc(TC.h, 1)
                    TC.inc()
            s3_tc_q = TC.n

            @block.vector
            def _(e):
                for oc in range(8):
                    e.wait_ge(TC.h, s3_tc1 + oc + 1)
                    e.tensor_scalar(qTsb[:, oc * K:(oc + 1) * K], psum[oc % 2][:, 0:K],
                                    bcols[:, oc:oc + 1], None, op0=ALU.add
                                    ).then_inc(VC.h, 1)
                    VC.inc()
                e.tensor_copy(gapA[:], gapB[:])
                for oc in range(8):
                    e.tensor_scalar(tq32[:, oc * K:(oc + 1) * K],
                                    qTsb[:, oc * K:(oc + 1) * K],
                                    bcols[:, 8 + oc:9 + oc], None, op0=ALU.mult)
                e.engine_nop().then_inc(VC.h, 1)
                VC.inc()
            s3_vc_q = VC.n

            if stop_stage <= 2.7:
                barrier(block)
                return nc

            s3_tc2, s3_vc2 = TC.n, VC.n

            @block.tensor
            def _(e):
                e.wait_ge(VC.h, s3_vc_q)
                n = 0
                for h in range(H):
                    r, po = h // 2, (h % 2) * 64
                    for fc in range(8):
                        if n >= 4:
                            e.wait_ge(VC.h, s3_vc2 + n - 3)
                        e.matmul(psum[4 + n % 4][0:128, 0:K],
                                 wkbf[po:po + 64, r * 1024 + fc * 128:r * 1024 + (fc + 1) * 128],
                                 qTsb[po:po + 64, r * K:(r + 1) * K],
                                 start=True, stop=True,
                                 tile_position=(po, 0)).then_inc(TC.h, 1)
                        TC.inc(); n += 1
                if stop_stage > 2.75:
                    for h in range(QB_N):
                        po = (h % 2) * 64
                        dst = (psum[2] if h % 2 == 0 else psum[3])[
                            0:1, (h // 2) * 64:(h // 2) * 64 + K]
                        e.matmul(dst, ones32c[po:po + 64, 0:1],
                                 tq32[po:po + 64, (h // 2) * K:(h // 2 + 1) * K],
                                 start=True, stop=True, tile_position=(po, 0))
                e.nop().then_inc(TC.h, 1)
                TC.inc()
            s3_tc_qb = TC.n

            @block.vector
            def _(e):
                n = 0
                for h in range(H):
                    for fc in range(8):
                        e.wait_ge(TC.h, s3_tc2 + n + 1)
                        e.tensor_copy(qWf[:, fc * RPAD + h * K:fc * RPAD + (h + 1) * K],
                                      psum[4 + n % 4][0:128, 0:K]).then_inc(VC.h, 1)
                        VC.inc(); n += 1
            s3_vc_qwf = VC.n

            if stop_stage <= 2.8:
                barrier(block)
                return nc

            s3_vc3 = VC.n

            @block.vector
            def _(e):
                e.wait_ge(VC.h, s3_vc_qwf)
                for fc in range(8):
                    if fc >= 2:
                        e.wait_ge(RN.h, fc - 1)
                    e.tensor_tensor(sq720[fc % 2][:],
                                    qWf[:, fc * RPAD:(fc + 1) * RPAD],
                                    qWf[:, fc * RPAD:(fc + 1) * RPAD],
                                    op=ALU.mult).then_inc(VC.h, 1)
                    VC.inc()

            @block.tensor
            def _(e):
                for fc in range(8):
                    e.wait_ge(VC.h, s3_vc3 + fc + 1)
                    e.matmul(psum[6][0:1, :], ones32c[:], sq720[fc % 2][:, 0:512],
                             start=(fc == 0), stop=(fc == 7))
                    e.matmul(psum[7][0:1, 0:256], ones32c[:], sq720[fc % 2][:, 512:768],
                             start=(fc == 0), stop=(fc == 7)).then_inc(RN.h, 1)
                    RN.inc()
                e.nop().then_inc(TC.h, 1)
                TC.inc()
            s3_tc_rn = TC.n

            if stop_stage <= 2.9:
                barrier(block)
                return nc

            s3_sc0 = SC.n
            @block.scalar
            def _(e):
                e.wait_ge(TC.h, s3_tc_rn)
                e.activation(out=rn_row[0:1, 0:512], in_=psum[6][0:1, :], func=AF.Sqrt)
                e.activation(out=rn_row[0:1, 512:768], in_=psum[7][0:1, 0:256],
                             func=AF.Sqrt)
                for h in range(H):
                    srcp = (psum[2] if h % 2 == 0 else psum[3])[
                        0:1, (h // 2) * 64:(h // 2) * 64 + K]
                    e.activation(out=qb_row[0:1, h * K:(h + 1) * K], in_=srcp,
                                 func=AF.Identity)
                e.activation(out=qb_row[0:1, 720:768], in_=psum[3][0:1, 464:512],
                             func=AF.Identity, scale=0.0)
                e.activation(out=tmp_row[:], in_=qb_row[:], func=AF.Abs)
                e.activation(out=smax[:], in_=smax[:], func=AF.Sqrt
                             ).then_inc(SC.h, 1)
                SC.inc()
            s3_sc_rows = SC.n

            s3_vc4 = VC.n
            @block.vector
            def _(e):
                e.wait_ge(SC.h, s3_sc_rows)
                e.tensor_scalar(rn_row[:], rn_row[:], smax[0:1, 0:1], None,
                                op0=ALU.mult)
                e.tensor_tensor(eb_row[:], qb_row[:], tmp_row[:], op=ALU.subtract)
                e.tensor_copy(gapA[:], gapB[:])
                e.tensor_tensor(eb_row[:], eb_row[:], rn_row[:], op=ALU.subtract)
                e.tensor_copy(gapA[:], gapB[:])
                e.tensor_scalar(eb_row[:], eb_row[:], 0.125, None, op0=ALU.mult)
                e.memset(eb_row[0:1, ROWS:RPAD], 0.0)
                e.engine_nop().then_inc(VC.h, 1)
                VC.inc()
            s3_vc_eb = VC.n

            s3_tc5, s3_vc5 = TC.n, VC.n
            @block.tensor
            def _(e):
                e.wait_ge(VC.h, s3_vc_eb)
                for rt in range(NRT):
                    if rt >= 2:
                        e.wait_ge(VC.h, s3_vc5 + (rt - 2) + 1)
                    e.transpose(psum[rt % 2][0:128, 0:1],
                                eb_row[0:1, rt * 128:(rt + 1) * 128],
                                id32[0:1, 0:1]).then_inc(TC.h, 1)
                    TC.inc()
            s3_tc_ebt = TC.n

            @block.vector
            def _(e):
                for rt in range(NRT):
                    e.wait_ge(TC.h, s3_tc5 + rt + 1)
                    e.tensor_copy(ebias[:, rt:rt + 1], psum[rt % 2][0:128, 0:1]
                                  ).then_inc(VC.h, 1)
                    VC.inc()

            barrier(block)
            tap(block, "wq32", wq32[:])
            tap(block, "wkbf", wkbf[:])
            tap(block, "wqT", wqT[:])
            tap(block, "qTsb", qTsb[:])
            tap(block, "tq32", tq32[:])
            tap(block, "qbrow", qb_row[:])
            g3.close()
            tap(block, "qWf", qWf[:])
            tap(block, "ebias", ebias[:])
            if stop_stage <= 3:
                return nc

            emit_cast_slice(block, 26, 48)

            g4 = ExitStack(); root.enter_context(g4)
            sb4 = mk_sb(g4)
            SLICE = 2048
            NSL = S // SLICE
            xT = sb4("xT", [128, 8 * SLICE], BF16)
            attst = [sb4(f"attst{i}", [128, 512], BF16) for i in range(4)]
            s4_tc0, s4_sc0 = TC.n, SC.n
            ixT = [[None] * 8 for _ in range(NSL)]
            iaw = [None] * (NSL * NRT * 4)

            @block.sync
            def _(e):
                for sl in range(NSL):
                    if sl >= 1:
                        e.wait_ge(TC.h, s4_tc0 + sl * 24)
                    for fc in range(8):
                        ixT[sl][fc] = SYN.issue(
                            e, xT[:, fc * SLICE:(fc + 1) * SLICE],
                            xscr[sl * SLICE:(sl + 1) * SLICE,
                                 fc * 128:(fc + 1) * 128],
                            transpose=True)

            @block.tensor
            def _(e):
                n = 0
                for sl in range(NSL):
                    SYN.wait_upto(e, ixT[sl][7])
                    for rt in range(NRT):
                        for sub in range(4):
                            if n >= 4:
                                e.wait_ge(SC.h, s4_sc0 + (n - 4) + 1)
                            for kc in range(8):
                                mm = e.matmul(psum[n % 4][:],
                                              qWf[:, kc * RPAD + rt * 128:
                                                  kc * RPAD + (rt + 1) * 128],
                                              xT[:, kc * SLICE + sub * 512:
                                                 kc * SLICE + (sub + 1) * 512],
                                              start=(kc == 0), stop=(kc == 7))
                            mm.then_inc(TC.h, 1)
                            TC.inc(); n += 1

            @block.scalar
            def _(e):
                n = 0
                for sl in range(NSL):
                    for rt in range(NRT):
                        for sub in range(4):
                            e.wait_ge(TC.h, s4_tc0 + n + 1)
                            if n >= 4:
                                SCL.wait(e, iaw[n - 4])
                            e.activation(out=attst[n % 4][:], in_=psum[n % 4][:],
                                         func=AF.Exp, bias=ebias[:, rt:rt + 1],
                                         scale=0.125,
                                         accum_out=sump[:, rt * 16 + sl * 4 + sub:
                                                        rt * 16 + sl * 4 + sub + 1]
                                         ).then_inc(SC.h, 1)
                            SC.inc()
                            e.wait_ge(SC.h, SC.n)
                            iaw[n] = SCL.issue(
                                e, attnscr[rt * 128:(rt + 1) * 128,
                                           sl * SLICE + sub * 512:
                                           sl * SLICE + (sub + 1) * 512],
                                attst[n % 4][:])
                            n += 1
            s4_sc_end = SC.n

            @block.vector
            def _(e):
                e.wait_ge(SC.h, s4_sc_end)
                for rt in range(NRT):
                    e.reduce_sum(sumex[:, rt:rt + 1], sump[:, rt * 16:(rt + 1) * 16],
                                 axis=AX.X)
                e.tensor_copy(gapA[:], gapB[:])
                e.reciprocal(rsum[:], sumex[:])
                e.engine_nop().then_inc(VC.h, 1)
                VC.inc()

            barrier(block)
            g4.close()
            tap(block, "sumex", sumex[:])
            if stop_stage <= 4:
                return nc

            emit_cast_slice(block, 48, 64)

            g5 = ExitStack(); root.enter_context(g5)
            sb5 = mk_sb(g5)
            atT = [sb5(f"atT{i}", [128, RPAD], BF16) for i in range(4)]
            xh = [sb5(f"xh{i}", [128, 512], BF16) for i in range(4)]
            ctxxn = sb5("ctxxn", [128, NRT * 1024])
            s5_tc0, s5_vc0 = TC.n, VC.n
            iat = [None] * 128
            ixh = [None] * 128

            @block.sync
            def _(e):
                for i in range(128):
                    c = i % 64
                    if i >= 4:
                        e.wait_ge(TC.h, s5_tc0 + (i - 3))
                    iat[i] = SYN.issue(e, atT[i % 4][:],
                                       attnscr[:, c * 128:(c + 1) * 128],
                                       transpose=True)

            @block.scalar
            def _(e):
                for i in range(128):
                    fh, c = i // 64, i % 64
                    if i >= 4:
                        e.wait_ge(TC.h, s5_tc0 + (i - 3))
                    ixh[i] = SCL.issue(e, xh[i % 4][:],
                                       xscr[c * 128:(c + 1) * 128,
                                            fh * 512:(fh + 1) * 512])

            @block.tensor
            def _(e):
                for i in range(128):
                    fh, c = i // 64, i % 64
                    SYN.wait(e, iat[i])
                    SCL.wait(e, ixh[i])
                    if fh == 1 and c == 0:
                        e.wait_ge(VC.h, s5_vc0 + NRT)
                    for rt in range(NRT):
                        mm = e.matmul(psum[rt][:],
                                      atT[i % 4][:, rt * 128:(rt + 1) * 128],
                                      xh[i % 4][:], start=(c == 0), stop=(c == 63))
                    mm.then_inc(TC.h, 1)
                    TC.inc()
            s5_tc_end = TC.n

            @block.vector
            def _(e):
                for fh in range(2):
                    e.wait_ge(TC.h, s5_tc0 + 64 * (fh + 1))
                    for rt in range(NRT):
                        e.tensor_scalar(ctxxn[:, rt * 1024 + fh * 512:
                                              rt * 1024 + (fh + 1) * 512],
                                        psum[rt][:], rsum[:, rt:rt + 1], None,
                                        op0=ALU.mult).then_inc(VC.h, 1)
                        VC.inc()

            barrier(block)
            tap(block, "ctxxn", ctxxn[:])
            if stop_stage <= 5:
                return nc

            g6 = ExitStack(); root.enter_context(g6)
            sb6 = mk_sb(g6)
            wv32 = sb6("wv32", [128, 8192])
            wvT = sb6("wvT", [128, H * 8 * 64], BF16)
            wo32 = sb6("wo32", [128, 8192])
            woT = sb6("woT", [128, 8192], BF16)
            ctxxT = sb6("ctxxT", [128, 8 * RPAD], BF16)
            ctxT = sb6("ctxT", [128, 8 * K], BF16)
            cm32 = sb6("cm32", [128, 1])
            p32 = [sb6(f"p32_{i}", [128, K]) for i in range(2)]

            @block.sync
            def _(e):
                MSC.issue(e, wv32[:].rearrange("p (r f) -> p r f", f=1024),
                          ipw[2048:3072, :].rearrange("(r p) f -> p r f", p=128))
                MSC.issue(e, wo32[:].rearrange("p (r f) -> p r f", f=1024),
                          outw.rearrange("(r p) f -> p r f", p=128))
            i_s6w = len(MSC.marks) - 1

            s6_tc0, s6_vc0 = TC.n, VC.n

            @block.tensor
            def _(e):
                MSC.wait_upto(e, i_s6w)
                n = 0
                for kc in range(8):
                    for oc in range(8):
                        if n >= 2:
                            e.wait_ge(VC.h, s6_vc0 + n - 1)
                        e.transpose(psum[n % 2][:, 0:128],
                                    wo32[:, oc * 1024 + kc * 128:oc * 1024 + (kc + 1) * 128],
                                    id32[:]).then_inc(TC.h, 1)
                        TC.inc(); n += 1
                for h in range(H):
                    r, po = h // 2, (h % 2) * 64
                    for fc in range(8):
                        e.wait_ge(VC.h, s6_vc0 + n - 1)
                        e.transpose(psum[n % 2][:, 0:64],
                                    wv32[po:po + 64, r * 1024 + fc * 128:r * 1024 + (fc + 1) * 128],
                                    id32[po:po + 64, po:po + 64],
                                    tile_position=(po, 0)).then_inc(TC.h, 1)
                        TC.inc(); n += 1
                for rt in range(NRT):
                    for fc in range(8):
                        e.wait_ge(VC.h, s6_vc0 + n - 1)
                        e.transpose(psum[n % 2][:, 0:128],
                                    ctxxn[:, rt * 1024 + fc * 128:rt * 1024 + (fc + 1) * 128],
                                    id32[:]).then_inc(TC.h, 1)
                        TC.inc(); n += 1
            s6_tc_tr = TC.n

            @block.vector
            def _(e):
                n = 0
                for kc in range(8):
                    for oc in range(8):
                        e.wait_ge(TC.h, s6_tc0 + n + 1)
                        e.tensor_copy(woT[:, kc * 1024 + oc * 128:kc * 1024 + (oc + 1) * 128],
                                      psum[n % 2][:, 0:128]).then_inc(VC.h, 1)
                        VC.inc(); n += 1
                for h in range(H):
                    for fc in range(8):
                        e.wait_ge(TC.h, s6_tc0 + n + 1)
                        e.tensor_copy(wvT[:, (h * 8 + fc) * 64:(h * 8 + fc + 1) * 64],
                                      psum[n % 2][:, 0:64]).then_inc(VC.h, 1)
                        VC.inc(); n += 1
                for rt in range(NRT):
                    for fc in range(8):
                        e.wait_ge(TC.h, s6_tc0 + n + 1)
                        e.tensor_copy(ctxxT[:, fc * RPAD + rt * 128:
                                            fc * RPAD + (rt + 1) * 128],
                                      psum[n % 2][:, 0:128]).then_inc(VC.h, 1)
                        VC.inc(); n += 1
            s6_vc_tr = VC.n

            s6_tc1, s6_vc1 = TC.n, VC.n

            @block.tensor
            def _(e):
                e.wait_ge(VC.h, s6_vc_tr)
                for h in range(H):
                    po = (h % 2) * 64
                    if h >= 2:
                        e.wait_ge(VC.h, s6_vc1 + (h - 2) + 1)
                    for fc in range(8):
                        mm = e.matmul(psum[h % 2][po:po + 64, 0:K],
                                      wvT[:, (h * 8 + fc) * 64:(h * 8 + fc + 1) * 64],
                                      ctxxT[:, fc * RPAD + h * K:fc * RPAD + (h + 1) * K],
                                      start=(fc == 0), stop=(fc == 7),
                                      tile_position=(0, po))
                    mm.then_inc(TC.h, 1)
                    TC.inc()
            s6_tc_ctx = TC.n

            @block.vector
            def _(e):
                for h in range(H):
                    po = (h % 2) * 64
                    e.wait_ge(TC.h, s6_tc1 + h + 1)
                    e.tensor_scalar(ctxT[po:po + 64, (h // 2) * K:(h // 2 + 1) * K],
                                    psum[h % 2][po:po + 64, 0:K],
                                    bcols[po:po + 64, 16 + h // 2:17 + h // 2],
                                    None, op0=ALU.add).then_inc(VC.h, 1)
                    VC.inc()
            s6_vc_ctx = VC.n

            @block.tensor
            def _(e):
                e.wait_ge(VC.h, s6_vc_ctx)
                for ec in range(8):
                    e.matmul(psum[2][0:K, :], ctxT[:, ec * K:(ec + 1) * K],
                             woT[:, ec * 1024:ec * 1024 + 512],
                             start=(ec == 0), stop=(ec == 7))
                    mm = e.matmul(psum[3][0:K, :], ctxT[:, ec * K:(ec + 1) * K],
                                  woT[:, ec * 1024 + 512:(ec + 1) * 1024],
                                  start=(ec == 0), stop=(ec == 7))
                mm.then_inc(TC.h, 1)
                TC.inc()
            s6_tc_os = TC.n

            @block.vector
            def _(e):
                e.wait_ge(TC.h, s6_tc_os)
                e.tensor_copy(oselr[:, 0:512], psum[2][0:K, :])
                e.tensor_copy(oselr[:, 512:1024], psum[3][0:K, :]
                              ).then_inc(VC.h, 1)
                VC.inc()
            s6_vc_os = VC.n

            s6_tc3, s6_vc3, s6_sc3 = TC.n, VC.n, SC.n

            @block.vector
            def _(e):
                for ci in range(16):
                    if ci >= 2:
                        e.wait_ge(TC.h, s6_tc3 + 2 * (ci - 1))
                    e.tensor_copy(p32[ci % 2][:], ptbf[:, ci * K:(ci + 1) * K]
                                  ).then_inc(VC.h, 1)
                    VC.inc()
                    if ci >= 1:
                        e.wait_ge(TC.h, s6_tc3 + 2 * ci)
                    e.reduce_sum(cm32[:], ptbf[:, ci * K:(ci + 1) * K], axis=AX.X
                                 ).then_inc(VC.h, 1)
                    VC.inc()

            @block.tensor
            def _(e):
                for ci in range(16):
                    if ci >= 2:
                        e.wait_ge(SC.h, s6_sc3 + 2 * (ci - 2) + 2)
                    e.wait_ge(VC.h, s6_vc3 + 2 * ci + 1)
                    e.transpose(psum[4 + (2 * ci) % 4][0:K, 0:128],
                                p32[ci % 2][:],
                                id32[:]).then_inc(TC.h, 1)
                    TC.inc()
                    e.wait_ge(VC.h, s6_vc3 + 2 * ci + 2)
                    e.transpose(psum[4 + (2 * ci + 1) % 4][0:1, 0:128], cm32[:],
                                id32[:]).then_inc(TC.h, 1)
                    TC.inc()

            @block.scalar
            def _(e):
                for ci in range(16):
                    e.wait_ge(TC.h, s6_tc3 + 2 * ci + 1)
                    e.activation(out=slhsP[:, ci * 128:(ci + 1) * 128],
                                 in_=psum[4 + (2 * ci) % 4][0:K, 0:128],
                                 func=AF.Identity).then_inc(SC.h, 1)
                    SC.inc()
                    e.wait_ge(TC.h, s6_tc3 + 2 * ci + 2)
                    e.activation(out=cmrow[0:1, ci * 128:(ci + 1) * 128],
                                 in_=psum[4 + (2 * ci + 1) % 4][0:1, 0:128],
                                 func=AF.Identity).then_inc(SC.h, 1)
                    SC.inc()

            barrier(block)
            g6.close()
            g5.close()
            g_mid.close()
            tap(block, "stackrhs", oselr[:])
            tap(block, "slhs", slhsP[:])
            if stop_stage <= 6:
                return nc

            g7 = ExitStack(); root.enter_context(g7)
            sb7 = mk_sb(g7)
            xlf = sb7("xlf", [128, 4096])
            y0b = sb7("y0b", [128, 4096])
            ffb = sb7("ffb", [128, 4096])
            y1T = sb7("y1T", [128, 8 * 512], BF16)
            Hsb = sb7("Hsb", [128, NHC * 512], BF16)
            ffbb = ffb[:].bitcast(BF16)
            wst = [ffbb[:, 4096 + i * 1024:4096 + (i + 1) * 1024] for i in range(4)]
            lnm = sb7("lnm", [128, 4])
            lnv = sb7("lnv", [128, 4])
            lnr = sb7("lnr", [128, 4])
            glogT = sb7("glogT", [128, 32])
            gms = sb7("gms", [128, 4])
            grs = sb7("grs", [128, 4])
            gmx = sb7("gmx", [128, 4])
            gbm = sb7("gbm", [128, 4])
            gse = sb7("gse", [128, 4])
            gg = sb7("gg", [128, 4])
            gt2 = sb7("gt2", [128, 4])
            ng = sb7("ng", [128, 4])
            sgap = sb7("sgap", [128, 1])

            for t in range(NTILE):
                st = {"tc0": TC.n, "vc0": VC.n, "sc0": SC.n}

                @block.sync
                def _(e, t=t, st=st):
                    st["i_xl"] = MSC.issue(
                        e, xlf[:].rearrange("p (c f) -> p c f", f=1024),
                        xb[t * TT:(t + 1) * TT, :]
                        .rearrange("(c p) f -> p c f", p=128))

                @block.tensor
                def _(e, t=t, st=st):
                    for ci in range(4):
                        gci = t * 4 + ci
                        for nh in range(2):
                            pb = psum[2 * ci + nh]
                            e.matmul(pb[:], slhsP[:, gci * 128:(gci + 1) * 128],
                                     oselr[:, nh * 512:(nh + 1) * 512],
                                     start=True, stop=False)
                            e.matmul(pb[:], cmrow[0:1, gci * 128:(gci + 1) * 128],
                                     obmrow[:, nh * 512:(nh + 1) * 512],
                                     start=False, stop=False)
                            mm = e.matmul(pb[:], onesbfr[:],
                                          meanrow[:, nh * 512:(nh + 1) * 512],
                                          start=False, stop=True)
                    mm.then_inc(TC.h, 1)
                    TC.inc()
                    st["tc_sc"] = TC.n

                @block.vector
                def _(e, t=t, st=st):
                    e.wait_ge(TC.h, st["tc_sc"])
                    MSC.wait(e, st["i_xl"])
                    for ci in range(4):
                        e.tensor_tensor(y0b[:, ci * 1024:ci * 1024 + 512],
                                        psum[2 * ci][:],
                                        xlf[:, ci * 1024:ci * 1024 + 512], op=ALU.add)
                        e.tensor_tensor(y0b[:, ci * 1024 + 512:(ci + 1) * 1024],
                                        psum[2 * ci + 1][:],
                                        xlf[:, ci * 1024 + 512:(ci + 1) * 1024],
                                        op=ALU.add)
                    e.engine_nop().then_inc(VC.h, 1); VC.inc()
                    st["vc_y0"] = VC.n
                    for ci in range(4):
                        e.reduce_sum(lnm[:, ci:ci + 1],
                                     y0b[:, ci * 1024:(ci + 1) * 1024], axis=AX.X)
                    e.tensor_copy(gapA[:], gapB[:])
                    e.tensor_scalar(lnm[:], lnm[:], 1.0 / 1024, None, op0=ALU.mult)
                    e.tensor_copy(gapA[:], gapB[:])
                    for ci in range(4):
                        e.tensor_scalar(xlf[:, ci * 1024:(ci + 1) * 1024],
                                        y0b[:, ci * 1024:(ci + 1) * 1024],
                                        lnm[:, ci:ci + 1], None, op0=ALU.subtract)
                    e.engine_nop().then_inc(VC.h, 1); VC.inc()
                    st["vc_xc"] = VC.n

                @block.scalar
                def _(e, t=t, st=st):
                    e.wait_ge(VC.h, st["vc_xc"])
                    for ci in range(4):
                        e.activation(out=ffb[:, ci * 1024:(ci + 1) * 1024],
                                     in_=xlf[:, ci * 1024:(ci + 1) * 1024],
                                     func=AF.Square, accum_out=lnv[:, ci:ci + 1])
                    e.activation(out=sgap[:], in_=gapB[:], func=AF.Identity)
                    e.activation(out=lnr[:], in_=lnv[:], func=AF.Ln,
                                 scale=1.0 / 1024, bias=cst[:, 0:1])
                    e.activation(out=lnr[:], in_=lnr[:], func=AF.Exp, scale=-0.5
                                 ).then_inc(SC.h, 1)
                    SC.inc()
                    st["sc_ln1"] = SC.n

                @block.vector
                def _(e, t=t, st=st):
                    e.wait_ge(SC.h, st["sc_ln1"])
                    for ci in range(4):
                        e.tensor_scalar(y0b[:, ci * 1024:(ci + 1) * 1024],
                                        xlf[:, ci * 1024:(ci + 1) * 1024],
                                        lnr[:, ci:ci + 1], None, op0=ALU.mult)
                    e.engine_nop().then_inc(VC.h, 1); VC.inc()
                    st["vc_y1"] = VC.n

                if t == 0 and "t0y1" in dbg_out:
                    @block.sync
                    def _(e, st=st):
                        e.wait_ge(VC.h, st["vc_y1"])
                        MSC.issue(e, dbg_out["t0y1"], y0b[:])

                @block.tensor
                def _(e, t=t, st=st):
                    e.wait_ge(VC.h, st["vc_y1"])
                    for kc in range(8):
                        if kc >= 2:
                            e.wait_ge(YE.h, 8 * t + kc - 1)
                        for ci in range(4):
                            tr = e.transpose(psum[kc % 2][:, ci * 128:(ci + 1) * 128],
                                             y0b[:, ci * 1024 + kc * 128:
                                                 ci * 1024 + (kc + 1) * 128],
                                             id32[:])
                        tr.then_inc(YT.h, 1)
                        YT.inc()

                @block.vector
                def _(e, t=t, st=st):
                    for kc in range(8):
                        e.wait_ge(YT.h, 8 * t + kc + 1)
                        e.tensor_copy(y1T[:, kc * 512:(kc + 1) * 512],
                                      psum[kc % 2][:, 0:512]).then_inc(YE.h, 1)
                        YE.inc()
                    e.engine_nop().then_inc(VC.h, 1); VC.inc()
                    st["vc_y1t"] = VC.n

                @block.tensor
                def _(e, t=t, st=st):
                    e.wait_ge(VC.h, st["vc_y1t"])
                    for kc in range(8):
                        mm = e.matmul(psum[4][0:8, :], gwT[:, kc * 8:(kc + 1) * 8],
                                      y1T[:, kc * 512:(kc + 1) * 512],
                                      start=(kc == 0), stop=(kc == 7))
                    mm.then_inc(TC.h, 1)
                    TC.inc()
                    st["tc_gl"] = TC.n

                @block.vector
                def _(e, t=t, st=st):
                    e.wait_ge(TC.h, st["tc_gl"])
                    e.tensor_scalar(xlf[0:8, 2048:2560], psum[4][0:8, :], gbcol[:, 0:1], None,
                                    op0=ALU.add).then_inc(VC.h, 1)
                    VC.inc()
                    st["vc_gl"] = VC.n

                @block.tensor
                def _(e, t=t, st=st):
                    e.wait_ge(VC.h, st["vc_gl"])
                    for ci in range(4):
                        tr = e.transpose(psum[5][0:128, ci * 8:(ci + 1) * 8],
                                         xlf[0:8, 2048 + ci * 128:2048 + (ci + 1) * 128],
                                         id32[0:8, 0:8])
                    tr.then_inc(TC.h, 1)
                    TC.inc()
                    st["tc_glt"] = TC.n

                @block.vector
                def _(e, t=t, st=st):
                    e.wait_ge(TC.h, st["tc_glt"])
                    e.tensor_copy(glogT[:], psum[5][0:128, 0:32]).then_inc(VC.h, 1)
                    VC.inc()
                    for ci in range(4):
                        e.tensor_tensor(xlf[:, 2560 + ci * 8:2560 + (ci + 1) * 8],
                                        glogT[:, ci * 8:(ci + 1) * 8],
                                        glogT[:, ci * 8:(ci + 1) * 8], op=ALU.mult)
                    for ci in range(4):
                        e.reduce_sum(gms[:, ci:ci + 1], xlf[:, 2560 + ci * 8:2560 + (ci + 1) * 8],
                                     axis=AX.X)
                    e.engine_nop().then_inc(VC.h, 1); VC.inc()
                    st["vc_gms"] = VC.n

                @block.scalar
                def _(e, t=t, st=st):
                    e.wait_ge(VC.h, st["vc_gms"])
                    e.activation(out=grs[:], in_=gms[:], func=AF.Ln,
                                 scale=1.0 / 8, bias=cst[:, 1:2])
                    e.activation(out=grs[:], in_=grs[:], func=AF.Exp, scale=-0.5
                                 ).then_inc(SC.h, 1)
                    SC.inc()
                    st["sc_grs"] = SC.n

                @block.vector
                def _(e, t=t, st=st):
                    e.wait_ge(SC.h, st["sc_grs"])
                    for ci in range(4):
                        e.tensor_scalar(xlf[:, 2592 + ci * 8:2592 + (ci + 1) * 8],
                                        glogT[:, ci * 8:(ci + 1) * 8],
                                        grs[:, ci:ci + 1], None, op0=ALU.mult)
                    e.tensor_copy(gapA[:], gapB[:])
                    for ci in range(4):
                        e.tensor_tensor(xlf[:, 2592 + ci * 8:2592 + (ci + 1) * 8],
                                        xlf[:, 2592 + ci * 8:2592 + (ci + 1) * 8],
                                        rmsb[:], op=ALU.mult)
                    e.tensor_copy(gapA[:], gapB[:])
                    for ci in range(4):
                        e.reduce_max(gmx[:, ci:ci + 1],
                                     xlf[:, 2592 + ci * 8:2592 + (ci + 1) * 8],
                                     axis=AX.X)
                    e.tensor_copy(gapA[:], gapB[:])
                    e.tensor_scalar(gbm[:], gmx[:], -2.0, None, op0=ALU.mult
                                    ).then_inc(VC.h, 1)
                    VC.inc()
                    st["vc_gbm"] = VC.n

                @block.scalar
                def _(e, t=t, st=st):
                    e.wait_ge(VC.h, st["vc_gbm"])
                    for ci in range(3):
                        e.activation(out=xlf[:, 2560 + ci * 8:2560 + (ci + 1) * 8],
                                     in_=xlf[:, 2592 + ci * 8:2592 + (ci + 1) * 8], func=AF.Exp,
                                     scale=2.0, bias=gbm[:, ci:ci + 1],
                                     accum_out=gse[:, ci:ci + 1])
                    e.activation(out=xlf[:, 2584:2592],
                                 in_=xlf[:, 2616:2624], func=AF.Exp,
                                 scale=2.0, bias=gbm[:, 3:4],
                                 accum_out=gse[:, 3:4]).then_inc(SC.h, 1)
                    SC.inc()
                    st["sc_gse"] = SC.n

                @block.vector
                def _(e, t=t, st=st):
                    e.wait_ge(SC.h, st["sc_gse"])
                    e.reciprocal(gg[:], gse[:])
                    e.tensor_copy(gapA[:], gapB[:])
                    e.tensor_scalar(gt2[:], gg[:], 1e-6, None, op0=ALU.add)
                    e.tensor_copy(gapA[:], gapB[:])
                    e.reciprocal(gt2[:], gt2[:])
                    e.tensor_copy(gapA[:], gapB[:])
                    e.tensor_tensor(ng[:], gg[:], gt2[:], op=ALU.mult
                                    ).then_inc(VC.h, 1)
                    VC.inc()
                    st["vc_ng"] = VC.n

                if t == 0 and "t0ng" in dbg_out:
                    @block.sync
                    def _(e, st=st):
                        e.wait_ge(VC.h, st["vc_ng"])
                        MSC.issue(e, dbg_out["t0ng"], ng[:])

                @block.sync
                def _(e, t=t, st=st):
                    iw1 = [None] * NHC
                    iw2 = [None] * NHC
                    st["iw1"] = iw1
                    st["iw2"] = iw2
                    e.wait_ge(SC.h, st["sc_ln1"])
                    for hc in range(NHC):
                        g = t * NHC + hc
                        if g >= 4:
                            e.wait_ge(WA.h, g - 3)
                        iw1[hc] = SYN.issue(e, wst[hc % 4], w1scr[hc])
                    for hc in range(NHC):
                        if hc < 4:
                            e.wait_ge(WA.h, (t + 1) * NHC)
                        else:
                            e.wait_ge(WB.h, t * NHC + hc - 3)
                        iw2[hc] = SYN.issue(e, wst[hc % 4],
                                            w2scr[hc * 128:(hc + 1) * 128, :])

                @block.tensor
                def _(e, t=t, st=st):
                    e.wait_ge(VC.h, st["vc_y1t"])
                    for hc in range(NHC):
                        SYN.wait(e, st["iw1"][hc])
                        if hc >= 4:
                            e.wait_ge(GE.h, t * NHC + hc - 3)
                        for kc in range(8):
                            mm = e.matmul(psum[hc % 4][:],
                                          wst[hc % 4][:, kc * 128:(kc + 1) * 128],
                                          y1T[:, kc * 512:(kc + 1) * 512],
                                          start=(kc == 0), stop=(kc == 7))
                        mm.then_inc(WA.h, 1)
                        WA.inc()

                @block.scalar
                def _(e, t=t, st=st):
                    for hc in range(NHC):
                        e.wait_ge(WA.h, t * NHC + hc + 1)
                        e.activation(out=Hsb[:, hc * 512:(hc + 1) * 512],
                                     in_=psum[hc % 4][:], func=AF.Gelu,
                                     bias=b1col[:, hc:hc + 1]).then_inc(GE.h, 1)
                        GE.inc()

                @block.tensor
                def _(e, t=t, st=st):
                    e.wait_ge(GE.h, (t + 1) * NHC)
                    for m in range(4):
                        for n_ in range(2):
                            e.matmul(psum[m * 2 + n_][:], onesbfr[:],
                                     b2sumr[:, n_ * 512:(n_ + 1) * 512],
                                     start=True, stop=False)
                    for hc in range(NHC):
                        SYN.wait(e, st["iw2"][hc])
                        for m in range(4):
                            for n_ in range(2):
                                mm = e.matmul(psum[m * 2 + n_][:],
                                              Hsb[:, hc * 512 + m * 128:
                                                  hc * 512 + (m + 1) * 128],
                                              wst[hc % 4][:, n_ * 512:(n_ + 1) * 512],
                                              start=False, stop=(hc == NHC - 1))
                        mm.then_inc(WB.h, 1)
                        WB.inc()

                @block.vector
                def _(e, t=t, st=st):
                    e.wait_ge(WB.h, (t + 1) * NHC)
                    e.wait_ge(VC.h, st["vc_ng"])
                    for m in range(4):
                        for n_ in range(2):
                            e.tensor_scalar(ffb[:, m * 1024 + n_ * 512:
                                                m * 1024 + (n_ + 1) * 512],
                                            psum[m * 2 + n_][:], ng[:, m:m + 1],
                                            None, op0=ALU.mult)
                    e.engine_nop().then_inc(VC.h, 1); VC.inc()
                    for ci in range(4):
                        e.tensor_tensor(xlf[:, ci * 1024:(ci + 1) * 1024],
                                        y0b[:, ci * 1024:(ci + 1) * 1024],
                                        ffb[:, ci * 1024:(ci + 1) * 1024],
                                        op=ALU.add)
                    e.engine_nop().then_inc(VC.h, 1); VC.inc()
                    for ci in range(4):
                        e.reduce_sum(lnm[:, ci:ci + 1],
                                     xlf[:, ci * 1024:(ci + 1) * 1024], axis=AX.X)
                    e.tensor_copy(gapA[:], gapB[:])
                    e.tensor_scalar(lnm[:], lnm[:], 1.0 / 1024, None, op0=ALU.mult)
                    e.tensor_copy(gapA[:], gapB[:])
                    for ci in range(4):
                        e.tensor_scalar(y0b[:, ci * 1024:(ci + 1) * 1024],
                                        xlf[:, ci * 1024:(ci + 1) * 1024],
                                        lnm[:, ci:ci + 1], None, op0=ALU.subtract)
                    e.engine_nop().then_inc(VC.h, 1); VC.inc()
                    st["vc_xc2"] = VC.n

                if t == 0 and "t0ff" in dbg_out:
                    @block.sync
                    def _(e, st=st):
                        e.wait_ge(VC.h, st["vc_xc2"] - 2)
                        MSC.issue(e, dbg_out["t0ff"], ffb[:])

                if t == 0 and "t0H" in dbg_out:
                    @block.sync
                    def _(e, st=st):
                        e.wait_ge(GE.h, NHC)
                        MSC.issue(e, dbg_out["t0H"], Hsb[:, 0:2048])

                @block.scalar
                def _(e, t=t, st=st):
                    e.wait_ge(VC.h, st["vc_xc2"])
                    for ci in range(4):
                        e.activation(out=ffb[:, ci * 1024:(ci + 1) * 1024],
                                     in_=y0b[:, ci * 1024:(ci + 1) * 1024],
                                     func=AF.Square, accum_out=lnv[:, ci:ci + 1])
                    e.activation(out=sgap[:], in_=gapB[:], func=AF.Identity)
                    e.activation(out=lnr[:], in_=lnv[:], func=AF.Ln,
                                 scale=1.0 / 1024, bias=cst[:, 0:1])
                    e.activation(out=lnr[:], in_=lnr[:], func=AF.Exp, scale=-0.5
                                 ).then_inc(SC.h, 1)
                    SC.inc()
                    st["sc_ln2"] = SC.n

                @block.vector
                def _(e, t=t, st=st):
                    e.wait_ge(SC.h, st["sc_ln2"])
                    for ci in range(4):
                        e.tensor_scalar(ffb[:, ci * 1024:(ci + 1) * 1024],
                                        y0b[:, ci * 1024:(ci + 1) * 1024],
                                        lnr[:, ci:ci + 1], None, op0=ALU.mult)
                    e.engine_nop().then_inc(VC.h, 1); VC.inc()
                    st["vc_y2"] = VC.n

                @block.scalar
                def _(e, t=t, st=st):
                    e.wait_ge(VC.h, st["vc_y2"])
                    SCL.issue(e, yout[t * TT:(t + 1) * TT, :]
                              .rearrange("(c p) f -> p c f", p=128),
                              ffb[:].rearrange("p (c f) -> p c f", f=1024))

                barrier(block)

            g7.close()

            qf = [(s.h, s.n) for s in ALLQ]

            @block.gpsimd
            def _(e):
                for h, v in qf:
                    e.wait_ge(h, v)

    return nc


_CACHED = {}


def _prep_inputs(L0_attn_x, L0_attn_in_proj_w, L0_attn_in_proj_b, L0_attn_out_w,
                 L0_attn_out_b, L0_ffn_gate_w, L0_ffn_gate_b, L0_ffn_rms_w,
                 L0_ffn_w1, L0_ffn_b1, L0_ffn_w2, L0_ffn_b2):
    c = lambda a: np.ascontiguousarray(np.asarray(a, dtype=np.float32))
    x = c(L0_attn_x)
    w1r = c(np.transpose(np.asarray(L0_ffn_w1, np.float32)
                         .reshape(8, 8, 128, 16, 128), (0, 3, 2, 1, 4))
            .reshape(NHC, 128, 1024))
    w2r = c(np.asarray(L0_ffn_w2, np.float32).reshape(NHID, E))
    b1c = c(np.asarray(L0_ffn_b1, np.float32).reshape(128, 128).T)
    common = {
        "ipw": c(L0_attn_in_proj_w),
        "ipb": c(np.asarray(L0_attn_in_proj_b, np.float32).reshape(24, 128)),
        "outw": c(L0_attn_out_w),
        "outb": c(np.asarray(L0_attn_out_b, np.float32).reshape(1, E)),
        "gatew": c(L0_ffn_gate_w),
        "gateb": c(np.asarray(L0_ffn_gate_b, np.float32).reshape(1, NEXP)),
        "rmsw": c(np.asarray(L0_ffn_rms_w, np.float32).reshape(1, NEXP)),
        "w1r": w1r, "w2r": w2r, "b1c": b1c,
        "b2": c(L0_ffn_b2),
        "ident": np.eye(128, dtype=np.float32),
        "onesc": np.ones((128, 1), np.float32),
        "onesr": np.ones((1, 128), np.float32),
    }
    in_maps = []
    for core in range(8):
        b, part = core // 4, core % 4
        m = dict(common)
        m["xb"] = c(np.roll(x[b], -part * NTOK, axis=0))
        in_maps.append(m)
    return in_maps


def kernel(**inputs):
    in_maps = _prep_inputs(**inputs)
    if "nc" not in _CACHED:
        _CACHED["nc"] = build_program()
    nc = _CACHED["nc"]
    res = run_bass_kernel_spmd(nc, in_maps, core_ids=list(range(8)))
    out = np.empty((B, S, E), np.float32)
    for core in range(8):
        b, part = core // 4, core % 4
        out[b, part * NTOK:(part + 1) * NTOK] = res.results[core]["y"]
    return out
